# version 05
# baseline (speedup 1.0000x reference)
import numpy as np
import concourse.bass as bass
import concourse.mybir as mybir

F32 = mybir.dt.float32
BF16 = mybir.dt.bfloat16
I32 = mybir.dt.int32
U32 = mybir.dt.uint32
AF = mybir.ActivationFunctionType
ALU = mybir.AluOpType
f32 = np.float32

D = 2048
S = 4096
TOWN = 2048
NF = 16
EPS = f32(1e-5)

ENGS = ("tensor", "vector", "scalar", "sync", "gpsimd")


class Ctr:
    def __init__(self, sem, name=""):
        self.sem = sem
        self.v = 0
        self.name = name


class P:
    def __init__(self, nc):
        self.nc = nc
        self.seq = {k: [] for k in ENGS}
        self.pool = {}
        self._stack = None
        self.pending = {}

    def S(self, name):
        if name not in self.pool:
            self.pool[name] = Ctr(self._stack.enter_context(self.nc.semaphore(name)))
        return self.pool[name]

    def begin_part(self):
        snap = [(c, c.v) for c in self.pool.values()]
        self.pending = {ek: list(snap) for ek in ENGS}

    def emit(self, ek, fn, waits=(), inc=None, incn=1):
        waits = list(waits)
        if self.pending.get(ek):
            waits = self.pending.pop(ek) + waits
        wl = tuple((c.sem, int(v)) for (c, v) in waits)
        if inc is not None:
            inc.v += incn
            sem, n = inc.sem, incn
        else:
            sem = None
            n = 0

        def run(eng, fn=fn, wl=wl, sem=sem, n=n):
            for s, v in wl:
                eng.wait_ge(s, v)
            inst = fn(eng)
            if sem is not None:
                assert inst is not None
                inst.then_inc(sem, n)
        self.seq[ek].append(run)

    def dma(self, ek, ctr, out, in_, waits=()):
        self.emit(ek, lambda e, o=out, i=in_: e.dma_start(out=o, in_=i),
                  waits=waits, inc=ctr, incn=16)
        return ctr.v

    def mm(self, out, lhsT, rhs, start, stop, waits=(), inc=None, **kw):
        self.emit("tensor",
                  lambda e, o=out, l=lhsT, r=rhs, st=start, sp=stop, kw=kw:
                  e.matmul(o, l, r, start=st, stop=sp, **kw),
                  waits=waits, inc=inc)
        return None if inc is None else inc.v

    def run(self, block):
        for ek in ENGS:
            fns = self.seq[ek]
            getattr(block, ek)(lambda eng, fns=fns: [fn(eng) for fn in fns])


def ln_rows(p, C, V, T, srow, var_i, rstd_i, m_i, ms_i, mr_i, statY, statY2, ntok_inv,
            waits_stats):
    p.emit("vector", lambda e: e.tensor_scalar_mul(srow[0:1, m_i, :], statY, float(ntok_inv)),
           waits=waits_stats, inc=V)
    p.emit("vector", lambda e: e.tensor_scalar_mul(srow[0:1, ms_i, :], statY2, float(ntok_inv)),
           inc=V)
    p.emit("vector", lambda e: e.tensor_tensor(srow[0:1, var_i, :], srow[0:1, m_i, :],
                                               srow[0:1, m_i, :], ALU.mult), waits=[(V, V.v)], inc=V)
    p.emit("vector", lambda e: e.tensor_tensor(srow[0:1, var_i, :], srow[0:1, ms_i, :],
                                               srow[0:1, var_i, :], ALU.subtract), waits=[(V, V.v)], inc=V)
    p.emit("scalar", lambda e: e.activation(out=srow[0:1, rstd_i, :], in_=srow[0:1, var_i, :],
                                            func=AF.Ln, bias=EPS),
           waits=[(V, V.v)], inc=C)
    p.emit("scalar", lambda e: e.activation(out=srow[0:1, rstd_i, :], in_=srow[0:1, rstd_i, :],
                                            func=AF.Exp, scale=f32(-0.5)), waits=[(C, C.v)], inc=C)
    p.emit("vector", lambda e: e.tensor_tensor(srow[0:1, mr_i, :], srow[0:1, m_i, :],
                                               srow[0:1, rstd_i, :], ALU.mult),
           waits=[(C, C.v), (V, V.v)], inc=V)




from contextlib import ExitStack


def emit_l0_attn(p, nc, XT, WQKV, BQKV, WO, BO, IDENT, X1T, KVS, MAT=None):
    p.begin_part()
    with ExitStack() as ctx:
        sb = lambda *a: ctx.enter_context(nc.sbuf_tensor(*a, allow_name_mangling=True))
        psum = lambda *a: ctx.enter_context(nc.psum_tensor(*a, allow_name_mangling=True))
        sem = lambda n: p.S(n)
        xbf = sb("xbf", [128, 16, 2048], BF16)
        qt = sb("qt", [128, 16, 2048], BF16)
        wstr = sb("wstr", [128, 2, 16, 128], BF16)
        bcol24 = sb("bcol24", [128, 24], F32)
        bocol = sb("bocol", [128, 16], F32)
        evac = sb("evac", [128, 2, 512], BF16)
        kg2 = sb("kg2", [128, 1, S], BF16)
        vg = sb("vg", [64, 1, S], BF16)
        vtok = sb("vtok", [128, 32, 65], BF16)
        pt = sb("pt", [128, 3, 512], BF16)
        rrowb = sb("rrowb", [1, 2, 512], F32)
        otmp = sb("otmp", [64, 512], F32)
        ones128 = sb("ones128", [128, 1], F32)
        ones1 = sb("ones1", [1, 128], F32)
        ident_s = sb("ident_s", [128, 128], BF16)
        xres = sb("xres", [128, 2, 512], F32)
        ysq = sb("ysq", [128, 2, 512], F32)
        srow = sb("srow", [1, 5, 512], F32)
        bcast = sb("bcast", [128, 2, 512], F32)
        x1c = sb("x1c", [128, 2, 512], F32)
        aot = xbf
        ybuf = qt.ap().rearrange("p c f -> p (c f)").bitcast(F32).rearrange("p (c f) -> p c f", c=16)
        PP = [psum("PP0", [128, 512], F32), psum("PP1", [128, 512], F32)]
        STp = [psum("ST0", [128, 512], F32), psum("ST1", [128, 512], F32)]
        STp3 = [STp[0], STp[1], PP[1]]
        OPp = [psum("OP0", [65, 512], F32), psum("OP1", [65, 512], F32)]
        TRB = [psum("TRB0", [128, 128], BF16), psum("TRB1", [128, 128], BF16)]
        ST0, ST1 = STp[0], STp[1]
        T = sem("sT"); C = sem("sC"); V = sem("sV")
        Spre = sem("dPre"); SpreG = sem("dPreG"); Sxbf = sem("dXbf")
        Sw = [sem("dW0"), sem("dW1")]
        Skvw = [sem("dKvw0"), sem("dKvw1")]
        Skg = sem("dKg")
        Sxr = [sem("dXr0"), sem("dXr1")]
        Sx1 = [sem("dX10"), sem("dX11")]

        XTv = XT.rearrange("d (b c) -> d b c", c=512)
        X1Tv = X1T.rearrange("d (b c) -> d b c", c=512)

        p.emit("vector", lambda e: e.memset(ones128[:], 1.0), inc=V)
        p.emit("vector", lambda e: e.memset(ones1[:], 1.0), inc=V)
        p.emit("vector", lambda e: e.memset(vtok[:, :, 64:65], 1.0), inc=V)
        v_pre = V.v
        bg = []
        if MAT is not None:
            GIDXT, UIDXT, DIDXT, LUT, GSCR, USCR, DSCR = MAT
            idxq = sb("idxq", [128, 2, 1376], BF16)
            accq = sb("accq", [128, 2, 1376], BF16)
            maskq = sb("maskq", [128, 1376], BF16)
            lutrow = sb("lutrow", [1, 16], F32)
            lutcol = sb("lutcol", [128, 16], F32)
            Sidx = [p.S("dIdx0"), p.S("dIdx1")]
            Sacc = [p.S("dAcc0"), p.S("dAcc1")]
            Slut = p.S("dLut")
            p.dma("sync", Slut, lutrow[:], LUT)
            p.mm(PP[0][:, 0:16], ones1[0:1, :], lutrow[0:1, :], start=True, stop=True,
                 waits=[(Slut, Slut.v), (V, v_pre)], inc=T)
            p.emit("vector", lambda e: e.tensor_copy(lutcol[:, :], PP[0][:, 0:16]),
                   waits=[(T, T.v)], inc=V)
            v_lut = V.v
            mstate = {"mi": 0, "vmat": {}}

            def _mk_thunks(IDXT, SCR, nfi, F):
                for fi in range(nfi):
                    for q4 in range(4):
                        qf = F // 4
                        mi = mstate["mi"]

                        def t_dma(mi=mi, fi=fi, q4=q4, qf=qf, IDXT=IDXT):
                            iw = [(V, mstate["vmat"].get(mi - 2, v_lut))]
                            p.dma("gpsimd", Sidx[mi % 2], idxq[:, mi % 2, 0:qf],
                                  IDXT[fi * 128:(fi + 1) * 128, q4 * qf:(q4 + 1) * qf],
                                  waits=iw)
                        bg.append(t_dma)
                        for vg in range(4):
                            def t_ops(mi=mi, vg=vg, qf=qf):
                                g_i = Sidx[mi % 2].v
                                for vv in range(vg * 4, vg * 4 + 4):
                                    w0 = [(Sidx[mi % 2], g_i), (V, V.v)]
                                    if vv == 0:
                                        w0.append((Sacc[mi % 2], Sacc[mi % 2].v))
                                        p.emit("vector",
                                               lambda e, vv=vv, qf=qf, mi=mi:
                                               e.tensor_scalar(accq[:, mi % 2, 0:qf],
                                                               idxq[:, mi % 2, 0:qf],
                                                               float(vv), lutcol[:, vv:vv + 1],
                                                               op0=ALU.is_equal, op1=ALU.mult),
                                               waits=w0, inc=V)
                                    else:
                                        p.emit("vector",
                                               lambda e, vv=vv, qf=qf, mi=mi:
                                               e.tensor_scalar(maskq[:, 0:qf],
                                                               idxq[:, mi % 2, 0:qf],
                                                               float(vv), lutcol[:, vv:vv + 1],
                                                               op0=ALU.is_equal, op1=ALU.mult),
                                               waits=w0, inc=V)
                                        p.emit("vector",
                                               lambda e, qf=qf, mi=mi:
                                               e.tensor_tensor(accq[:, mi % 2, 0:qf],
                                                               accq[:, mi % 2, 0:qf],
                                                               maskq[:, 0:qf], ALU.add),
                                               waits=[(V, V.v)], inc=V)
                            bg.append(t_ops)

                        def t_store(mi=mi, fi=fi, q4=q4, qf=qf, SCR=SCR):
                            mstate["vmat"][mi] = V.v
                            p.dma("sync", Sacc[mi % 2],
                                  SCR[fi * 128:(fi + 1) * 128, q4 * qf:(q4 + 1) * qf],
                                  accq[:, mi % 2, 0:qf], waits=[(V, V.v)])
                        bg.append(t_store)
                        mstate["mi"] = mi + 1

            _mk_thunks(GIDXT, GSCR, 16, 5504)
            _mk_thunks(UIDXT, USCR, 16, 5504)
            _mk_thunks(DIDXT, DSCR, 43, 2048)

        def drain(n):
            for _ in range(n):
                if bg:
                    bg.pop(0)()
        p.dma("sync", Spre, bcol24[:, :], BQKV)
        p.dma("sync", Spre, bocol[:, :], BO, waits=[(Spre, Spre.v)])
        p.dma("gpsimd", SpreG, ident_s[:], IDENT)
        spre = Spre.v
        spreg = SpreG.v

        t_group_end = {}
        v_evac = {}
        gidx = 0
        ev_i = 0
        t_half_end = 0
        for half in range(2):
            for fi in range(16):
                p.dma("gpsimd", Sxbf, xbf[:, fi, :],
                      XT[fi * 128:(fi + 1) * 128, half * 2048:(half + 1) * 2048],
                      waits=([(T, t_half_end)] if (half == 1 and fi == 0) else ()))
            g_x = Sxbf.v
            nfo = 24 if half == 0 else 8
            for fo in range(nfo):
                col = (16 + fo) if fo < 8 else (fo - 8)
                wwait = [(T, t_group_end[gidx - 2])] if gidx >= 2 else []
                p.dma("gpsimd", Sw[gidx % 2], wstr[:, gidx % 2, :, :],
                      WQKV[:, col * 128:(col + 1) * 128].rearrange("(c p) f -> p c f", p=128),
                      waits=wwait)
                g_w = Sw[gidx % 2].v
                for tc in range(4):
                    pj = (gidx * 4 + tc) % 2
                    key = gidx * 4 + tc
                    mwaits = [(Sw[gidx % 2], g_w), (Sxbf, g_x), (Spre, spre)]
                    if MAT is not None and gidx == 0 and tc == 0:
                        mwaits.append((V, v_lut))
                    if key - 2 in v_evac:
                        mwaits.append((V, v_evac[key - 2]))
                    for fi in range(16):
                        p.mm(PP[pj][:, :], wstr[:, gidx % 2, fi, :],
                             xbf[:, fi, tc * 512:(tc + 1) * 512],
                             start=(fi == 0), stop=(fi == 15),
                             waits=(mwaits if fi == 0 else ()), inc=(T if fi == 15 else None))
                    t_mm = T.v
                    if fo < 8:
                        ej = ev_i % 2
                        ewaits = [(T, t_mm), (Spre, spre), (Skvw[ej], Skvw[ej].v)]
                        p.emit("vector",
                               lambda e, ej=ej, pj=pj, col=col:
                               e.tensor_scalar_add(evac[:, ej, :], PP[pj][:, :], bcol24[:, col:col + 1]),
                               waits=ewaits, inc=V)
                        v_evac[key] = V.v
                        p.dma("sync", Skvw[ej],
                              KVS[(col - 16) * 128:(col - 15) * 128,
                                  half * 2048 + tc * 512: half * 2048 + (tc + 1) * 512],
                              evac[:, ej, :], waits=[(V, V.v)])
                        ev_i += 1
                    else:
                        p.emit("vector",
                               lambda e, pj=pj, col=col, tc=tc:
                               e.tensor_scalar_add(qt[:, col, tc * 512:(tc + 1) * 512],
                                                   PP[pj][:, :], bcol24[:, col:col + 1]),
                               waits=[(T, t_mm), (Spre, spre)], inc=V)
                        v_evac[key] = V.v
                    t_group_end[gidx] = t_mm
                    drain(2)
                gidx += 1
            t_half_end = T.v
        kv_done = [(Skvw[0], Skvw[0].v), (Skvw[1], Skvw[1].v)]
        t_p1_end = T.v

        t_g_end = {}
        si = 0
        c_exp = {}
        t_pv = {}
        v_norm = {}
        v_recd = {}
        prev_tr = [0, 0]

        def _a_norm_tail(b):
            blo = ((b // 4) % 2) * 64
            bqtile = (b // 4) // 2
            bqc = b % 4
            p.mm(PP[0][0:64, :], ones1[0:1, 0:64], rrowb[0:1, b % 2, :], start=True, stop=True,
                 waits=[(V, v_recd[b])] + ([(V, v_norm[b - 1])] if b - 1 in v_norm else []),
                 inc=T)
            t_bc = T.v
            p.emit("vector", lambda e, b=b: e.tensor_copy(otmp[:], OPp[b % 2][0:64, :]),
                   waits=[(T, t_bc)], inc=V)
            p.emit("vector",
                   lambda e, blo=blo, bqtile=bqtile, bqc=bqc:
                   e.tensor_tensor(aot[blo:blo + 64, bqtile, bqc * 512:(bqc + 1) * 512],
                                   otmp[:], PP[0][0:64, :], ALU.mult),
                   waits=[(V, V.v)], inc=V)
            v_norm[b] = V.v
        for g8 in range(8):
            kwait = list(kv_done)
            if g8 >= 1:
                kwait.append((T, t_g_end[g8 - 1]))
            p.dma("sync", Skg, kg2[0:64, 0, :], KVS[g8 * 64:(g8 + 1) * 64, :], waits=kwait)
            p.dma("sync", Skg, kg2[64:128, 0, :], KVS[g8 * 64:(g8 + 1) * 64, :])
            p.dma("sync", Skg, vg[:, 0, :], KVS[512 + g8 * 64:512 + (g8 + 1) * 64, :])
            sy_kgv = Skg.v
            v_tr = {}
            for kc in range(32):
                trw = [(Skg, sy_kgv), (V, v_pre), (SpreG, spreg)]
                trw.append((V, prev_tr[kc % 2]) if kc < 2 else (V, v_tr[kc - 2]))
                p.emit("tensor",
                       lambda e, kc=kc: e.transpose(TRB[kc % 2][:, 0:64],
                                                    vg[:, 0, kc * 128:(kc + 1) * 128],
                                                    ident_s[0:64, 0:64]),
                       waits=trw, inc=T)
                p.emit("vector",
                       lambda e, kc=kc: e.tensor_copy(vtok[:, kc, 0:64], TRB[kc % 2][:, 0:64]),
                       waits=[(T, T.v)], inc=V)
                v_tr[kc] = V.v
            prev_tr = [v_tr[30], v_tr[31]]
            v_vtok = V.v
            for h in range(4):
                hh = g8 * 4 + h
                lo = (hh % 2) * 64
                qtile = hh // 2
                for qc in range(4):
                    a = hh * 4 + qc

                    def _st(kc, si0):
                        stw = []
                        if kc <= 1:
                            stw.append((V, v_vtok))
                        if si0 - 3 in c_exp:
                            stw.append((C, c_exp[si0 - 3]))
                        p.mm(STp3[si0 % 3][:, :],
                             kg2[lo:lo + 64, 0, kc * 128:(kc + 1) * 128],
                             qt[lo:lo + 64, qtile, qc * 512:(qc + 1) * 512],
                             start=True, stop=True, waits=stw, inc=T)
                        expw = [(T, T.v)]
                        if si0 - 3 in t_pv:
                            expw.append((T, t_pv[si0 - 3]))
                        p.emit("scalar",
                               lambda e, si0=si0: e.activation(out=pt[:, si0 % 3, :],
                                                               in_=STp3[si0 % 3][:, :],
                                                               func=AF.Exp, scale=f32(0.125)),
                               waits=expw, inc=C)
                        c_exp[si0] = C.v

                    _st(0, si)
                    _st(1, si + 1)
                    for kc in range(32):
                        if kc + 2 < 32:
                            _st(kc + 2, si + kc + 2)
                        s2 = si + kc
                        p.mm(OPp[a % 2][:, :], vtok[:, kc, :], pt[:, s2 % 3, :],
                             start=(kc == 0), stop=(kc == 31),
                             waits=[(C, c_exp[s2])] + ([(V, v_norm[a - 2])] if (kc == 0 and a - 2 in v_norm) else []),
                             inc=T)
                        t_pv[s2] = T.v
                    si += 32
                    drain(10)
                    p.emit("vector", lambda e, a=a: e.reciprocal(rrowb[:, a % 2, :],
                                                                 OPp[a % 2][64:65, :]),
                           waits=[(T, t_pv[si - 1])], inc=V)
                    v_recd[a] = V.v
                    if a >= 1:
                        _a_norm_tail(a - 1)
            t_g_end[g8] = T.v
        _a_norm_tail(127)
        v_aot = V.v

        t_wo_end = {}
        t_stat = {}
        v_y = {}
        v_lnread = 0
        widx = 0
        for tc in range(4):
            for fo in range(16):
                wwait = [(T, t_p1_end)] if widx < 2 else [(T, t_wo_end[widx - 2])]
                p.dma("gpsimd", Sw[widx % 2], wstr[:, widx % 2, :, :],
                      WO[:, fo * 128:(fo + 1) * 128].rearrange("(c p) f -> p c f", p=128),
                      waits=wwait)
                g_wo = Sw[widx % 2].v
                xw = [(V, v_y[widx - 2])] if widx >= 2 else []
                p.dma("sync", Sxr[widx % 2], xres[:, widx % 2, :], XTv[fo * 128:(fo + 1) * 128, tc, :],
                      waits=xw)
                sy_x = Sxr[widx % 2].v
                mwaits = [(Sw[widx % 2], g_wo), (V, v_aot)]
                if widx >= 2:
                    mwaits.append((V, v_y[widx - 2]))
                for fi in range(16):
                    p.mm(PP[widx % 2][:, :], wstr[:, widx % 2, fi, :],
                         aot[:, fi, tc * 512:(tc + 1) * 512],
                         start=(fi == 0), stop=(fi == 15),
                         waits=(mwaits if fi == 0 else ()), inc=(T if fi == 15 else None))
                t_wo_end[widx] = T.v
                p.emit("vector",
                       lambda e, widx=widx, fo=fo:
                       e.scalar_tensor_tensor(ybuf[:, fo, 0:512], PP[widx % 2][:, :], bocol[:, fo:fo + 1],
                                              xres[:, widx % 2, :], ALU.add, ALU.add),
                       waits=[(T, t_wo_end[widx]), (Sxr[widx % 2], sy_x), (Spre, spre)], inc=V)
                v_y[widx] = V.v
                sqw = [(V, v_y[widx])]
                if widx >= 2:
                    sqw.append((T, t_stat[widx - 2]))
                p.emit("scalar",
                       lambda e, widx=widx, fo=fo: e.activation(out=ysq[:, widx % 2, :],
                                                                in_=ybuf[:, fo, 0:512],
                                                                func=AF.Square),
                       waits=sqw, inc=C)
                c_sq = C.v
                statw = [(V, v_y[widx])]
                if fo == 0:
                    statw.append((V, v_lnread))
                p.mm(OPp[0][0:1, :], ones128[:, :], ybuf[:, fo, 0:512], start=(fo == 0), stop=(fo == 15),
                     waits=statw, inc=None)
                p.mm(OPp[1][0:1, :], ones128[:, :], ysq[:, widx % 2, :], start=(fo == 0), stop=(fo == 15),
                     waits=([(C, c_sq)] + ([(V, v_lnread)] if fo == 0 else [])), inc=T)
                t_stat[widx] = T.v
                drain(2)
                widx += 1
            ln_rows(p, C, V, T, srow, var_i=2, rstd_i=3, m_i=0, ms_i=1, mr_i=4,
                    statY=OPp[0][0:1, :], statY2=OPp[1][0:1, :], ntok_inv=1.0 / 2048.0,
                    waits_stats=[(T, T.v)])
            v_lnread = V.v
            v_rows = V.v
            p.mm(ST0[:, :], ones1[0:1, :], srow[0:1, 3, :], start=True, stop=True,
                 waits=[(V, v_rows)], inc=T)
            p.mm(ST1[:, :], ones1[0:1, :], srow[0:1, 4, :], start=True, stop=True, inc=T)
            t_bc2 = T.v
            p.emit("vector", lambda e: e.tensor_copy(bcast[:, 0, :], ST0[:, :]),
                   waits=[(T, t_bc2)], inc=V)
            p.emit("vector", lambda e: e.tensor_copy(bcast[:, 1, :], ST1[:, :]), inc=V)
            v_bc = V.v
            for fo in range(16):
                oidx = tc * 16 + fo
                ow = [(V, v_bc), (Sx1[oidx % 2], Sx1[oidx % 2].v)]
                p.emit("vector",
                       lambda e, fo=fo:
                       e.scalar_tensor_tensor(x1c[:, fo % 2, :], ybuf[:, fo, 0:512], 1.0,
                                              bcast[:, 0, :], ALU.mult, ALU.mult),
                       waits=ow, inc=V)
                p.emit("vector",
                       lambda e, fo=fo: e.tensor_tensor(x1c[:, fo % 2, :], x1c[:, fo % 2, :],
                                                        bcast[:, 1, :], ALU.subtract),
                       waits=[(V, V.v)], inc=V)
                p.dma("sync", Sx1[oidx % 2], X1Tv[fo * 128:(fo + 1) * 128, tc, :], x1c[:, fo % 2, :],
                      waits=[(V, V.v)])
        while bg:
            bg.pop(0)()
    return p




def emit_l0_ffn(p, nc, X1T, GIDXT, UIDXT, DIDXT, LUT, X2T, GSCR, USCR, DSCR, HSCR, MOVED_MAT=True):
    p.begin_part()
    NI = 5504
    NIC = 43
    with ExitStack() as ctx:
        sb = lambda *a: ctx.enter_context(nc.sbuf_tensor(*a, allow_name_mangling=True))
        psum = lambda *a: ctx.enter_context(nc.psum_tensor(*a, allow_name_mangling=True))
        sem = lambda n: p.S(n)
        arena = sb("arena", [128, 61440], BF16)
        ar = arena.ap()
        x1bf = ar[:, 0:32768].rearrange("p (c f) -> p c f", c=16)
        idxb = ar[:, 32768:43776].rearrange("p (c f) -> p c f", c=2)
        accb = ar[:, 43776:54784].rearrange("p (c f) -> p c f", c=2)
        lutrow = sb("lutrow", [1, 16], F32)
        lutcol = sb("lutcol", [128, 16], F32)
        ones1 = sb("ones1", [1, 128], F32)
        ones128 = sb("ones128", [128, 1], F32)
        gw = sb("gw", [128, 2, 16, 128], BF16)
        uw = sb("uw", [128, 2, 16, 128], BF16)
        sg = sb("sg", [128, 512], BF16)
        hc = sb("hc", [128, 2, 512], BF16)
        ht = ar[:, 0:22016].rearrange("p (c f) -> p c f", c=NIC)
        dw = ar[:, 22016:33024].rearrange("p (b c f) -> p b c f", b=2, c=NIC)
        xres = sb("xres", [128, 2, 512], F32)
        ybuf = ar[:, 33024:49408].bitcast(F32).rearrange("p (c f) -> p c f", c=16)
        ysq = sb("ysq", [128, 2, 512], F32)
        srow = sb("srow", [1, 5, 512], F32)
        bcast = sb("bcast", [128, 2, 512], F32)
        x2c = sb("x2c", [128, 2, 512], F32)
        PP = [psum("PP0", [128, 512], F32), psum("PP1", [128, 512], F32)]
        GP = [psum("GP0", [128, 512], F32), psum("GP1", [128, 512], F32)]
        UP = [psum("UP0", [128, 512], F32), psum("UP1", [128, 512], F32)]
        SB0 = psum("SB0", [128, 512], F32)
        SB1 = psum("SB1", [128, 512], F32)
        T = sem("sT"); C = sem("sC"); V = sem("sV")
        Sx1 = sem("dX1"); Sidx = [sem("dIdx0"), sem("dIdx1")]; Sacc = [sem("dAcc0"), sem("dAcc1")]
        Sgw = [sem("dGw0"), sem("dGw1")]; Suw = [sem("dUw0"), sem("dUw1")]
        Shw = [sem("dHw0"), sem("dHw1")]
        Sht = sem("dHt"); Sdw = [sem("dDw0"), sem("dDw1")]
        Sxr = [sem("dXr0"), sem("dXr1")]; So = [sem("dO0"), sem("dO1")]
        Spre = sem("dPre")

        X1Tv = X1T.rearrange("d (b c) -> d b c", c=512)
        X2Tv = X2T.rearrange("d (b c) -> d b c", c=512)

        p.emit("vector", lambda e: e.memset(ones1[:], 1.0), inc=V)
        p.emit("vector", lambda e: e.memset(ones128[:], 1.0), inc=V)
        v_ones = V.v
        p.dma("sync", Spre, lutrow[:], LUT)
        spre = Spre.v
        p.mm(SB1[:, 0:16], ones1[0:1, :], lutrow[0:1, :], start=True, stop=True,
             waits=[(Spre, spre), (V, v_ones)], inc=T)
        p.emit("vector", lambda e: e.tensor_copy(lutcol[:, :], SB1[:, 0:16]),
               waits=[(T, T.v)], inc=V)
        v_lut = V.v
        for fi in range(16):
            p.dma("gpsimd", Sx1, x1bf[:, fi, :], X1T[fi * 128:(fi + 1) * 128, :])
        g_x1 = Sx1.v

        sgm = ar[:, 54784:60288]
        def materialize(IDXT, SCR, nfi, F):
            nonlocal mat_i
            for fi in range(nfi):
                mi = mat_i
                iw = [(V, v_mat.get(mi - 2, v_lut))]
                p.dma("gpsimd", Sidx[mi % 2], idxb[:, mi % 2, 0:F],
                      IDXT[fi * 128:(fi + 1) * 128, :], waits=iw)
                g_i = Sidx[mi % 2].v
                aw = [(Sacc[mi % 2], Sacc[mi % 2].v), (Sidx[mi % 2], g_i), (V, v_lut)]
                p.emit("vector", lambda e, mi=mi, F=F: e.memset(accb[:, mi % 2, 0:F], 0.0),
                       waits=aw, inc=V)
                for vv in range(16):
                    p.emit("vector",
                           lambda e, mi=mi, vv=vv, F=F:
                           e.tensor_scalar(idxb[:, (mi % 2), 0:F] if False else sgm[:, 0:F],
                                           idxb[:, mi % 2, 0:F], float(vv), lutcol[:, vv:vv + 1],
                                           op0=ALU.is_equal, op1=ALU.mult),
                           waits=[(V, V.v)], inc=V)
                    p.emit("vector",
                           lambda e, mi=mi, F=F:
                           e.tensor_tensor(accb[:, mi % 2, 0:F], accb[:, mi % 2, 0:F],
                                           sgm[:, 0:F], ALU.add),
                           waits=[(V, V.v)], inc=V)
                v_mat[mi] = V.v
                p.dma("sync", Sacc[mi % 2], SCR[fi * 128:(fi + 1) * 128, 0:F],
                      accb[:, mi % 2, 0:F], waits=[(V, V.v)])
                mat_i += 1
        mat_i = 0
        v_mat = {}
        if MOVED_MAT is None:
            materialize(GIDXT, GSCR, 16, NI)
            materialize(UIDXT, USCR, 16, NI)
            materialize(DIDXT, DSCR, NIC, D)
        mat_done = [(Sacc[0], Sacc[0].v), (Sacc[1], Sacc[1].v)]

        t_gu_end = {}
        v_h = {}
        c_sg = {}
        for fo in range(NIC):
            wwait = list(mat_done) if fo < 2 else [(T, t_gu_end[fo - 2])]
            p.dma("gpsimd", Sgw[fo % 2], gw[:, fo % 2, :, :],
                  GSCR[:, fo * 128:(fo + 1) * 128].rearrange("(c p) f -> p c f", p=128),
                  waits=wwait)
            g_gw = Sgw[fo % 2].v
            p.dma("gpsimd", Suw[fo % 2], uw[:, fo % 2, :, :],
                  USCR[:, fo * 128:(fo + 1) * 128].rearrange("(c p) f -> p c f", p=128))
            g_uw = Suw[fo % 2].v
            for tc in range(4):
                k = fo * 4 + tc
                gw0 = [(Sgw[fo % 2], g_gw), (Sx1, g_x1)]
                if k - 2 in c_sg:
                    gw0.append((C, c_sg[k - 2]))
                for fi in range(16):
                    p.mm(GP[k % 2][:, :], gw[:, fo % 2, fi, :], x1bf[:, fi, tc * 512:(tc + 1) * 512],
                         start=(fi == 0), stop=(fi == 15),
                         waits=(gw0 if fi == 0 else ()), inc=(T if fi == 15 else None))
                t_g = T.v
                uw0 = [(Suw[fo % 2], g_uw)]
                if k - 2 in v_h:
                    uw0.append((V, v_h[k - 2]))
                for fi in range(16):
                    p.mm(UP[k % 2][:, :], uw[:, fo % 2, fi, :], x1bf[:, fi, tc * 512:(tc + 1) * 512],
                         start=(fi == 0), stop=(fi == 15),
                         waits=(uw0 if fi == 0 else ()), inc=(T if fi == 15 else None))
                t_u = T.v
                p.emit("scalar", lambda e, k=k: e.activation(out=sg[:, :], in_=GP[k % 2][:, :],
                                                             func=AF.Silu),
                       waits=[(T, t_g), (V, v_h.get(k - 1, 0))], inc=C)
                c_sg[k] = C.v
                hw = [(C, c_sg[k]), (T, t_u), (Shw[k % 2], Shw[k % 2].v)]
                p.emit("vector",
                       lambda e, k=k: e.tensor_tensor(hc[:, k % 2, :], sg[:, :], UP[k % 2][:, :],
                                                      ALU.mult),
                       waits=hw, inc=V)
                v_h[k] = V.v
                p.dma("sync", Shw[k % 2],
                      HSCR[fo * 128:(fo + 1) * 128, tc * 512:(tc + 1) * 512],
                      hc[:, k % 2, :], waits=[(V, V.v)])
            t_gu_end[fo] = T.v
        h_done = [(Shw[0], Shw[0].v), (Shw[1], Shw[1].v)]
        t_gu_all = T.v

        t_dn_end = {}
        t_stat = {}
        v_y = {}
        v_lnread = 0
        widx = 0
        for tc in range(4):
            hw2 = list(h_done) + ([(T, t_gu_all)] if tc == 0 else [(T, t_dn_end[(tc - 1) * 16 + 15])])
            p.dma("sync", Sht, ht[:, :, :],
                  HSCR[:, tc * 512:(tc + 1) * 512].rearrange("(c p) f -> p c f", p=128),
                  waits=hw2)
            g_ht = Sht.v
            for fo in range(16):
                wwait = [(T, t_gu_all)] if widx < 2 else [(T, t_dn_end[widx - 2])]
                p.dma("gpsimd", Sdw[widx % 2], dw[:, widx % 2, :, :],
                      DSCR[:, fo * 128:(fo + 1) * 128].rearrange("(c p) f -> p c f", p=128),
                      waits=wwait)
                g_dw = Sdw[widx % 2].v
                xw = [(V, v_y[widx - 2])] if widx >= 2 else []
                p.dma("sync", Sxr[widx % 2], xres[:, widx % 2, :], X1Tv[fo * 128:(fo + 1) * 128, tc, :],
                      waits=xw)
                sy_x = Sxr[widx % 2].v
                mw = [(Sdw[widx % 2], g_dw), (Sht, g_ht)]
                if widx >= 2:
                    mw.append((V, v_y[widx - 2]))
                for fi in range(NIC):
                    p.mm(PP[widx % 2][:, :], dw[:, widx % 2, fi, :], ht[:, fi, :],
                         start=(fi == 0), stop=(fi == NIC - 1),
                         waits=(mw if fi == 0 else ()), inc=(T if fi == NIC - 1 else None))
                t_dn_end[widx] = T.v
                p.emit("vector",
                       lambda e, widx=widx, fo=fo:
                       e.scalar_tensor_tensor(ybuf[:, fo, :], PP[widx % 2][:, :], 1.0,
                                              xres[:, widx % 2, :], ALU.mult, ALU.add),
                       waits=[(T, t_dn_end[widx]), (Sxr[widx % 2], sy_x)], inc=V)
                v_y[widx] = V.v
                sqw = [(V, v_y[widx])]
                if widx >= 2:
                    sqw.append((T, t_stat[widx - 2]))
                p.emit("scalar",
                       lambda e, widx=widx, fo=fo: e.activation(out=ysq[:, widx % 2, :],
                                                                in_=ybuf[:, fo, :],
                                                                func=AF.Square),
                       waits=sqw, inc=C)
                c_sq = C.v
                statw = [(V, v_y[widx])]
                if fo == 0:
                    statw.append((V, v_lnread))
                p.mm(SB0[0:1, :], ones128[:, :], ybuf[:, fo, :], start=(fo == 0), stop=(fo == 15),
                     waits=statw, inc=None)
                p.mm(SB1[0:1, :], ones128[:, :], ysq[:, widx % 2, :], start=(fo == 0), stop=(fo == 15),
                     waits=([(C, c_sq)] + ([(V, v_lnread)] if fo == 0 else [])), inc=T)
                t_stat[widx] = T.v
                widx += 1
            ln_rows(p, C, V, T, srow, var_i=2, rstd_i=3, m_i=0, ms_i=1, mr_i=4,
                    statY=SB0[0:1, :], statY2=SB1[0:1, :], ntok_inv=1.0 / 2048.0,
                    waits_stats=[(T, T.v)])
            v_lnread = V.v
            v_rows = V.v
            p.mm(GP[0][:, :], ones1[0:1, :], srow[0:1, 3, :], start=True, stop=True,
                 waits=[(V, v_rows)], inc=T)
            p.mm(GP[1][:, :], ones1[0:1, :], srow[0:1, 4, :], start=True, stop=True, inc=T)
            t_bc2 = T.v
            p.emit("vector", lambda e: e.tensor_copy(bcast[:, 0, :], GP[0][:, :]),
                   waits=[(T, t_bc2)], inc=V)
            p.emit("vector", lambda e: e.tensor_copy(bcast[:, 1, :], GP[1][:, :]), inc=V)
            v_bc = V.v
            for fo in range(16):
                oidx = tc * 16 + fo
                ow = [(V, v_bc), (So[oidx % 2], So[oidx % 2].v)]
                p.emit("vector",
                       lambda e, fo=fo:
                       e.scalar_tensor_tensor(x2c[:, fo % 2, :], ybuf[:, fo, :], 1.0,
                                              bcast[:, 0, :], ALU.mult, ALU.mult),
                       waits=ow, inc=V)
                p.emit("vector",
                       lambda e, fo=fo: e.tensor_tensor(x2c[:, fo % 2, :], x2c[:, fo % 2, :],
                                                        bcast[:, 1, :], ALU.subtract),
                       waits=[(V, V.v)], inc=V)
                p.dma("sync", So[oidx % 2], X2Tv[fo * 128:(fo + 1) * 128, tc, :], x2c[:, fo % 2, :],
                      waits=[(V, V.v)])
    return p




def emit_l1_attn(p, nc, X2T, X2FULLT, WQ, WK, WV, WOL, LQ1, LK1, LQ2, LK2, SUBW,
                 MASKS, MCOL, META, IDENT, X3T, X3TOK, KSCR, VSCR, EXCH=None):
    p.begin_part()
    LAMBDA_INIT = 0.8 - 0.6 * float(np.exp(-0.3 * 12))
    SCALE = float(128 ** -0.5)
    with ExitStack() as ctx:
        sb = lambda *a: ctx.enter_context(nc.sbuf_tensor(*a, allow_name_mangling=True))
        psum = lambda *a: ctx.enter_context(nc.psum_tensor(*a, allow_name_mangling=True))
        sem = lambda n: p.S(n)
        arena = sb("arena", [128, 32768], BF16)
        ar = arena.ap()
        x2bf = ar.rearrange("p (c f) -> p c f", c=16)
        aot = ar.rearrange("p (c f) -> p c f", c=16)
        arena2 = sb("arena2", [128, 32768], BF16)
        ar2 = arena2.ap()
        qt = ar2.rearrange("p (c f) -> p c f", c=16)
        ybuf = ar2[:, 0:16384].bitcast(F32).rearrange("p (c f) -> p c f", c=16)
        x3ts = ar2[:, 16384:32768].bitcast(F32).rearrange("p (b f) -> p b f", b=4)
        wstr = sb("wstr", [128, 2, 16, 128], BF16)
        kvarena = sb("kvarena", [128, 12288], BF16)
        kva = kvarena.ap()
        vw = kva[:, 0:8192].rearrange("p (c f) -> p c f", c=16)
        kt = kva[:, 0:4096].rearrange("p (c f) -> p c f", c=1)
        vs = kva[:, 4096:12288].rearrange("p (c f) -> p c f", c=32)
        evac = sb("evac", [128, 2, 512], BF16)
        masks = sb("masks", [128, 4, 512], BF16)
        mcol = sb("mcol", [128, 8], F32)
        pt = sb("pt", [128, 2, 512], BF16)
        cf32 = sb("cf32", [128, 2, 4, 512], F32)
        sqb2 = sb("sqb2", [128, 2, 512], F32)
        lam = sb("lam", [128, 4], F32)
        lrows = sb("lrows", [1, 8], F32)
        subcol = sb("subcol", [128, 2], F32)
        ones1 = sb("ones1", [1, 128], F32)
        ones128 = sb("ones128", [128, 1], F32)
        onesbf = sb("onesbf", [128, 1], BF16)
        ident_s = sb("ident_s", [128, 128], F32)
        xres = sb("xres", [128, 2, 512], F32)
        ysq = sqb2.ap()
        srow = sb("srow", [1, 5, 512], F32)
        crec = srow.ap()[:, 1:5, :].rearrange("a (p b) q -> a p b q", p=2)
        bcast = masks.ap().rearrange("p j q -> p (j q)").bitcast(F32).rearrange("p (c f) -> p c f", c=2)
        x3c = sb("x3c", [128, 2, 512], F32)
        meta_s = sb("meta_s", [1, 64], U32)
        PS = [psum("PS0", [128, 512], F32), psum("PS1", [128, 512], F32)]
        OAp = psum("OAp", [128, 512], F32)
        OBp = psum("OBp", [128, 512], F32)
        ODp = psum("ODp", [128, 512], F32)
        CB0 = psum("CB0", [128, 512], F32)
        CB1 = psum("CB1", [128, 512], F32)
        CB2 = psum("CB2", [128, 512], F32)
        OD = [ODp, CB2]
        T = sem("sT"); C = sem("sC"); V = sem("sV")
        Spre = sem("dPre"); SpreG = sem("dPreG"); Sx2 = sem("dX2")
        Sw = [sem("dW0"), sem("dW1")]
        Svw = sem("dVw")
        Skw = [sem("dKw0"), sem("dKw1")]
        Svv = [sem("dVv0"), sem("dVv1")]
        Skt = sem("dKt")
        Svs = sem("dVs")
        Sxr = [sem("dXr0"), sem("dXr1")]
        So = [sem("dO0"), sem("dO1")]
        Sot = sem("dOt")

        X2Tv = X2T.rearrange("d (b c) -> d b c", c=512)
        X3Tv = X3T.rearrange("d (b c) -> d b c", c=512)
        KSCRq = KSCR.rearrange("d (i q c) -> d i q c", q=2, c=512)
        VSCRq = VSCR.rearrange("(i q b c) f -> i q b c f", q=2, b=4, c=128)


        p.emit("vector", lambda e: e.memset(ones1[:], 1.0), inc=V)
        p.emit("vector", lambda e: e.memset(ones128[:], 1.0), inc=V)
        p.emit("vector", lambda e: e.memset(onesbf[:], 1.0), inc=V)
        v_ones = V.v
        p.dma("sync", Spre, meta_s[:], META)
        p.dma("sync", Spre, lam[:, 0:1], LQ1, waits=[(Spre, Spre.v)])
        p.dma("sync", Spre, lam[:, 1:2], LK1, waits=[(Spre, Spre.v)])
        p.dma("sync", Spre, lam[:, 2:3], LQ2, waits=[(Spre, Spre.v)])
        p.dma("sync", Spre, lam[:, 3:4], LK2, waits=[(Spre, Spre.v)])
        p.dma("sync", Spre, subcol[:, :], SUBW, waits=[(Spre, Spre.v)])
        spre = Spre.v
        p.dma("gpsimd", SpreG, ident_s[:], IDENT)
        p.dma("gpsimd", SpreG, masks[:, :, :], MASKS[:, 0:4, :])
        p.dma("sync", Spre, mcol[:, :], MCOL, waits=[(Spre, Spre.v)])
        spre = Spre.v
        spreg = SpreG.v
        p.emit("vector", lambda e: e.tensor_tensor(lam[:, 0:1], lam[:, 0:1], lam[:, 1:2], ALU.mult),
               waits=[(Spre, spre)], inc=V)
        p.emit("vector", lambda e: e.tensor_tensor(lam[:, 1:2], lam[:, 2:3], lam[:, 3:4], ALU.mult),
               waits=[(V, V.v)], inc=V)
        v_lamprod = V.v
        p.mm(OD[0][0:1, 0:2], ones128[:, 0:1], lam[:, 0:2], start=True, stop=True,
             waits=[(V, v_lamprod)], inc=T)
        p.emit("scalar", lambda e: e.activation(out=lrows[0:1, 0:2], in_=OD[0][0:1, 0:2],
                                                func=AF.Exp), waits=[(T, T.v)], inc=C)
        p.emit("vector", lambda e: e.tensor_tensor(lrows[0:1, 2:3], lrows[0:1, 0:1],
                                                   lrows[0:1, 1:2], ALU.subtract),
               waits=[(C, C.v)], inc=V)
        p.emit("vector", lambda e: e.tensor_scalar(lrows[0:1, 3:4], lrows[0:1, 2:3],
                                                   float(LAMBDA_INIT), -1.0,
                                                   op0=ALU.add, op1=ALU.mult),
               waits=[(V, V.v)], inc=V)
        v_nlam = V.v
        p.mm(OD[1][:, 0:1], ones1[0:1, :], lrows[0:1, 3:4], start=True, stop=True,
             waits=[(V, v_nlam)], inc=T)
        p.emit("vector", lambda e: e.tensor_copy(lam[:, 2:3], OD[1][:, 0:1]),
               waits=[(T, T.v)], inc=V)
        p.emit("vector", lambda e: e.tensor_scalar_mul(subcol[:, :], subcol[:, :],
                                                       float(1.0 - LAMBDA_INIT)),
               waits=[(Spre, spre), (V, V.v)], inc=V)
        v_const = V.v

        HOLD = {}
        if EXCH is not None:
            Sex = sem("dEx")
            Sfl = sem("dFl")
            Ssp = sem("dSp")
            flag_in = sb("flag_in", [1, 16], U32)
            shflagv = EXCH["shflag"].rearrange("(a one) f -> a one f", one=1)

            def _push(e):
                HOLD["po"] = e.value_load(meta_s[0:1, 40:41])
                HOLD["pa"] = e.value_load(meta_s[0:1, 41:42])
                HOLD["nonce"] = e.value_load(meta_s[0:1, 42:43])
                return e.dma_start(out=EXCH["shx2"][HOLD["po"]], in_=X2T)
            p.emit("gpsimd", _push, waits=[(Spre, spre)], inc=Sex, incn=16)
            p.emit("gpsimd",
                   lambda e: e.dma_start(out=shflagv[HOLD["po"]][:, 0:1],
                                         in_=meta_s[0:1, 42:43]),
                   waits=[(Sex, Sex.v)], inc=Sfl, incn=16)

            def _spin(e):
                if EXCH.get("stub"):
                    return None
                eng = [e.engine]
                loop_id = nc.next_id()
                ln_, dn_ = f"spin_{loop_id}", f"spin_{loop_id}_done"
                cnt = nc.alloc_registers(f"cnt_{loop_id}", engines=eng)
                nc.regs_mov(cnt, Ssp.v)
                nc.br(ln_, engines=eng)
                with nc.body(ln_, valid_engines=eng):
                    nc.regs_alu(cnt, cnt, 16, op=ALU.add)
                    e.dma_start(out=flag_in[:, 0:1], in_=shflagv[HOLD["pa"]][:, 0:1]).then_inc(Ssp.sem, 16)
                    e.wait_ge(Ssp.sem, nc.snap(cnt, engines=eng))
                    v = e.value_load(flag_in[0:1, 0:1])
                    nc.br_cmp(v, HOLD["nonce"], on_true=dn_, on_false=ln_,
                              comp_op="IS_EQ", engines=eng)
                nc.switch_bb(dn_)
                return None

        t_pj = {}
        v_pj = {}
        v_vev = {}
        gidx = 0
        kev = 0
        vev = 0
        t_half_end = 0
        for half in range(2):
            if EXCH is not None and half == 1:
                p.emit("gpsimd", _spin, waits=[(Sfl, Sfl.v)])
            for fi in range(16):
                hw0 = [(T, t_half_end)] if (half == 1 and fi == 0) else []
                if EXCH is None:
                    p.dma("gpsimd", Sx2, x2bf[:, fi, :],
                          X2FULLT[fi * 128:(fi + 1) * 128, half * 2048:(half + 1) * 2048],
                          waits=hw0)
                elif half == 0:
                    p.dma("gpsimd", Sx2, x2bf[:, fi, :],
                          X2T[fi * 128:(fi + 1) * 128, :], waits=hw0)
                else:
                    key = "po" if EXCH.get("stub") else "pa"
                    p.emit("gpsimd",
                           lambda e, fi=fi, key=key:
                           e.dma_start(out=x2bf[:, fi, :],
                                       in_=EXCH["shx2"][HOLD[key]][fi * 128:(fi + 1) * 128, :]),
                           waits=hw0, inc=Sx2, incn=16)
            g_x2 = Sx2.v
            wlist = [("k", WK)] + ([("q", WQ)] if half == 0 else [])
            v_guard = V.v
            kseg = 0
            for wname, W in wlist:
                for fo in range(16):
                    wwait = [(T, t_pj[gidx - 2])] if gidx >= 2 else []
                    p.dma("gpsimd", Sw[gidx % 2], wstr[:, gidx % 2, :, :],
                          W[:, fo * 128:(fo + 1) * 128].rearrange("(c p) f -> p c f", p=128),
                          waits=wwait)
                    g_w = Sw[gidx % 2].v
                    for tc in range(4):
                        k = gidx * 4 + tc
                        mw = [(Sw[gidx % 2], g_w), (Sx2, g_x2)]
                        if kseg < 2:
                            mw.append((V, v_guard))
                        elif k - 2 in v_pj:
                            mw.append((V, v_pj[k - 2]))
                        kseg += 1
                        for fi in range(16):
                            p.mm(PS[k % 2][:, :], wstr[:, gidx % 2, fi, :],
                                 x2bf[:, fi, tc * 512:(tc + 1) * 512],
                                 start=(fi == 0), stop=(fi == 15),
                                 waits=(mw if fi == 0 else ()), inc=(T if fi == 15 else None))
                        t_mm = T.v
                        if wname == "q":
                            p.emit("vector",
                                   lambda e, k=k, fo=fo, tc=tc:
                                   e.tensor_copy(qt[:, fo, tc * 512:(tc + 1) * 512], PS[k % 2][:, :]),
                                   waits=[(T, t_mm)], inc=V)
                        else:
                            ej = kev % 2
                            p.emit("vector",
                                   lambda e, k=k, ej=ej: e.tensor_copy(evac[:, ej, :], PS[k % 2][:, :]),
                                   waits=[(T, t_mm), (Skw[ej], Skw[ej].v), (Svv[ej], Svv[ej].v)], inc=V)
                            p.emit("gpsimd",
                                   lambda e, ej=ej, fo=fo, half=half, tc=tc:
                                   e.dma_start(out=KSCRq[fo * 128:(fo + 1) * 128, tc, half, :],
                                               in_=evac[:, ej, :]),
                                   waits=[(V, V.v)], inc=Skw[ej], incn=16)
                            kev += 1
                        v_pj[k] = V.v
                    t_pj[gidx] = T.v
                    gidx += 1
            v_guard2 = V.v
            vseg = 0
            for fc in range(4):
                p.dma("gpsimd", Svw, vw[:, :, :],
                      WV[:, fc * 512:(fc + 1) * 512].rearrange("(c p) f -> p c f", p=128),
                      waits=[(T, t_pj[gidx - 1])] if (half, fc) != (0, 0) else [])
                g_vw = Svw.v
                for tb in range(16):
                    k = gidx * 4 + tb
                    mw = [(Svw, g_vw), (Sx2, g_x2)]
                    if vseg < 2:
                        mw.append((V, v_guard2))
                    elif vev >= 2:
                        mw.append((V, v_vev[vev - 2]))
                    vseg += 1
                    for fi in range(16):
                        p.mm(PS[(vev) % 2][:, :], x2bf[:, fi, tb * 128:(tb + 1) * 128],
                             vw[:, fi, :],
                             start=(fi == 0), stop=(fi == 15),
                             waits=(mw if fi == 0 else ()), inc=(T if fi == 15 else None))
                    t_mm = T.v
                    ej = vev % 2
                    evw = [(T, t_mm), (Svv[vev % 2], Svv[vev % 2].v), (Skw[ej], Skw[ej].v)]
                    p.emit("vector",
                           lambda e, ej=ej, vev=vev: e.tensor_copy(evac[:, ej, :], PS[vev % 2][:, :]),
                           waits=evw, inc=V)
                    v_vev[vev] = V.v
                    p.emit("gpsimd",
                           lambda e, ej=ej, fc=fc, half=half, tb=tb:
                           e.dma_start(out=VSCRq[tb // 4, half, tb % 4, :, fc * 512:(fc + 1) * 512],
                                       in_=evac[:, ej, :]),
                           waits=[(V, V.v)], inc=Svv[vev % 2], incn=16)
                    vev += 1
                t_pj[gidx] = T.v
                gidx += 1
            t_half_end = T.v
        kv_done = [(Skw[0], Skw[0].v), (Skw[1], Skw[1].v), (Svv[0], Svv[0].v), (Svv[1], Svv[1].v)]
        t_p1_end = T.v

        si = 0
        c_exp = {}
        v_mask = {}
        t_pv = {}
        v_comb = {}
        v_stage0 = {}
        v_part1 = {}
        c_rstd_d = {}
        t_ssum_d = {}
        t_h_end = {}

        def _c_part1(bb):
            pw = [(V, v_stage0[(bb, 1)])]
            if bb - 1 in v_comb:
                pw.append((V, v_comb[bb - 1]))
            p.mm(CB0[:, :], ones1[0:1, :], crec[0:1, bb % 2, 0, :], start=True, stop=True,
                 waits=pw, inc=T)
            p.mm(CB1[:, :], ones1[0:1, :], crec[0:1, bb % 2, 1, :], start=True, stop=True, inc=T)
            t_bc = T.v
            for j in range(4):
                p.emit("vector",
                       lambda e, bb=bb, j=j: e.tensor_tensor(cf32[:, bb % 2, j, :],
                                                             cf32[:, bb % 2, j, :],
                                                             (CB0 if j < 2 else CB1)[:, :], ALU.mult),
                       waits=[(T, t_bc), (V, V.v)], inc=V)
            for j in range(2):
                p.emit("vector",
                       lambda e, bb=bb, j=j:
                       e.scalar_tensor_tensor(cf32[:, bb % 2, j, :], cf32[:, bb % 2, 2 + j, :],
                                              lam[:, 2:3], cf32[:, bb % 2, j, :],
                                              ALU.mult, ALU.add),
                       waits=[(V, V.v)], inc=V)
            v_part1[bb] = V.v
            sqw = [(V, v_part1[bb])]
            if bb - 1 in t_ssum_d:
                sqw.append((T, t_ssum_d[bb - 1]))
            p.emit("scalar", lambda e, bb=bb: e.activation(out=sqb2[:, 0, :],
                                                           in_=cf32[:, bb % 2, 0, :],
                                                           func=AF.Square), waits=sqw, inc=C)
            p.emit("scalar", lambda e, bb=bb: e.activation(out=sqb2[:, 1, :],
                                                           in_=cf32[:, bb % 2, 1, :],
                                                           func=AF.Square), inc=C)
            c_rstd_d[bb] = C.v

        def _c_part2(bb):
            ssw = [(C, c_rstd_d[bb])]
            if bb - 1 in c_rstd_d:
                pass
            p.mm(CB2[0:1, :], ones128[:, 0:1], sqb2[:, 0, :], start=True, stop=False,
                 waits=ssw, inc=None)
            p.mm(CB2[0:1, :], ones128[:, 0:1], sqb2[:, 1, :], start=False, stop=True, inc=T)
            t_ssum_d[bb] = T.v
            p.emit("scalar", lambda e: e.activation(out=srow[0:1, 0, :], in_=CB2[0:1, :],
                                                    func=AF.Ln, bias=EPS, scale=f32(1.0 / 256.0)),
                   waits=[(T, t_ssum_d[bb]), (C, C.v)], inc=C)
            p.emit("scalar", lambda e: e.activation(out=srow[0:1, 0, :], in_=srow[0:1, 0, :],
                                                    func=AF.Exp, scale=f32(-0.5)),
                   waits=[(C, C.v)], inc=C)
            c_r = C.v
            p.mm(CB0[:, :], ones1[0:1, :], srow[0:1, 0, :], start=True, stop=True,
                 waits=[(C, c_r), (V, v_part1[bb])], inc=T)
            t_bcr = T.v
            for j in range(2):
                p.emit("vector",
                       lambda e, bb=bb, j=j: e.tensor_tensor(cf32[:, bb % 2, j, :],
                                                             cf32[:, bb % 2, j, :],
                                                             CB0[:, :], ALU.mult),
                       waits=[(T, t_bcr), (V, V.v)], inc=V)
                p.emit("vector",
                       lambda e, bb=bb, j=j:
                       e.tensor_scalar(aot[:, 2 * (bb // 4) + j, (bb % 4) * 512:(bb % 4) * 512 + 512],
                                       cf32[:, bb % 2, j, :], subcol[:, j:j + 1], None,
                                       op0=ALU.mult),
                       waits=[(V, V.v)], inc=V)
            v_comb[bb] = V.v
        for h in range(8):
            vsw = list(kv_done)
            if h >= 1:
                vsw.append((T, t_h_end[h - 1]))
            p.dma("sync", Svs, vs[:, :, :],
                  VSCR[:, h * 256:(h + 1) * 256].rearrange("(c p) f -> p c f", p=128),
                  waits=vsw)
            s_vs = Svs.v

            for qc in range(4):
                nkc = 8 * qc + 8
                aa = h * 4 + qc
                for br in range(2):
                    ktw = [(T, t_pv[si - 1])] if si >= 1 else []
                    p.dma("sync", Skt, kt[:, 0, :],
                          KSCR[(2 * h + br) * 128:(2 * h + br + 1) * 128, :], waits=ktw)
                    s_kt = Skt.v
                    for kc2 in range(0, nkc, 2):
                        t_sts = {}
                        for o in (0, 1):
                            kc = kc2 + o
                            s2 = si + o
                            stw = [(C, c_exp[s2 - 2])] if s2 - 2 in c_exp else []
                            if kc == 0:
                                stw += [(Skt, s_kt), (Svs, s_vs), (SpreG, spreg), (Spre, spre)]
                                prevk = (aa, br - 1) if br == 1 else (aa - 1, 1)
                                if prevk in v_stage0:
                                    stw.append((V, v_stage0[prevk]))
                            p.mm(PS[s2 % 2][:, :], kt[:, 0, kc * 128:(kc + 1) * 128],
                                 qt[:, 2 * h + br, qc * 512:(qc + 1) * 512],
                                 start=True, stop=True, waits=stw, inc=T)
                            t_sts[o] = T.v
                        vm = {}
                        for o in (0, 1):
                            kc = kc2 + o
                            s2 = si + o
                            expw = [(T, t_sts[o])]
                            if s2 - 2 in t_pv:
                                expw.append((T, t_pv[s2 - 2]))
                            p.emit("scalar",
                                   lambda e, s2=s2: e.activation(out=pt[:, s2 % 2, :],
                                                                 in_=PS[s2 % 2][:, :],
                                                                 func=AF.Exp, scale=f32(SCALE)),
                                   waits=expw, inc=C)
                            c_exp[s2] = C.v
                            if kc >= 8 * qc:
                                j = kc - 8 * qc
                                if j < 4:
                                    p.emit("vector",
                                           lambda e, s2=s2, j=j:
                                           e.tensor_tensor(pt[:, s2 % 2, :], pt[:, s2 % 2, :],
                                                           masks[:, j, :], ALU.mult),
                                           waits=[(C, c_exp[s2]), (V, V.v)], inc=V)
                                else:
                                    p.emit("vector",
                                           lambda e, s2=s2, j=j:
                                           e.tensor_scalar_mul(pt[:, s2 % 2, :], pt[:, s2 % 2, :],
                                                               mcol[:, j:j + 1]),
                                           waits=[(C, c_exp[s2]), (V, V.v)], inc=V)
                                vm[o] = V.v
                        for o in (0, 1):
                            kc = kc2 + o
                            s2 = si + o
                            pvw0 = [(V, vm[o])] if o in vm else [(C, c_exp[s2])]
                            p.mm(OAp[:, :], vs[:, kc, 0:128], pt[:, s2 % 2, :],
                                 start=(kc == 0), stop=(kc == nkc - 1), waits=pvw0, inc=None)
                            p.mm(OBp[:, :], vs[:, kc, 128:256], pt[:, s2 % 2, :],
                                 start=(kc == 0), stop=(kc == nkc - 1), inc=None)
                            p.mm(ODp[0:1, :], onesbf[:, 0:1], pt[:, s2 % 2, :],
                                 start=(kc == 0), stop=(kc == nkc - 1), inc=T)
                            t_pv[s2] = T.v
                        si += 2
                    p.emit("vector",
                           lambda e, aa=aa, br=br: e.tensor_copy(cf32[:, aa % 2, 2 * br, :], OAp[:, :]),
                           waits=[(T, t_pv[si - 1])], inc=V)
                    p.emit("vector",
                           lambda e, aa=aa, br=br: e.tensor_copy(cf32[:, aa % 2, 2 * br + 1, :], OBp[:, :]),
                           inc=V)
                    p.emit("vector",
                           lambda e, aa=aa, br=br: e.reciprocal(crec[0:1, aa % 2, br, :], ODp[0:1, :]),
                           inc=V)
                    v_stage0[(aa, br)] = V.v
                    if br == 0 and aa >= 1:
                        _c_part1(aa - 1)
                if aa >= 1:
                    _c_part2(aa - 1)
            t_h_end[h] = T.v
        _c_part1(31)
        _c_part2(31)
        v_aot = V.v

        t_wo_end = {}
        t_stat = {}
        v_y = {}
        t_tr = {}
        v_lnread = 0
        widx = 0
        for tc in range(4):
            for fo in range(16):
                wwait = [(T, t_p1_end)] if widx < 2 else [(T, t_wo_end[widx - 2])]
                p.dma("gpsimd", Sw[widx % 2], wstr[:, widx % 2, :, :],
                      WOL[:, fo * 128:(fo + 1) * 128].rearrange("(c p) f -> p c f", p=128),
                      waits=wwait)
                g_wo = Sw[widx % 2].v
                xw = [(V, v_y[widx - 2])] if widx >= 2 else []
                p.dma("sync", Sxr[widx % 2], xres[:, widx % 2, :], X2Tv[fo * 128:(fo + 1) * 128, tc, :],
                      waits=xw)
                sy_x = Sxr[widx % 2].v
                mw = [(Sw[widx % 2], g_wo), (V, v_aot)]
                if widx >= 2:
                    mw.append((V, v_y[widx - 2]))
                for fi in range(16):
                    p.mm(PS[widx % 2][:, :], wstr[:, widx % 2, fi, :],
                         aot[:, fi, tc * 512:(tc + 1) * 512],
                         start=(fi == 0), stop=(fi == 15),
                         waits=(mw if fi == 0 else ()), inc=(T if fi == 15 else None))
                t_wo_end[widx] = T.v
                p.emit("vector",
                       lambda e, widx=widx, fo=fo:
                       e.scalar_tensor_tensor(ybuf[:, fo, :], PS[widx % 2][:, :], 1.0,
                                              xres[:, widx % 2, :], ALU.mult, ALU.add),
                       waits=[(T, t_wo_end[widx]), (Sxr[widx % 2], sy_x)], inc=V)
                v_y[widx] = V.v
                sqw = [(V, v_y[widx])]
                if widx >= 1:
                    sqw.append((T, t_stat[widx - 1]))
                p.emit("scalar",
                       lambda e, widx=widx, fo=fo: e.activation(out=ysq[:, 0, :],
                                                                in_=ybuf[:, fo, :],
                                                                func=AF.Square),
                       waits=sqw, inc=C)
                c_sq2 = C.v
                statw = [(V, v_y[widx])]
                if fo == 0:
                    statw.append((V, v_lnread))
                p.mm(OAp[0:1, :], ones128[:, 0:1], ybuf[:, fo, :], start=(fo == 0), stop=(fo == 15),
                     waits=statw, inc=None)
                p.mm(OBp[0:1, :], ones128[:, 0:1], ysq[:, 0, :], start=(fo == 0), stop=(fo == 15),
                     waits=([(C, c_sq2)] + ([(V, v_lnread)] if fo == 0 else [])), inc=T)
                t_stat[widx] = T.v
                widx += 1
            ln_rows(p, C, V, T, srow, var_i=2, rstd_i=3, m_i=0, ms_i=1, mr_i=2,
                    statY=OAp[0:1, :], statY2=OBp[0:1, :], ntok_inv=1.0 / 2048.0,
                    waits_stats=[(T, T.v)])
            v_lnread = V.v
            v_rows = V.v
            p.mm(OD[0][:, :], ones1[0:1, :], srow[0:1, 3, :], start=True, stop=True,
                 waits=[(V, v_rows)], inc=T)
            p.mm(OD[1][:, :], ones1[0:1, :], srow[0:1, 2, :], start=True, stop=True, inc=T)
            t_bc2 = T.v
            p.emit("vector", lambda e: e.tensor_copy(bcast[:, 0, 0:512], OD[0][:, :]),
                   waits=[(T, t_bc2)], inc=V)
            p.emit("vector", lambda e: e.tensor_copy(bcast[:, 1, 0:512], OD[1][:, :]), inc=V)
            v_bc = V.v
            sot_prev = Sot.v
            for fo in range(16):
                oidx = tc * 16 + fo
                ow = [(V, v_bc), (So[oidx % 2], So[oidx % 2].v)]
                if oidx - 2 in t_tr:
                    ow.append((T, t_tr[oidx - 2]))
                p.emit("vector",
                       lambda e, fo=fo:
                       e.scalar_tensor_tensor(x3c[:, fo % 2, :], ybuf[:, fo, :], 1.0,
                                              bcast[:, 0, 0:512], ALU.mult, ALU.mult),
                       waits=ow, inc=V)
                p.emit("vector",
                       lambda e, fo=fo: e.tensor_tensor(x3c[:, fo % 2, :], x3c[:, fo % 2, :],
                                                        bcast[:, 1, 0:512], ALU.subtract),
                       waits=[(V, V.v)], inc=V)
                v_x3 = V.v
                p.dma("sync", So[oidx % 2], X3Tv[fo * 128:(fo + 1) * 128, tc, :], x3c[:, fo % 2, :],
                      waits=[(V, v_x3)])
                for tb in range(4):
                    trw = [(V, v_x3), (SpreG, spreg)]
                    if (oidx, tb) != (0, 0):
                        trw.append((V, V.v))
                    p.emit("tensor",
                           lambda e, fo=fo, tb=tb:
                           e.transpose(CB0[:, 0:128].bitcast(F32) if tb % 2 == 0 else CB1[:, 0:128].bitcast(F32),
                                       x3c[:, fo % 2, tb * 128:(tb + 1) * 128], ident_s[:]),
                           waits=trw, inc=T)
                    p.emit("vector",
                           lambda e, fo=fo, tb=tb:
                           e.tensor_copy(x3ts[:, tb, fo * 128:(fo + 1) * 128],
                                         (CB0[:, 0:128] if tb % 2 == 0 else CB1[:, 0:128])),
                           waits=[(T, T.v), (Sot, sot_prev)], inc=V)
                t_tr[oidx] = T.v
            v_fin = V.v
            for tb in range(4):
                p.dma("sync", Sot, X3TOK[tc * 512 + tb * 128: tc * 512 + (tb + 1) * 128, :],
                      x3ts[:, tb, :], waits=[(V, v_fin)])
    return p




def emit_l1_moe(p, nc, X3T, X3TOK, RWT, RB, EWT, EB, OUT, ASCR):
    p.begin_part()
    BIG = 60000.0
    with ExitStack() as ctx:
        sb = lambda *a: ctx.enter_context(nc.sbuf_tensor(*a, allow_name_mangling=True))
        psum = lambda *a: ctx.enter_context(nc.psum_tensor(*a, allow_name_mangling=True))
        sem = lambda n: p.S(n)
        arena = sb("arena", [128, 32768], BF16)
        x3bf = arena.ap().rearrange("p (c f) -> p c f", c=16)
        arena2 = sb("arena2", [128, 16384], BF16)
        lg = arena2.ap().bitcast(F32).rearrange("p (c f) -> p c f", c=16)
        acc = lg
        rw = sb("rw", [128, 2, 16, 512], BF16)
        ew = rw
        farena = sb("farena", [128, 28672], BF16)
        far = farena.ap()
        rbb = far[:, 0:16384].rearrange("p (c f) -> p c f", c=32)
        t2 = far[:, 16384:26624].bitcast(F32).rearrange("p (c f) -> p c f", c=5)
        cbuf = sb("cbuf", [128, 16, 8], F32)
        chalf = sb("chalf", [128, 2, 8], F32)
        rbrow = sb("rbrow", [1, 2, 512], F32)
        ebrow = sb("ebrow", [1, 2, 512], F32)
        ones1 = sb("ones1", [1, 128], F32)
        yb = far[:, 0:8192].bitcast(F32).rearrange("p (c f) -> p c f", c=2)
        xtc = far[:, 8192:16384].bitcast(F32).rearrange("p (c f) -> p c f", c=2)
        outc = far[:, 16384:24576].bitcast(F32).rearrange("p (c f) -> p c f", c=2)
        sqf = far[:, 24576:28672].bitcast(F32)
        cols = sb("cols", [128, 8], F32)
        LG = [psum("LG0", [128, 512], F32), psum("LG1", [128, 512], F32),
              psum("LG2", [128, 512], F32), psum("LG3", [128, 512], F32)]
        RBP = psum("RBP", [128, 512], F32)
        T = sem("sT"); C = sem("sC"); V = sem("sV")
        Sx3 = sem("dX3"); Srw = [sem("dRw0"), sem("dRw1")]
        Srb = [sem("dRb0"), sem("dRb1")]
        Seb = [sem("dEb0"), sem("dEb1")]
        Sac = [sem("dAc0"), sem("dAc1")]
        Say = sem("dAy"); Syb = [sem("dYb0"), sem("dYb1")]
        Sxt = [sem("dXt0"), sem("dXt1")]
        So = [sem("dO0"), sem("dO1")]

        p.emit("vector", lambda e: e.memset(ones1[:], 1.0), inc=V)
        v_ones = V.v
        for fi in range(16):
            p.dma("gpsimd", Sx3, x3bf[:, fi, :], X3T[fi * 128:(fi + 1) * 128, :])
        g_x3 = Sx3.v
        t_rbb = {}
        v_rbb = {}
        for rc in range(32):
            p.dma("sync", Srb[rc % 2], rbrow[:, rc % 2, :], RB[:, rc * 512:(rc + 1) * 512],
                  waits=([(T, t_rbb[rc - 2])] if rc >= 2 else []))
            s_rb = Srb[rc % 2].v
            p.mm(RBP[:, :], ones1[0:1, :], rbrow[0:1, rc % 2, :], start=True, stop=True,
                 waits=[(Srb[rc % 2], s_rb), (V, v_ones)] + ([(V, v_rbb[rc - 1])] if rc >= 1 else []),
                 inc=T)
            t_rbb[rc] = T.v
            p.emit("vector", lambda e, rc=rc: e.tensor_copy(rbb[:, rc, :], RBP[:, :]),
                   waits=[(T, t_rbb[rc])], inc=V)
            v_rbb[rc] = V.v
        v_rbb_all = V.v

        pend_top2 = []

        def _top2_body(lgv, tcb, g4, v_half, VE):
            m1 = t2[:, 0, 0:512]
            m2 = t2[:, 1, 0:512]
            tmp = t2[:, 2, 0:512]
            w1 = t2[:, 3, 0:512]
            dd = t2[:, 4, 0:512]
            VE(lambda e: e.tensor_tensor(m1, lgv[:, :, 0], lgv[:, :, 1], ALU.max),
               w=[(V, v_half), (V, V.v)])
            for e8 in range(2, 8):
                VE(lambda e, e8=e8: e.tensor_tensor(m1, m1, lgv[:, :, e8], ALU.max))
            for e8 in range(8):
                VE(lambda e, e8=e8: e.tensor_tensor(tmp, lgv[:, :, e8], m1, ALU.is_equal))
                VE(lambda e, e8=e8: e.scalar_tensor_tensor(tmp, tmp, -BIG, lgv[:, :, e8],
                                                           ALU.mult, ALU.add))
                if e8 == 0:
                    VE(lambda e: e.tensor_copy(m2, tmp))
                else:
                    VE(lambda e: e.tensor_tensor(m2, m2, tmp, ALU.max))
            VE(lambda e: e.tensor_tensor(dd, m1, m2, ALU.subtract))
            p.emit("scalar", lambda e: e.activation(out=w1, in_=dd, func=AF.Sigmoid),
                   waits=[(V, V.v)], inc=C)
            c_w1 = C.v
            for e8 in range(8):
                VE(lambda e, e8=e8: e.scalar_tensor_tensor(
                    tmp, lgv[:, :, e8], 1.0, m2, ALU.mult, ALU.is_equal,
                    accum_out=chalf[:, 0, e8:e8 + 1]), w=[(V, V.v), (C, c_w1)])
                VE(lambda e, e8=e8: e.tensor_tensor(dd, lgv[:, :, e8], m1, ALU.is_equal))
                VE(lambda e: e.tensor_tensor(dd, dd, tmp, ALU.subtract))
                VE(lambda e, e8=e8: e.scalar_tensor_tensor(
                    tmp, dd, 1.0, w1, ALU.mult, ALU.mult,
                    accum_out=chalf[:, 1, e8:e8 + 1]))
                VE(lambda e, e8=e8: e.tensor_tensor(
                    chalf[:, 0, e8:e8 + 1], chalf[:, 0, e8:e8 + 1],
                    chalf[:, 1, e8:e8 + 1], ALU.add))
                if g4 == 0:
                    VE(lambda e, e8=e8, tcb=tcb: e.tensor_copy(
                        cbuf[:, tcb, e8:e8 + 1], chalf[:, 0, e8:e8 + 1]))
                else:
                    VE(lambda e, e8=e8, tcb=tcb: e.tensor_tensor(
                        cbuf[:, tcb, e8:e8 + 1], cbuf[:, tcb, e8:e8 + 1],
                        chalf[:, 0, e8:e8 + 1], ALU.add))

        t_lg = {}
        v_lg = {}
        v_top2 = {}
        pend_top2 = []
        ridx = 0
        for tcb in range(16):
            for g4 in range(2):
                pass
            for g4 in range(4):
                for rcl8 in range(8):
                    rc = g4 * 8 + rcl8
                    rcl = (g4 % 2) * 8 + rcl8
                    k = ridx
                    wwait = [(V, v_lg[k - 2])] if k >= 2 else []
                    p.dma("gpsimd", Srw[k % 2], rw[:, k % 2, :, :],
                          RWT[:, rc * 512:(rc + 1) * 512].rearrange("(c p) f -> p c f", p=128),
                          waits=wwait)
                    g_rw = Srw[k % 2].v
                    mw = [(Srw[k % 2], g_rw), (Sx3, g_x3)]
                    if k - 4 in v_lg:
                        mw.append((V, v_lg[k - 4]))
                    if rcl8 == 0:
                        gk = tcb * 4 + g4
                        if gk - 2 >= 0 and gk - 2 in v_top2:
                            mw.append((V, v_top2[gk - 2]))
                    for fi in range(16):
                        p.mm(LG[k % 4][:, :], x3bf[:, fi, tcb * 128:(tcb + 1) * 128],
                             rw[:, k % 2, fi, :],
                             start=(fi == 0), stop=(fi == 15),
                             waits=(mw if fi == 0 else ()), inc=(T if fi == 15 else None))
                    t_lg[k] = T.v
                    p.emit("vector",
                           lambda e, k=k, rcl=rcl, rc=rc:
                           e.scalar_tensor_tensor(lg[:, rcl, :], LG[k % 4][:, :], 1.0,
                                                  rbb[:, rc, :], ALU.mult, ALU.add),
                           waits=[(T, t_lg[k]), (V, v_rbb_all)], inc=V)
                    v_lg[k] = V.v
                    ridx += 1
                gk = tcb * 4 + g4
                lgv = lg.rearrange("p c f -> p (c f)")[:, (g4 % 2) * 4096:(g4 % 2) * 4096 + 4096]                         .rearrange("p (o e) -> p o e", e=8)
                v_grp = V.v

                def VE(fn, w=None):
                    p.emit("vector", fn, waits=([(V, V.v)] if w is None else w), inc=V)

                def _t2(lgv=lgv, tcb=tcb, g4=g4, v_grp=v_grp, gk=gk):
                    _top2_body(lgv, tcb, g4, v_grp, VE)
                    v_top2[gk] = V.v
                pend_top2.append(_t2)
                if len(pend_top2) > 1:
                    pend_top2.pop(0)()
        while pend_top2:
            pend_top2.pop(0)()
        v_c_all = V.v
        t_r_end = T.v

        t_eo = {}
        v_ac = {}
        eidx = 0
        for fc in range(4):
            for e8 in range(8):
                wwait = [(V, v_ac[eidx * 16 - 1]), (T, t_eo[eidx - 2])] if eidx >= 2 else [(V, v_c_all), (T, t_r_end), (Srw[0], Srw[0].v), (Srw[1], Srw[1].v)]
                p.dma("gpsimd", Srw[eidx % 2], rw[:, eidx % 2, :, :],
                      EWT[e8].rearrange("(c p) f -> p c f", p=128)[:, :, fc * 512:(fc + 1) * 512],
                      waits=wwait)
                g_ew = Srw[eidx % 2].v
                p.dma("sync", Seb[eidx % 2], ebrow[:, eidx % 2, :],
                      EB[e8:e8 + 1, fc * 512:(fc + 1) * 512],
                      waits=([(T, t_eo[eidx - 2])] if eidx >= 2 else []))
                s_eb = Seb[eidx % 2].v
                for tcb in range(16):
                    kk = eidx * 16 + tcb
                    mw = [(Srw[eidx % 2], g_ew), (Seb[eidx % 2], s_eb), (Sx3, g_x3)]
                    if kk - 2 in v_ac:
                        mw.append((V, v_ac[kk - 2]))
                    for fi in range(16):
                        p.mm(LG[kk % 2][:, :], x3bf[:, fi, tcb * 128:(tcb + 1) * 128],
                             rw[:, eidx % 2, fi, :],
                             start=(fi == 0), stop=False,
                             waits=(mw if fi == 0 else ()), inc=None)
                    p.mm(LG[kk % 2][:, :], ones1[0:1, :], ebrow[0:1, eidx % 2, :],
                         start=False, stop=True, inc=T)
                    t_mm = T.v
                    if e8 == 0:
                        p.emit("vector",
                               lambda e, kk=kk, tcb=tcb, e8=e8:
                               e.tensor_scalar(acc[:, tcb, :], LG[kk % 2][:, :],
                                               cbuf[:, tcb, e8:e8 + 1], None, op0=ALU.mult),
                               waits=[(T, t_mm), (V, v_c_all)] +
                                     ([(Say, Say.v)] if fc > 0 else []), inc=V)
                    else:
                        p.emit("vector",
                               lambda e, kk=kk, tcb=tcb, e8=e8:
                               e.scalar_tensor_tensor(acc[:, tcb, :], LG[kk % 2][:, :],
                                                      cbuf[:, tcb, e8:e8 + 1], acc[:, tcb, :],
                                                      ALU.mult, ALU.add),
                               waits=[(T, t_mm), (V, V.v)], inc=V)
                    v_ac[kk] = V.v
                t_eo[eidx] = T.v
                eidx += 1
            for tcb in range(16):
                p.dma("sync", Say, ASCR[tcb * 128:(tcb + 1) * 128, fc * 512:(fc + 1) * 512],
                      acc[:, tcb, :], waits=[(V, v_ac[(eidx - 1) * 16 + tcb])])
        a_done = Say.v

        v_fin = {}
        for tcb in range(16):
            j = tcb % 2
            p.dma("sync", Syb[j], yb[:, j, :], ASCR[tcb * 128:(tcb + 1) * 128, :],
                  waits=([(V, v_fin[tcb - 2])] if tcb >= 2 else [(Say, a_done)]))
            s_y = Syb[j].v
            p.dma("sync", Sxt[j], xtc[:, j, :], X3TOK[tcb * 128:(tcb + 1) * 128, :],
                  waits=([(V, v_fin[tcb - 2])] if tcb >= 2 else []))
            s_x = Sxt[j].v
            p.emit("vector",
                   lambda e, j=j: e.tensor_tensor(yb[:, j, :], yb[:, j, :], xtc[:, j, :], ALU.add),
                   waits=[(Syb[j], s_y), (Sxt[j], s_x), (V, V.v)], inc=V)
            v_y = V.v
            p.emit("vector", lambda e, j=j: e.reduce_sum(cols[:, 0:1], yb[:, j, :],
                                                         axis=mybir.AxisListType.X),
                   waits=[(V, v_y)], inc=V)
            p.emit("scalar", lambda e, j=j: e.activation(out=sqf[:, :], in_=yb[:, j, :],
                                                         func=AF.Square),
                   waits=[(V, v_y), (V, V.v)], inc=C)
            c_sq = C.v
            p.emit("vector", lambda e: e.reduce_sum(cols[:, 1:2], sqf[:, :],
                                                    axis=mybir.AxisListType.X),
                   waits=[(C, c_sq), (V, V.v)], inc=V)
            p.emit("vector", lambda e: e.tensor_scalar_mul(cols[:, 2:3], cols[:, 0:1], 1.0 / 2048.0),
                   waits=[(V, V.v)], inc=V)
            p.emit("vector", lambda e: e.tensor_scalar_mul(cols[:, 3:4], cols[:, 1:2], 1.0 / 2048.0),
                   waits=[(V, V.v)], inc=V)
            p.emit("vector", lambda e: e.tensor_tensor(cols[:, 4:5], cols[:, 2:3], cols[:, 2:3],
                                                       ALU.mult), waits=[(V, V.v)], inc=V)
            p.emit("vector", lambda e: e.tensor_tensor(cols[:, 4:5], cols[:, 3:4], cols[:, 4:5],
                                                       ALU.subtract), waits=[(V, V.v)], inc=V)
            v_var = V.v
            p.emit("scalar", lambda e: e.activation(out=cols[:, 5:6], in_=cols[:, 4:5],
                                                    func=AF.Ln, bias=EPS),
                   waits=[(V, v_var)], inc=C)
            p.emit("scalar", lambda e: e.activation(out=cols[:, 5:6], in_=cols[:, 5:6],
                                                    func=AF.Exp, scale=f32(-0.5)),
                   waits=[(C, C.v)], inc=C)
            c_rstd = C.v
            ow = [(C, c_rstd), (V, V.v), (So[j], So[j].v)]
            p.emit("vector",
                   lambda e, j=j: e.tensor_scalar(outc[:, j, :], yb[:, j, :], cols[:, 2:3],
                                                  None, op0=ALU.subtract),
                   waits=ow, inc=V)
            p.emit("vector",
                   lambda e, j=j: e.tensor_scalar(outc[:, j, :], outc[:, j, :], cols[:, 5:6],
                                                  None, op0=ALU.mult),
                   waits=[(V, V.v)], inc=V)
            v_fin[tcb] = V.v
            p.dma("sync", So[j], OUT[tcb * 128:(tcb + 1) * 128, :], outc[:, j, :],
                  waits=[(V, v_fin[tcb])])
    return p




def _declare_all(nc):
    d = {}
    t = lambda n, shp, dt, k="ExternalInput": nc.dram_tensor(n, shp, dt, kind=k).ap()
    d["XT"] = t("xt", [D, S], F32)
    d["WQKV"] = t("wqkv", [D, 3072], F32)
    d["BQKV"] = t("bqkv", [128, 24], F32)
    d["WO"] = t("wo", [D, D], F32)
    d["BO"] = t("bo", [128, 16], F32)
    d["IDENT"] = t("ident", [128, 128], F32)
    d["GIDXT"] = t("gidxt", [D, 5504], I32)
    d["UIDXT"] = t("uidxt", [D, 5504], I32)
    d["DIDXT"] = t("didxt", [5504, D], I32)
    d["LUT"] = t("lut", [1, 16], F32)
    d["WQ"] = t("wq", [D, D], F32)
    d["WK"] = t("wk", [D, D], F32)
    d["WV"] = t("wv", [D, D], F32)
    d["WOL"] = t("wol", [D, D], F32)
    d["LQ1"] = t("lq1", [128, 1], F32)
    d["LK1"] = t("lk1", [128, 1], F32)
    d["LQ2"] = t("lq2", [128, 1], F32)
    d["LK2"] = t("lk2", [128, 1], F32)
    d["SUBW"] = t("subw", [128, 2], F32)
    d["MASKS"] = t("maskst", [128, 8 * 512], F32).rearrange("p (j q) -> p j q", j=8)
    d["MCOL"] = t("mcol", [128, 8], F32)
    d["META"] = t("meta", [1, 64], U32)
    d["RWT"] = t("rwt", [D, 16384], F32)
    d["RB"] = t("rb", [1, 16384], F32)
    d["EWT"] = t("ewt", [8, D, D], F32)
    d["EB"] = t("eb", [8, D], F32)
    d["OUT"] = t("outk", [TOWN, D], F32, "ExternalOutput")
    s = lambda n, shp, dt: nc.dram_tensor(n, shp, dt).ap()
    d["KVS"] = s("kvs", [1024, S], BF16)
    d["X1T"] = s("x1ts", [D, TOWN], F32)
    d["GSCR"] = s("gscr", [D, 5504], BF16)
    d["USCR"] = s("uscr", [D, 5504], BF16)
    d["DSCR"] = s("dscr", [5504, D], BF16)
    d["HSCR"] = s("hscr", [5504, TOWN], BF16)
    d["X2T"] = s("x2ts", [D, TOWN], F32)
    d["KSCR"] = s("kscr", [D, S], BF16)
    d["VSCR"] = s("vscr", [S, D], BF16)
    d["X3T"] = s("x3ts_d", [D, TOWN], F32)
    d["X3TOK"] = s("x3tok", [TOWN, D], F32)
    d["ASCR"] = s("ascr", [TOWN, D], F32)
    d["SHX2"] = nc.dram_tensor("shx2", [2, D, TOWN], BF16, addr_space="Shared").ap()
    d["SHFLAG"] = nc.dram_tensor("shflag", [2, 16], U32, addr_space="Shared").ap()
    return d


def build_fused(stub=False):
    nc = bass.Bass("TRN2", target_bir_lowering=False)
    d = _declare_all(nc)
    with ExitStack() as stk:
        p = P(nc)
        p._stack = stk
        emit_l0_attn(p, nc, d["XT"], d["WQKV"], d["BQKV"], d["WO"], d["BO"], d["IDENT"],
                     d["X1T"], d["KVS"],
                     MAT=(d["GIDXT"], d["UIDXT"], d["DIDXT"], d["LUT"],
                          d["GSCR"], d["USCR"], d["DSCR"]))
        emit_l0_ffn(p, nc, d["X1T"], d["GIDXT"], d["UIDXT"], d["DIDXT"], d["LUT"],
                    d["X2T"], d["GSCR"], d["USCR"], d["DSCR"], d["HSCR"])
        emit_l1_attn(p, nc, d["X2T"], None, d["WQ"], d["WK"], d["WV"], d["WOL"],
                     d["LQ1"], d["LK1"], d["LQ2"], d["LK2"], d["SUBW"],
                     d["MASKS"], d["MCOL"], d["META"], d["IDENT"], d["X3T"], d["X3TOK"],
                     d["KSCR"], d["VSCR"],
                     EXCH=dict(shx2=d["SHX2"], shflag=d["SHFLAG"], stub=stub))
        emit_l1_moe(p, nc, d["X3T"], d["X3TOK"], d["RWT"], d["RB"], d["EWT"], d["EB"],
                    d["OUT"], d["ASCR"])
        block = stk.enter_context(nc.Block())
        p.emit("sync", lambda e: None,
               waits=[(c, c.v) for nm, c in p.pool.items() if nm != "dSp"])
        p.run(block)
    return nc


from concourse.bass_utils import run_bass_kernel_spmd


def _perm_for_core(c):
    pp = c & 1
    own = [2 * i + pp for i in range(4)]
    other = [2 * i + (1 - pp) for i in range(4)]
    return np.concatenate([np.arange(g * 512, (g + 1) * 512) for g in own + other])


_NC = {}


def kernel(L0_attn_x, L0_attn_Wqkv, L0_attn_bqkv, L0_attn_Wo, L0_attn_bo,
           L0_ffn_lut, L0_ffn_gate_idx, L0_ffn_up_idx, L0_ffn_down_idx,
           L1_attn_Wq, L1_attn_Wk, L1_attn_Wv, L1_attn_Wo,
           L1_attn_lambda_q1, L1_attn_lambda_k1, L1_attn_lambda_q2, L1_attn_lambda_k2,
           L1_attn_subln_weight, L1_ffn_router_w, L1_ffn_router_b,
           L1_ffn_expert_w, L1_ffn_expert_b):
    x = np.ascontiguousarray(np.asarray(L0_attn_x, dtype=np.float32))
    if "F" not in _NC:
        _NC["F"] = build_fused(stub=False)
    nc = _NC["F"]
    ident = np.eye(128, dtype=np.float32)
    bq = np.ascontiguousarray(np.asarray(L0_attn_bqkv, np.float32).reshape(24, 128).T)
    bo = np.ascontiguousarray(np.asarray(L0_attn_bo, np.float32).reshape(16, 128).T)
    gidxt = np.ascontiguousarray(np.asarray(L0_ffn_gate_idx).T)
    uidxt = np.ascontiguousarray(np.asarray(L0_ffn_up_idx).T)
    didxt = np.ascontiguousarray(np.asarray(L0_ffn_down_idx).T)
    lut = np.asarray(L0_ffn_lut, np.float32).reshape(1, 16)
    subw = np.ascontiguousarray(np.asarray(L1_attn_subln_weight, np.float32).reshape(2, 128).T)
    rwt = np.ascontiguousarray(np.asarray(L1_ffn_router_w, np.float32).T)
    rb = np.asarray(L1_ffn_router_b, np.float32).reshape(1, 16384)
    ewt = np.ascontiguousarray(np.asarray(L1_ffn_expert_w, np.float32).transpose(0, 2, 1))
    eb = np.asarray(L1_ffn_expert_b, np.float32)
    nonce = np.uint32(np.random.randint(1, 2 ** 31 - 1))
    kk = np.arange(128)[:, None]
    qq = np.arange(512)[None, :]
    perms = [_perm_for_core(c) for c in range(8)]
    in_maps = []
    for c in range(8):
        b, pp = c // 2, c & 1
        m = np.zeros((8, 128, 512), np.float32)
        for q2 in range(2):
            for bb in range(4):
                m[q2 * 4 + bb] = ((bb * 128 + kk) <= qq).astype(np.float32) if q2 == 0 \
                    else (1.0 if pp == 1 else 0.0)
        mc = np.ones((128, 8), np.float32)
        mc[:, 4:8] = 1.0 if pp == 1 else 0.0
        meta = np.zeros((1, 64), np.uint32)
        meta[0, 40] = pp
        meta[0, 41] = 1 - pp
        meta[0, 42] = nonce
        in_maps.append(dict(
            xt=np.ascontiguousarray(x[b].T[:, perms[c]]),
            wqkv=np.asarray(L0_attn_Wqkv, np.float32), bqkv=bq,
            wo=np.asarray(L0_attn_Wo, np.float32), bo=bo, ident=ident,
            gidxt=gidxt, uidxt=uidxt, didxt=didxt, lut=lut,
            wq=np.asarray(L1_attn_Wq, np.float32), wk=np.asarray(L1_attn_Wk, np.float32),
            wv=np.asarray(L1_attn_Wv, np.float32), wol=np.asarray(L1_attn_Wo, np.float32),
            lq1=np.asarray(L1_attn_lambda_q1, np.float32).reshape(128, 1),
            lk1=np.asarray(L1_attn_lambda_k1, np.float32).reshape(128, 1),
            lq2=np.asarray(L1_attn_lambda_q2, np.float32).reshape(128, 1),
            lk2=np.asarray(L1_attn_lambda_k2, np.float32).reshape(128, 1),
            subw=subw,
            maskst=np.ascontiguousarray(m.transpose(1, 0, 2).reshape(128, 8 * 512)),
            mcol=mc,
            meta=meta, rwt=rwt, rb=rb, ewt=ewt, eb=eb,
        ))
    res = run_bass_kernel_spmd(nc, in_maps, core_ids=list(range(8)))
    out = np.empty((4, 4096, 2048), np.float32)
    for c in range(8):
        out[c // 2][perms[c][:2048]] = res.results[c]["outk"]
    return out


# version 06 (final)
# speedup vs baseline: 1.0862x; 1.0862x over previous
import numpy as np
import concourse.bass as bass
import concourse.mybir as mybir

F32 = mybir.dt.float32
BF16 = mybir.dt.bfloat16
I32 = mybir.dt.int32
U32 = mybir.dt.uint32
AF = mybir.ActivationFunctionType
ALU = mybir.AluOpType
f32 = np.float32

D = 2048
S = 4096
TOWN = 2048
NF = 16
EPS = f32(1e-5)

ENGS = ("tensor", "vector", "scalar", "sync", "gpsimd")


class Ctr:
    def __init__(self, sem, name=""):
        self.sem = sem
        self.v = 0
        self.name = name


class P:
    def __init__(self, nc):
        self.nc = nc
        self.seq = {k: [] for k in ENGS}
        self.pool = {}
        self._stack = None
        self.pending = {}

    def S(self, name):
        if name not in self.pool:
            self.pool[name] = Ctr(self._stack.enter_context(self.nc.semaphore(name)))
        return self.pool[name]

    def begin_part(self):
        snap = [(c, c.v) for c in self.pool.values()]
        self.pending = {ek: list(snap) for ek in ENGS}

    def emit(self, ek, fn, waits=(), inc=None, incn=1):
        waits = list(waits)
        if self.pending.get(ek):
            waits = self.pending.pop(ek) + waits
        wl = tuple((c.sem, int(v)) for (c, v) in waits)
        if inc is not None:
            inc.v += incn
            sem, n = inc.sem, incn
        else:
            sem = None
            n = 0

        def run(eng, fn=fn, wl=wl, sem=sem, n=n):
            for s, v in wl:
                eng.wait_ge(s, v)
            inst = fn(eng)
            if sem is not None:
                assert inst is not None
                inst.then_inc(sem, n)
        self.seq[ek].append(run)

    def dma(self, ek, ctr, out, in_, waits=()):
        self.emit(ek, lambda e, o=out, i=in_: e.dma_start(out=o, in_=i),
                  waits=waits, inc=ctr, incn=16)
        return ctr.v

    def mm(self, out, lhsT, rhs, start, stop, waits=(), inc=None, **kw):
        self.emit("tensor",
                  lambda e, o=out, l=lhsT, r=rhs, st=start, sp=stop, kw=kw:
                  e.matmul(o, l, r, start=st, stop=sp, **kw),
                  waits=waits, inc=inc)
        return None if inc is None else inc.v

    def run(self, block):
        for ek in ENGS:
            fns = self.seq[ek]
            getattr(block, ek)(lambda eng, fns=fns: [fn(eng) for fn in fns])


def ln_rows(p, C, V, T, srow, var_i, rstd_i, m_i, ms_i, mr_i, statY, statY2, ntok_inv,
            waits_stats):
    p.emit("vector", lambda e: e.tensor_scalar_mul(srow[0:1, m_i, :], statY, float(ntok_inv)),
           waits=waits_stats, inc=V)
    p.emit("vector", lambda e: e.tensor_scalar_mul(srow[0:1, ms_i, :], statY2, float(ntok_inv)),
           inc=V)
    p.emit("vector", lambda e: e.tensor_tensor(srow[0:1, var_i, :], srow[0:1, m_i, :],
                                               srow[0:1, m_i, :], ALU.mult), waits=[(V, V.v)], inc=V)
    p.emit("vector", lambda e: e.tensor_tensor(srow[0:1, var_i, :], srow[0:1, ms_i, :],
                                               srow[0:1, var_i, :], ALU.subtract), waits=[(V, V.v)], inc=V)
    p.emit("scalar", lambda e: e.activation(out=srow[0:1, rstd_i, :], in_=srow[0:1, var_i, :],
                                            func=AF.Ln, bias=EPS),
           waits=[(V, V.v)], inc=C)
    p.emit("scalar", lambda e: e.activation(out=srow[0:1, rstd_i, :], in_=srow[0:1, rstd_i, :],
                                            func=AF.Exp, scale=f32(-0.5)), waits=[(C, C.v)], inc=C)
    p.emit("vector", lambda e: e.tensor_tensor(srow[0:1, mr_i, :], srow[0:1, m_i, :],
                                               srow[0:1, rstd_i, :], ALU.mult),
           waits=[(C, C.v), (V, V.v)], inc=V)




from contextlib import ExitStack


def emit_l0_attn(p, nc, XT, WQKV, BQKV, WO, BO, IDENT, X1T, KVS, MAT=None):
    p.begin_part()
    with ExitStack() as ctx:
        sb = lambda *a: ctx.enter_context(nc.sbuf_tensor(*a, allow_name_mangling=True))
        psum = lambda *a: ctx.enter_context(nc.psum_tensor(*a, allow_name_mangling=True))
        sem = lambda n: p.S(n)
        xbf = sb("xbf", [128, 16, 2048], BF16)
        qt = sb("qt", [128, 16, 2048], BF16)
        wstr = sb("wstr", [128, 2, 16, 128], BF16)
        bcol24 = sb("bcol24", [128, 24], F32)
        bocol = sb("bocol", [128, 16], F32)
        evac = sb("evac", [128, 2, 512], BF16)
        kg2 = sb("kg2", [128, 1, S], BF16)
        vg = sb("vg", [64, 1, S], BF16)
        vtok = sb("vtok", [128, 32, 65], BF16)
        pt = sb("pt", [128, 3, 512], BF16)
        rrowb = sb("rrowb", [1, 2, 512], F32)
        otmp = sb("otmp", [64, 512], F32)
        ones128 = sb("ones128", [128, 1], F32)
        ones1 = sb("ones1", [1, 128], F32)
        ident_s = sb("ident_s", [128, 128], BF16)
        xres = sb("xres", [128, 2, 512], F32)
        ysq = sb("ysq", [128, 2, 512], F32)
        srow = sb("srow", [1, 5, 512], F32)
        bcast = sb("bcast", [128, 2, 512], F32)
        x1c = sb("x1c", [128, 2, 512], F32)
        aot = xbf
        ybuf = qt.ap().rearrange("p c f -> p (c f)").bitcast(F32).rearrange("p (c f) -> p c f", c=16)
        PP = [psum("PP0", [128, 512], F32), psum("PP1", [128, 512], F32)]
        STp = [psum("ST0", [128, 512], F32), psum("ST1", [128, 512], F32)]
        STp3 = [STp[0], STp[1], PP[1]]
        OPp = [psum("OP0", [65, 512], F32), psum("OP1", [65, 512], F32)]
        TRB = [psum("TRB0", [128, 128], BF16), psum("TRB1", [128, 128], BF16)]
        ST0, ST1 = STp[0], STp[1]
        T = sem("sT"); C = sem("sC"); V = sem("sV")
        Spre = sem("dPre"); SpreG = sem("dPreG"); Sxbf = sem("dXbf")
        Sw = [sem("dW0"), sem("dW1")]
        Skvw = [sem("dKvw0"), sem("dKvw1")]
        Skg = sem("dKg")
        Sxr = [sem("dXr0"), sem("dXr1")]
        Sx1 = [sem("dX10"), sem("dX11")]

        XTv = XT.rearrange("d (b c) -> d b c", c=512)
        X1Tv = X1T.rearrange("d (b c) -> d b c", c=512)

        p.emit("vector", lambda e: e.memset(ones128[:], 1.0), inc=V)
        p.emit("vector", lambda e: e.memset(ones1[:], 1.0), inc=V)
        p.emit("vector", lambda e: e.memset(vtok[:, :, 64:65], 1.0), inc=V)
        v_pre = V.v
        bg = []
        if MAT is not None:
            GIDXT, UIDXT, DIDXT, LUT, GSCR, USCR, DSCR = MAT
            idxq = sb("idxq", [128, 2, 1376], BF16)
            accq = sb("accq", [128, 2, 1376], BF16)
            maskq = sb("maskq", [128, 1376], BF16)
            lutrow = sb("lutrow", [1, 16], F32)
            lutcol = sb("lutcol", [128, 16], F32)
            Sidx = [p.S("dIdx0"), p.S("dIdx1")]
            Sacc = [p.S("dAcc0"), p.S("dAcc1")]
            Slut = p.S("dLut")
            p.dma("sync", Slut, lutrow[:], LUT)
            p.mm(PP[0][:, 0:16], ones1[0:1, :], lutrow[0:1, :], start=True, stop=True,
                 waits=[(Slut, Slut.v), (V, v_pre)], inc=T)
            p.emit("vector", lambda e: e.tensor_copy(lutcol[:, :], PP[0][:, 0:16]),
                   waits=[(T, T.v)], inc=V)
            v_lut = V.v
            mstate = {"mi": 0, "vmat": {}}

            def _mk_thunks(IDXT, SCR, nfi, F):
                for fi in range(nfi):
                    for q4 in range(4):
                        qf = F // 4
                        mi = mstate["mi"]

                        def t_dma(mi=mi, fi=fi, q4=q4, qf=qf, IDXT=IDXT):
                            iw = [(V, mstate["vmat"].get(mi - 2, v_lut))]
                            p.dma("gpsimd", Sidx[mi % 2], idxq[:, mi % 2, 0:qf],
                                  IDXT[fi * 128:(fi + 1) * 128, q4 * qf:(q4 + 1) * qf],
                                  waits=iw)
                        bg.append(t_dma)
                        for vg in range(4):
                            def t_ops(mi=mi, vg=vg, qf=qf):
                                g_i = Sidx[mi % 2].v
                                for vv in range(vg * 4, vg * 4 + 4):
                                    w0 = [(Sidx[mi % 2], g_i), (V, V.v)]
                                    if vv == 0:
                                        w0.append((Sacc[mi % 2], Sacc[mi % 2].v))
                                        p.emit("vector",
                                               lambda e, vv=vv, qf=qf, mi=mi:
                                               e.tensor_scalar(accq[:, mi % 2, 0:qf],
                                                               idxq[:, mi % 2, 0:qf],
                                                               float(vv), lutcol[:, vv:vv + 1],
                                                               op0=ALU.is_equal, op1=ALU.mult),
                                               waits=w0, inc=V)
                                    else:
                                        p.emit("vector",
                                               lambda e, vv=vv, qf=qf, mi=mi:
                                               e.tensor_scalar(maskq[:, 0:qf],
                                                               idxq[:, mi % 2, 0:qf],
                                                               float(vv), lutcol[:, vv:vv + 1],
                                                               op0=ALU.is_equal, op1=ALU.mult),
                                               waits=w0, inc=V)
                                        p.emit("vector",
                                               lambda e, qf=qf, mi=mi:
                                               e.tensor_tensor(accq[:, mi % 2, 0:qf],
                                                               accq[:, mi % 2, 0:qf],
                                                               maskq[:, 0:qf], ALU.add),
                                               waits=[(V, V.v)], inc=V)
                            bg.append(t_ops)

                        def t_store(mi=mi, fi=fi, q4=q4, qf=qf, SCR=SCR):
                            mstate["vmat"][mi] = V.v
                            p.dma("sync", Sacc[mi % 2],
                                  SCR[fi * 128:(fi + 1) * 128, q4 * qf:(q4 + 1) * qf],
                                  accq[:, mi % 2, 0:qf], waits=[(V, V.v)])
                        bg.append(t_store)
                        mstate["mi"] = mi + 1

            _mk_thunks(GIDXT, GSCR, 16, 5504)
            _mk_thunks(UIDXT, USCR, 16, 5504)
            _mk_thunks(DIDXT, DSCR, 43, 2048)

        def drain(n):
            for _ in range(n):
                if bg:
                    bg.pop(0)()
        p.dma("sync", Spre, bcol24[:, :], BQKV)
        p.dma("sync", Spre, bocol[:, :], BO, waits=[(Spre, Spre.v)])
        p.dma("gpsimd", SpreG, ident_s[:], IDENT)
        spre = Spre.v
        spreg = SpreG.v

        t_group_end = {}
        v_evac = {}
        gidx = 0
        ev_i = 0
        t_half_end = 0
        for half in range(2):
            for fi in range(16):
                p.dma("gpsimd", Sxbf, xbf[:, fi, :],
                      XT[fi * 128:(fi + 1) * 128, half * 2048:(half + 1) * 2048],
                      waits=([(T, t_half_end)] if (half == 1 and fi == 0) else ()))
            g_x = Sxbf.v
            nfo = 24 if half == 0 else 8
            for fo in range(nfo):
                col = (16 + fo) if fo < 8 else (fo - 8)
                wwait = [(T, t_group_end[gidx - 2])] if gidx >= 2 else []
                p.dma("gpsimd", Sw[gidx % 2], wstr[:, gidx % 2, :, :],
                      WQKV[:, col * 128:(col + 1) * 128].rearrange("(c p) f -> p c f", p=128),
                      waits=wwait)
                g_w = Sw[gidx % 2].v
                for tc in range(4):
                    pj = (gidx * 4 + tc) % 2
                    key = gidx * 4 + tc
                    mwaits = [(Sw[gidx % 2], g_w), (Sxbf, g_x), (Spre, spre)]
                    if MAT is not None and gidx == 0 and tc == 0:
                        mwaits.append((V, v_lut))
                    if key - 2 in v_evac:
                        mwaits.append((V, v_evac[key - 2]))
                    for fi in range(16):
                        p.mm(PP[pj][:, :], wstr[:, gidx % 2, fi, :],
                             xbf[:, fi, tc * 512:(tc + 1) * 512],
                             start=(fi == 0), stop=(fi == 15),
                             waits=(mwaits if fi == 0 else ()), inc=(T if fi == 15 else None))
                    t_mm = T.v
                    if fo < 8:
                        ej = ev_i % 2
                        ewaits = [(T, t_mm), (Spre, spre), (Skvw[ej], Skvw[ej].v)]
                        p.emit("vector",
                               lambda e, ej=ej, pj=pj, col=col:
                               e.tensor_scalar_add(evac[:, ej, :], PP[pj][:, :], bcol24[:, col:col + 1]),
                               waits=ewaits, inc=V)
                        v_evac[key] = V.v
                        p.dma("sync", Skvw[ej],
                              KVS[(col - 16) * 128:(col - 15) * 128,
                                  half * 2048 + tc * 512: half * 2048 + (tc + 1) * 512],
                              evac[:, ej, :], waits=[(V, V.v)])
                        ev_i += 1
                    else:
                        p.emit("vector",
                               lambda e, pj=pj, col=col, tc=tc:
                               e.tensor_scalar_add(qt[:, col, tc * 512:(tc + 1) * 512],
                                                   PP[pj][:, :], bcol24[:, col:col + 1]),
                               waits=[(T, t_mm), (Spre, spre)], inc=V)
                        v_evac[key] = V.v
                    t_group_end[gidx] = t_mm
                    drain(2)
                gidx += 1
            t_half_end = T.v
        kv_done = [(Skvw[0], Skvw[0].v), (Skvw[1], Skvw[1].v)]
        t_p1_end = T.v

        t_g_end = {}
        si = 0
        c_exp = {}
        t_pv = {}
        v_norm = {}
        v_recd = {}
        prev_tr = [0, 0]

        def _a_norm_tail(b):
            blo = ((b // 4) % 2) * 64
            bqtile = (b // 4) // 2
            bqc = b % 4
            p.mm(PP[0][0:64, :], ones1[0:1, 0:64], rrowb[0:1, b % 2, :], start=True, stop=True,
                 waits=[(V, v_recd[b])] + ([(V, v_norm[b - 1])] if b - 1 in v_norm else []),
                 inc=T)
            t_bc = T.v
            p.emit("vector", lambda e, b=b: e.tensor_copy(otmp[:], OPp[b % 2][0:64, :]),
                   waits=[(T, t_bc)], inc=V)
            p.emit("vector",
                   lambda e, blo=blo, bqtile=bqtile, bqc=bqc:
                   e.tensor_tensor(aot[blo:blo + 64, bqtile, bqc * 512:(bqc + 1) * 512],
                                   otmp[:], PP[0][0:64, :], ALU.mult),
                   waits=[(V, V.v)], inc=V)
            v_norm[b] = V.v
        for g8 in range(8):
            kwait = list(kv_done)
            if g8 >= 1:
                kwait.append((T, t_g_end[g8 - 1]))
            p.dma("sync", Skg, kg2[0:64, 0, :], KVS[g8 * 64:(g8 + 1) * 64, :], waits=kwait)
            p.dma("sync", Skg, kg2[64:128, 0, :], KVS[g8 * 64:(g8 + 1) * 64, :])
            p.dma("sync", Skg, vg[:, 0, :], KVS[512 + g8 * 64:512 + (g8 + 1) * 64, :])
            sy_kgv = Skg.v
            v_tr = {}
            for kc in range(32):
                trw = [(Skg, sy_kgv), (V, v_pre), (SpreG, spreg)]
                trw.append((V, prev_tr[kc % 2]) if kc < 2 else (V, v_tr[kc - 2]))
                p.emit("tensor",
                       lambda e, kc=kc: e.transpose(TRB[kc % 2][:, 0:64],
                                                    vg[:, 0, kc * 128:(kc + 1) * 128],
                                                    ident_s[0:64, 0:64]),
                       waits=trw, inc=T)
                p.emit("vector",
                       lambda e, kc=kc: e.tensor_copy(vtok[:, kc, 0:64], TRB[kc % 2][:, 0:64]),
                       waits=[(T, T.v)], inc=V)
                v_tr[kc] = V.v
            prev_tr = [v_tr[30], v_tr[31]]
            v_vtok = V.v
            for h in range(4):
                hh = g8 * 4 + h
                lo = (hh % 2) * 64
                qtile = hh // 2
                for qc in range(4):
                    a = hh * 4 + qc

                    def _st(kc, si0):
                        stw = []
                        if kc <= 1:
                            stw.append((V, v_vtok))
                        if si0 - 3 in c_exp:
                            stw.append((C, c_exp[si0 - 3]))
                        p.mm(STp3[si0 % 3][:, :],
                             kg2[lo:lo + 64, 0, kc * 128:(kc + 1) * 128],
                             qt[lo:lo + 64, qtile, qc * 512:(qc + 1) * 512],
                             start=True, stop=True, waits=stw, inc=T)
                        expw = [(T, T.v)]
                        if si0 - 3 in t_pv:
                            expw.append((T, t_pv[si0 - 3]))
                        p.emit("scalar",
                               lambda e, si0=si0: e.activation(out=pt[:, si0 % 3, :],
                                                               in_=STp3[si0 % 3][:, :],
                                                               func=AF.Exp, scale=f32(0.125)),
                               waits=expw, inc=C)
                        c_exp[si0] = C.v

                    _st(0, si)
                    _st(1, si + 1)
                    for kc in range(32):
                        if kc + 2 < 32:
                            _st(kc + 2, si + kc + 2)
                        s2 = si + kc
                        p.mm(OPp[a % 2][:, :], vtok[:, kc, :], pt[:, s2 % 3, :],
                             start=(kc == 0), stop=(kc == 31),
                             waits=[(C, c_exp[s2])] + ([(V, v_norm[a - 2])] if (kc == 0 and a - 2 in v_norm) else []),
                             inc=T)
                        t_pv[s2] = T.v
                    si += 32
                    drain(10)
                    p.emit("vector", lambda e, a=a: e.reciprocal(rrowb[:, a % 2, :],
                                                                 OPp[a % 2][64:65, :]),
                           waits=[(T, t_pv[si - 1])], inc=V)
                    v_recd[a] = V.v
                    if a >= 1:
                        _a_norm_tail(a - 1)
            t_g_end[g8] = T.v
        _a_norm_tail(127)
        v_aot = V.v

        t_wo_end = {}
        t_stat = {}
        v_y = {}
        v_lnread = 0
        widx = 0
        for tc in range(4):
            for fo in range(16):
                wwait = [(T, t_p1_end)] if widx < 2 else [(T, t_wo_end[widx - 2])]
                p.dma("gpsimd", Sw[widx % 2], wstr[:, widx % 2, :, :],
                      WO[:, fo * 128:(fo + 1) * 128].rearrange("(c p) f -> p c f", p=128),
                      waits=wwait)
                g_wo = Sw[widx % 2].v
                xw = [(V, v_y[widx - 2])] if widx >= 2 else []
                p.dma("sync", Sxr[widx % 2], xres[:, widx % 2, :], XTv[fo * 128:(fo + 1) * 128, tc, :],
                      waits=xw)
                sy_x = Sxr[widx % 2].v
                mwaits = [(Sw[widx % 2], g_wo), (V, v_aot)]
                if widx >= 2:
                    mwaits.append((V, v_y[widx - 2]))
                for fi in range(16):
                    p.mm(PP[widx % 2][:, :], wstr[:, widx % 2, fi, :],
                         aot[:, fi, tc * 512:(tc + 1) * 512],
                         start=(fi == 0), stop=(fi == 15),
                         waits=(mwaits if fi == 0 else ()), inc=(T if fi == 15 else None))
                t_wo_end[widx] = T.v
                p.emit("vector",
                       lambda e, widx=widx, fo=fo:
                       e.scalar_tensor_tensor(ybuf[:, fo, 0:512], PP[widx % 2][:, :], bocol[:, fo:fo + 1],
                                              xres[:, widx % 2, :], ALU.add, ALU.add),
                       waits=[(T, t_wo_end[widx]), (Sxr[widx % 2], sy_x), (Spre, spre)], inc=V)
                v_y[widx] = V.v
                sqw = [(V, v_y[widx])]
                if widx >= 2:
                    sqw.append((T, t_stat[widx - 2]))
                p.emit("scalar",
                       lambda e, widx=widx, fo=fo: e.activation(out=ysq[:, widx % 2, :],
                                                                in_=ybuf[:, fo, 0:512],
                                                                func=AF.Square),
                       waits=sqw, inc=C)
                c_sq = C.v
                statw = [(V, v_y[widx])]
                if fo == 0:
                    statw.append((V, v_lnread))
                p.mm(OPp[0][0:1, :], ones128[:, :], ybuf[:, fo, 0:512], start=(fo == 0), stop=(fo == 15),
                     waits=statw, inc=None)
                p.mm(OPp[1][0:1, :], ones128[:, :], ysq[:, widx % 2, :], start=(fo == 0), stop=(fo == 15),
                     waits=([(C, c_sq)] + ([(V, v_lnread)] if fo == 0 else [])), inc=T)
                t_stat[widx] = T.v
                drain(2)
                widx += 1
            ln_rows(p, C, V, T, srow, var_i=2, rstd_i=3, m_i=0, ms_i=1, mr_i=4,
                    statY=OPp[0][0:1, :], statY2=OPp[1][0:1, :], ntok_inv=1.0 / 2048.0,
                    waits_stats=[(T, T.v)])
            v_lnread = V.v
            v_rows = V.v
            p.mm(ST0[:, :], ones1[0:1, :], srow[0:1, 3, :], start=True, stop=True,
                 waits=[(V, v_rows)], inc=T)
            p.mm(ST1[:, :], ones1[0:1, :], srow[0:1, 4, :], start=True, stop=True, inc=T)
            t_bc2 = T.v
            p.emit("vector", lambda e: e.tensor_copy(bcast[:, 0, :], ST0[:, :]),
                   waits=[(T, t_bc2)], inc=V)
            p.emit("vector", lambda e: e.tensor_copy(bcast[:, 1, :], ST1[:, :]), inc=V)
            v_bc = V.v
            for fo in range(16):
                oidx = tc * 16 + fo
                ow = [(V, v_bc), (Sx1[oidx % 2], Sx1[oidx % 2].v)]
                p.emit("vector",
                       lambda e, fo=fo:
                       e.scalar_tensor_tensor(x1c[:, fo % 2, :], ybuf[:, fo, 0:512], 1.0,
                                              bcast[:, 0, :], ALU.mult, ALU.mult),
                       waits=ow, inc=V)
                p.emit("vector",
                       lambda e, fo=fo: e.tensor_tensor(x1c[:, fo % 2, :], x1c[:, fo % 2, :],
                                                        bcast[:, 1, :], ALU.subtract),
                       waits=[(V, V.v)], inc=V)
                p.dma("sync", Sx1[oidx % 2], X1Tv[fo * 128:(fo + 1) * 128, tc, :], x1c[:, fo % 2, :],
                      waits=[(V, V.v)])
        while bg:
            bg.pop(0)()
    return p




def emit_l0_ffn(p, nc, X1T, GIDXT, UIDXT, DIDXT, LUT, X2T, GSCR, USCR, DSCR, HSCR, MOVED_MAT=True):
    p.begin_part()
    NI = 5504
    NIC = 43
    with ExitStack() as ctx:
        sb = lambda *a: ctx.enter_context(nc.sbuf_tensor(*a, allow_name_mangling=True))
        psum = lambda *a: ctx.enter_context(nc.psum_tensor(*a, allow_name_mangling=True))
        sem = lambda n: p.S(n)
        arena = sb("arena", [128, 61440], BF16)
        ar = arena.ap()
        x1bf = ar[:, 0:32768].rearrange("p (c f) -> p c f", c=16)
        idxb = ar[:, 32768:43776].rearrange("p (c f) -> p c f", c=2)
        accb = ar[:, 43776:54784].rearrange("p (c f) -> p c f", c=2)
        lutrow = sb("lutrow", [1, 16], F32)
        lutcol = sb("lutcol", [128, 16], F32)
        ones1 = sb("ones1", [1, 128], F32)
        ones128 = sb("ones128", [128, 1], F32)
        gw = sb("gw", [128, 2, 16, 128], BF16)
        uw = sb("uw", [128, 2, 16, 128], BF16)
        sg = sb("sg", [128, 512], BF16)
        hc = sb("hc", [128, 2, 512], BF16)
        ht = ar[:, 0:22016].rearrange("p (c f) -> p c f", c=NIC)
        dw = ar[:, 22016:33024].rearrange("p (b c f) -> p b c f", b=2, c=NIC)
        xres = sb("xres", [128, 2, 512], F32)
        ybuf = ar[:, 33024:49408].bitcast(F32).rearrange("p (c f) -> p c f", c=16)
        ysq = sb("ysq", [128, 2, 512], F32)
        srow = sb("srow", [1, 5, 512], F32)
        bcast = sb("bcast", [128, 2, 512], F32)
        x2c = sb("x2c", [128, 2, 512], F32)
        PP = [psum("PP0", [128, 512], F32), psum("PP1", [128, 512], F32)]
        GP = [psum("GP0", [128, 512], F32), psum("GP1", [128, 512], F32)]
        UP = [psum("UP0", [128, 512], F32), psum("UP1", [128, 512], F32)]
        SB0 = psum("SB0", [128, 512], F32)
        SB1 = psum("SB1", [128, 512], F32)
        T = sem("sT"); C = sem("sC"); V = sem("sV")
        Sx1 = sem("dX1"); Sidx = [sem("dIdx0"), sem("dIdx1")]; Sacc = [sem("dAcc0"), sem("dAcc1")]
        Sgw = [sem("dGw0"), sem("dGw1")]; Suw = [sem("dUw0"), sem("dUw1")]
        Shw = [sem("dHw0"), sem("dHw1")]
        Sht = sem("dHt"); Sdw = [sem("dDw0"), sem("dDw1")]
        Sxr = [sem("dXr0"), sem("dXr1")]; So = [sem("dO0"), sem("dO1")]
        Spre = sem("dPre")

        X1Tv = X1T.rearrange("d (b c) -> d b c", c=512)
        X2Tv = X2T.rearrange("d (b c) -> d b c", c=512)

        p.emit("vector", lambda e: e.memset(ones1[:], 1.0), inc=V)
        p.emit("vector", lambda e: e.memset(ones128[:], 1.0), inc=V)
        v_ones = V.v
        p.dma("sync", Spre, lutrow[:], LUT)
        spre = Spre.v
        p.mm(SB1[:, 0:16], ones1[0:1, :], lutrow[0:1, :], start=True, stop=True,
             waits=[(Spre, spre), (V, v_ones)], inc=T)
        p.emit("vector", lambda e: e.tensor_copy(lutcol[:, :], SB1[:, 0:16]),
               waits=[(T, T.v)], inc=V)
        v_lut = V.v
        for fi in range(16):
            p.dma("gpsimd", Sx1, x1bf[:, fi, :], X1T[fi * 128:(fi + 1) * 128, :])
        g_x1 = Sx1.v

        sgm = ar[:, 54784:60288]
        def materialize(IDXT, SCR, nfi, F):
            nonlocal mat_i
            for fi in range(nfi):
                mi = mat_i
                iw = [(V, v_mat.get(mi - 2, v_lut))]
                p.dma("gpsimd", Sidx[mi % 2], idxb[:, mi % 2, 0:F],
                      IDXT[fi * 128:(fi + 1) * 128, :], waits=iw)
                g_i = Sidx[mi % 2].v
                aw = [(Sacc[mi % 2], Sacc[mi % 2].v), (Sidx[mi % 2], g_i), (V, v_lut)]
                p.emit("vector", lambda e, mi=mi, F=F: e.memset(accb[:, mi % 2, 0:F], 0.0),
                       waits=aw, inc=V)
                for vv in range(16):
                    p.emit("vector",
                           lambda e, mi=mi, vv=vv, F=F:
                           e.tensor_scalar(idxb[:, (mi % 2), 0:F] if False else sgm[:, 0:F],
                                           idxb[:, mi % 2, 0:F], float(vv), lutcol[:, vv:vv + 1],
                                           op0=ALU.is_equal, op1=ALU.mult),
                           waits=[(V, V.v)], inc=V)
                    p.emit("vector",
                           lambda e, mi=mi, F=F:
                           e.tensor_tensor(accb[:, mi % 2, 0:F], accb[:, mi % 2, 0:F],
                                           sgm[:, 0:F], ALU.add),
                           waits=[(V, V.v)], inc=V)
                v_mat[mi] = V.v
                p.dma("sync", Sacc[mi % 2], SCR[fi * 128:(fi + 1) * 128, 0:F],
                      accb[:, mi % 2, 0:F], waits=[(V, V.v)])
                mat_i += 1
        mat_i = 0
        v_mat = {}
        if MOVED_MAT is None:
            materialize(GIDXT, GSCR, 16, NI)
            materialize(UIDXT, USCR, 16, NI)
            materialize(DIDXT, DSCR, NIC, D)
        mat_done = [(Sacc[0], Sacc[0].v), (Sacc[1], Sacc[1].v)]

        t_gu_end = {}
        v_h = {}
        c_sg = {}
        for fo in range(NIC):
            wwait = list(mat_done) if fo < 2 else [(T, t_gu_end[fo - 2])]
            p.dma("gpsimd", Sgw[fo % 2], gw[:, fo % 2, :, :],
                  GSCR[:, fo * 128:(fo + 1) * 128].rearrange("(c p) f -> p c f", p=128),
                  waits=wwait)
            g_gw = Sgw[fo % 2].v
            p.dma("gpsimd", Suw[fo % 2], uw[:, fo % 2, :, :],
                  USCR[:, fo * 128:(fo + 1) * 128].rearrange("(c p) f -> p c f", p=128))
            g_uw = Suw[fo % 2].v
            for tc in range(4):
                k = fo * 4 + tc
                gw0 = [(Sgw[fo % 2], g_gw), (Sx1, g_x1)]
                if k - 2 in c_sg:
                    gw0.append((C, c_sg[k - 2]))
                for fi in range(16):
                    p.mm(GP[k % 2][:, :], gw[:, fo % 2, fi, :], x1bf[:, fi, tc * 512:(tc + 1) * 512],
                         start=(fi == 0), stop=(fi == 15),
                         waits=(gw0 if fi == 0 else ()), inc=(T if fi == 15 else None))
                t_g = T.v
                uw0 = [(Suw[fo % 2], g_uw)]
                if k - 2 in v_h:
                    uw0.append((V, v_h[k - 2]))
                for fi in range(16):
                    p.mm(UP[k % 2][:, :], uw[:, fo % 2, fi, :], x1bf[:, fi, tc * 512:(tc + 1) * 512],
                         start=(fi == 0), stop=(fi == 15),
                         waits=(uw0 if fi == 0 else ()), inc=(T if fi == 15 else None))
                t_u = T.v
                p.emit("scalar", lambda e, k=k: e.activation(out=sg[:, :], in_=GP[k % 2][:, :],
                                                             func=AF.Silu),
                       waits=[(T, t_g), (V, v_h.get(k - 1, 0))], inc=C)
                c_sg[k] = C.v
                hw = [(C, c_sg[k]), (T, t_u), (Shw[k % 2], Shw[k % 2].v)]
                p.emit("vector",
                       lambda e, k=k: e.tensor_tensor(hc[:, k % 2, :], sg[:, :], UP[k % 2][:, :],
                                                      ALU.mult),
                       waits=hw, inc=V)
                v_h[k] = V.v
                p.dma("sync", Shw[k % 2],
                      HSCR[fo * 128:(fo + 1) * 128, tc * 512:(tc + 1) * 512],
                      hc[:, k % 2, :], waits=[(V, V.v)])
            t_gu_end[fo] = T.v
        h_done = [(Shw[0], Shw[0].v), (Shw[1], Shw[1].v)]
        t_gu_all = T.v

        t_dn_end = {}
        t_stat = {}
        v_y = {}
        v_lnread = 0
        widx = 0
        for tc in range(4):
            hw2 = list(h_done) + ([(T, t_gu_all)] if tc == 0 else [(T, t_dn_end[(tc - 1) * 16 + 15])])
            p.dma("sync", Sht, ht[:, :, :],
                  HSCR[:, tc * 512:(tc + 1) * 512].rearrange("(c p) f -> p c f", p=128),
                  waits=hw2)
            g_ht = Sht.v
            for fo in range(16):
                wwait = [(T, t_gu_all)] if widx < 2 else [(T, t_dn_end[widx - 2])]
                p.dma("gpsimd", Sdw[widx % 2], dw[:, widx % 2, :, :],
                      DSCR[:, fo * 128:(fo + 1) * 128].rearrange("(c p) f -> p c f", p=128),
                      waits=wwait)
                g_dw = Sdw[widx % 2].v
                xw = [(V, v_y[widx - 2])] if widx >= 2 else []
                p.dma("sync", Sxr[widx % 2], xres[:, widx % 2, :], X1Tv[fo * 128:(fo + 1) * 128, tc, :],
                      waits=xw)
                sy_x = Sxr[widx % 2].v
                mw = [(Sdw[widx % 2], g_dw), (Sht, g_ht)]
                if widx >= 2:
                    mw.append((V, v_y[widx - 2]))
                for fi in range(NIC):
                    p.mm(PP[widx % 2][:, :], dw[:, widx % 2, fi, :], ht[:, fi, :],
                         start=(fi == 0), stop=(fi == NIC - 1),
                         waits=(mw if fi == 0 else ()), inc=(T if fi == NIC - 1 else None))
                t_dn_end[widx] = T.v
                p.emit("vector",
                       lambda e, widx=widx, fo=fo:
                       e.scalar_tensor_tensor(ybuf[:, fo, :], PP[widx % 2][:, :], 1.0,
                                              xres[:, widx % 2, :], ALU.mult, ALU.add),
                       waits=[(T, t_dn_end[widx]), (Sxr[widx % 2], sy_x)], inc=V)
                v_y[widx] = V.v
                sqw = [(V, v_y[widx])]
                if widx >= 2:
                    sqw.append((T, t_stat[widx - 2]))
                p.emit("scalar",
                       lambda e, widx=widx, fo=fo: e.activation(out=ysq[:, widx % 2, :],
                                                                in_=ybuf[:, fo, :],
                                                                func=AF.Square),
                       waits=sqw, inc=C)
                c_sq = C.v
                statw = [(V, v_y[widx])]
                if fo == 0:
                    statw.append((V, v_lnread))
                p.mm(SB0[0:1, :], ones128[:, :], ybuf[:, fo, :], start=(fo == 0), stop=(fo == 15),
                     waits=statw, inc=None)
                p.mm(SB1[0:1, :], ones128[:, :], ysq[:, widx % 2, :], start=(fo == 0), stop=(fo == 15),
                     waits=([(C, c_sq)] + ([(V, v_lnread)] if fo == 0 else [])), inc=T)
                t_stat[widx] = T.v
                widx += 1
            ln_rows(p, C, V, T, srow, var_i=2, rstd_i=3, m_i=0, ms_i=1, mr_i=4,
                    statY=SB0[0:1, :], statY2=SB1[0:1, :], ntok_inv=1.0 / 2048.0,
                    waits_stats=[(T, T.v)])
            v_lnread = V.v
            v_rows = V.v
            p.mm(GP[0][:, :], ones1[0:1, :], srow[0:1, 3, :], start=True, stop=True,
                 waits=[(V, v_rows)], inc=T)
            p.mm(GP[1][:, :], ones1[0:1, :], srow[0:1, 4, :], start=True, stop=True, inc=T)
            t_bc2 = T.v
            p.emit("vector", lambda e: e.tensor_copy(bcast[:, 0, :], GP[0][:, :]),
                   waits=[(T, t_bc2)], inc=V)
            p.emit("vector", lambda e: e.tensor_copy(bcast[:, 1, :], GP[1][:, :]), inc=V)
            v_bc = V.v
            for fo in range(16):
                oidx = tc * 16 + fo
                ow = [(V, v_bc), (So[oidx % 2], So[oidx % 2].v)]
                p.emit("vector",
                       lambda e, fo=fo:
                       e.scalar_tensor_tensor(x2c[:, fo % 2, :], ybuf[:, fo, :], 1.0,
                                              bcast[:, 0, :], ALU.mult, ALU.mult),
                       waits=ow, inc=V)
                p.emit("vector",
                       lambda e, fo=fo: e.tensor_tensor(x2c[:, fo % 2, :], x2c[:, fo % 2, :],
                                                        bcast[:, 1, :], ALU.subtract),
                       waits=[(V, V.v)], inc=V)
                p.dma("sync", So[oidx % 2], X2Tv[fo * 128:(fo + 1) * 128, tc, :], x2c[:, fo % 2, :],
                      waits=[(V, V.v)])
    return p




def emit_l1_attn(p, nc, X2T, X2FULLT, WQ, WK, WV, WOL, LQ1, LK1, LQ2, LK2, SUBW,
                 MASKS, MCOL, META, IDENT, X3T, X3TOK, KSCR, VSCR, EXCH=None):
    p.begin_part()
    LAMBDA_INIT = 0.8 - 0.6 * float(np.exp(-0.3 * 12))
    SCALE = float(128 ** -0.5)
    with ExitStack() as ctx:
        sb = lambda *a: ctx.enter_context(nc.sbuf_tensor(*a, allow_name_mangling=True))
        psum = lambda *a: ctx.enter_context(nc.psum_tensor(*a, allow_name_mangling=True))
        sem = lambda n: p.S(n)
        arena = sb("arena", [128, 32768], BF16)
        ar = arena.ap()
        x2bf = ar.rearrange("p (c f) -> p c f", c=16)
        aot = ar.rearrange("p (c f) -> p c f", c=16)
        arena2 = sb("arena2", [128, 32768], BF16)
        ar2 = arena2.ap()
        qt = ar2.rearrange("p (c f) -> p c f", c=16)
        ybuf = ar2[:, 0:16384].bitcast(F32).rearrange("p (c f) -> p c f", c=16)
        x3ts = ar2[:, 16384:32768].bitcast(F32).rearrange("p (b f) -> p b f", b=4)
        wstr = sb("wstr", [128, 2, 16, 128], BF16)
        kvarena = sb("kvarena", [128, 12288], BF16)
        kva = kvarena.ap()
        vw = kva[:, 0:8192].rearrange("p (c f) -> p c f", c=16)
        kt = kva[:, 0:4096].rearrange("p (c f) -> p c f", c=1)
        vs = kva[:, 4096:12288].rearrange("p (c f) -> p c f", c=32)
        evac = sb("evac", [128, 2, 512], BF16)
        masks = sb("masks", [128, 4, 512], BF16)
        mcol = sb("mcol", [128, 8], F32)
        pt = sb("pt", [128, 2, 512], BF16)
        cf32 = sb("cf32", [128, 2, 4, 512], F32)
        sqb2 = sb("sqb2", [128, 2, 512], F32)
        lam = sb("lam", [128, 4], F32)
        lrows = sb("lrows", [1, 8], F32)
        subcol = sb("subcol", [128, 2], F32)
        ones1 = sb("ones1", [1, 128], F32)
        ones128 = sb("ones128", [128, 1], F32)
        onesbf = sb("onesbf", [128, 1], BF16)
        ident_s = sb("ident_s", [128, 128], F32)
        xres = sb("xres", [128, 2, 512], F32)
        ysq = sqb2.ap()
        srow = sb("srow", [1, 5, 512], F32)
        crec = srow.ap()[:, 1:5, :].rearrange("a (p b) q -> a p b q", p=2)
        bcast = masks.ap().rearrange("p j q -> p (j q)").bitcast(F32).rearrange("p (c f) -> p c f", c=2)
        x3c = sb("x3c", [128, 2, 512], F32)
        meta_s = sb("meta_s", [1, 64], U32)
        PS = [psum("PS0", [128, 512], F32), psum("PS1", [128, 512], F32)]
        OAp = psum("OAp", [128, 512], F32)
        OBp = psum("OBp", [128, 512], F32)
        ODp = psum("ODp", [128, 512], F32)
        CB0 = psum("CB0", [128, 512], F32)
        CB1 = psum("CB1", [128, 512], F32)
        CB2 = psum("CB2", [128, 512], F32)
        OD = [ODp, CB2]
        T = sem("sT"); C = sem("sC"); V = sem("sV")
        Spre = sem("dPre"); SpreG = sem("dPreG"); Sx2 = sem("dX2")
        Sw = [sem("dW0"), sem("dW1")]
        Svw = sem("dVw")
        Skw = [sem("dKw0"), sem("dKw1")]
        Svv = [sem("dVv0"), sem("dVv1")]
        Skt = sem("dKt")
        Svs = sem("dVs")
        Sxr = [sem("dXr0"), sem("dXr1")]
        So = [sem("dO0"), sem("dO1")]
        Sot = sem("dOt")

        X2Tv = X2T.rearrange("d (b c) -> d b c", c=512)
        X3Tv = X3T.rearrange("d (b c) -> d b c", c=512)
        KSCRq = KSCR.rearrange("d (i q c) -> d i q c", q=2, c=512)
        VSCRq = VSCR.rearrange("(i q b c) f -> i q b c f", q=2, b=4, c=128)


        p.emit("vector", lambda e: e.memset(ones1[:], 1.0), inc=V)
        p.emit("vector", lambda e: e.memset(ones128[:], 1.0), inc=V)
        p.emit("vector", lambda e: e.memset(onesbf[:], 1.0), inc=V)
        v_ones = V.v
        p.dma("sync", Spre, meta_s[:], META)
        p.dma("sync", Spre, lam[:, 0:1], LQ1, waits=[(Spre, Spre.v)])
        p.dma("sync", Spre, lam[:, 1:2], LK1, waits=[(Spre, Spre.v)])
        p.dma("sync", Spre, lam[:, 2:3], LQ2, waits=[(Spre, Spre.v)])
        p.dma("sync", Spre, lam[:, 3:4], LK2, waits=[(Spre, Spre.v)])
        p.dma("sync", Spre, subcol[:, :], SUBW, waits=[(Spre, Spre.v)])
        spre = Spre.v
        p.dma("gpsimd", SpreG, ident_s[:], IDENT)
        p.dma("gpsimd", SpreG, masks[:, :, :], MASKS[:, 0:4, :])
        p.dma("sync", Spre, mcol[:, :], MCOL, waits=[(Spre, Spre.v)])
        spre = Spre.v
        spreg = SpreG.v
        p.emit("vector", lambda e: e.tensor_tensor(lam[:, 0:1], lam[:, 0:1], lam[:, 1:2], ALU.mult),
               waits=[(Spre, spre)], inc=V)
        p.emit("vector", lambda e: e.tensor_tensor(lam[:, 1:2], lam[:, 2:3], lam[:, 3:4], ALU.mult),
               waits=[(V, V.v)], inc=V)
        v_lamprod = V.v
        p.mm(OD[0][0:1, 0:2], ones128[:, 0:1], lam[:, 0:2], start=True, stop=True,
             waits=[(V, v_lamprod)], inc=T)
        p.emit("scalar", lambda e: e.activation(out=lrows[0:1, 0:2], in_=OD[0][0:1, 0:2],
                                                func=AF.Exp), waits=[(T, T.v)], inc=C)
        p.emit("vector", lambda e: e.tensor_tensor(lrows[0:1, 2:3], lrows[0:1, 0:1],
                                                   lrows[0:1, 1:2], ALU.subtract),
               waits=[(C, C.v)], inc=V)
        p.emit("vector", lambda e: e.tensor_scalar(lrows[0:1, 3:4], lrows[0:1, 2:3],
                                                   float(LAMBDA_INIT), -1.0,
                                                   op0=ALU.add, op1=ALU.mult),
               waits=[(V, V.v)], inc=V)
        v_nlam = V.v
        p.mm(OD[1][:, 0:1], ones1[0:1, :], lrows[0:1, 3:4], start=True, stop=True,
             waits=[(V, v_nlam)], inc=T)
        p.emit("vector", lambda e: e.tensor_copy(lam[:, 2:3], OD[1][:, 0:1]),
               waits=[(T, T.v)], inc=V)
        p.emit("vector", lambda e: e.tensor_scalar_mul(subcol[:, :], subcol[:, :],
                                                       float(1.0 - LAMBDA_INIT)),
               waits=[(Spre, spre), (V, V.v)], inc=V)
        v_const = V.v

        HOLD = {}
        if EXCH is not None:
            Sex = sem("dEx")
            Sfl = sem("dFl")
            Ssp = sem("dSp")
            flag_in = sb("flag_in", [1, 16], U32)
            shflagv = EXCH["shflag"].rearrange("(a one) f -> a one f", one=1)

            def _push(e):
                HOLD["po"] = e.value_load(meta_s[0:1, 40:41])
                HOLD["pa"] = e.value_load(meta_s[0:1, 41:42])
                HOLD["nonce"] = e.value_load(meta_s[0:1, 42:43])
                return e.dma_start(out=EXCH["shx2"][HOLD["po"]], in_=X2T)
            p.emit("gpsimd", _push, waits=[(Spre, spre)], inc=Sex, incn=16)
            p.emit("gpsimd",
                   lambda e: e.dma_start(out=shflagv[HOLD["po"]][:, 0:1],
                                         in_=meta_s[0:1, 42:43]),
                   waits=[(Sex, Sex.v)], inc=Sfl, incn=16)

            def _spin(e):
                if EXCH.get("stub"):
                    return None
                eng = [e.engine]
                loop_id = nc.next_id()
                ln_, dn_ = f"spin_{loop_id}", f"spin_{loop_id}_done"
                cnt = nc.alloc_registers(f"cnt_{loop_id}", engines=eng)
                nc.regs_mov(cnt, Ssp.v)
                nc.br(ln_, engines=eng)
                with nc.body(ln_, valid_engines=eng):
                    nc.regs_alu(cnt, cnt, 16, op=ALU.add)
                    e.dma_start(out=flag_in[:, 0:1], in_=shflagv[HOLD["pa"]][:, 0:1]).then_inc(Ssp.sem, 16)
                    e.wait_ge(Ssp.sem, nc.snap(cnt, engines=eng))
                    v = e.value_load(flag_in[0:1, 0:1])
                    nc.br_cmp(v, HOLD["nonce"], on_true=dn_, on_false=ln_,
                              comp_op="IS_EQ", engines=eng)
                nc.switch_bb(dn_)
                return None

        t_pj = {}
        v_pj = {}
        v_vev = {}
        gidx = 0
        kev = 0
        vev = 0
        t_half_end = 0
        for half in range(2):
            if EXCH is not None and half == 1:
                p.emit("gpsimd", _spin, waits=[(Sfl, Sfl.v)])
            for fi in range(16):
                hw0 = [(T, t_half_end)] if (half == 1 and fi == 0) else []
                if EXCH is None:
                    p.dma("gpsimd", Sx2, x2bf[:, fi, :],
                          X2FULLT[fi * 128:(fi + 1) * 128, half * 2048:(half + 1) * 2048],
                          waits=hw0)
                elif half == 0:
                    p.dma("gpsimd", Sx2, x2bf[:, fi, :],
                          X2T[fi * 128:(fi + 1) * 128, :], waits=hw0)
                else:
                    key = "po" if EXCH.get("stub") else "pa"
                    p.emit("gpsimd",
                           lambda e, fi=fi, key=key:
                           e.dma_start(out=x2bf[:, fi, :],
                                       in_=EXCH["shx2"][HOLD[key]][fi * 128:(fi + 1) * 128, :]),
                           waits=hw0, inc=Sx2, incn=16)
            g_x2 = Sx2.v
            wlist = [("k", WK)] + ([("q", WQ)] if half == 0 else [])
            v_guard = V.v
            kseg = 0
            for wname, W in wlist:
                for fo in range(16):
                    wwait = [(T, t_pj[gidx - 2])] if gidx >= 2 else []
                    p.dma("gpsimd", Sw[gidx % 2], wstr[:, gidx % 2, :, :],
                          W[:, fo * 128:(fo + 1) * 128].rearrange("(c p) f -> p c f", p=128),
                          waits=wwait)
                    g_w = Sw[gidx % 2].v
                    for tc in range(4):
                        k = gidx * 4 + tc
                        mw = [(Sw[gidx % 2], g_w), (Sx2, g_x2)]
                        if kseg < 2:
                            mw.append((V, v_guard))
                        elif k - 2 in v_pj:
                            mw.append((V, v_pj[k - 2]))
                        kseg += 1
                        for fi in range(16):
                            p.mm(PS[k % 2][:, :], wstr[:, gidx % 2, fi, :],
                                 x2bf[:, fi, tc * 512:(tc + 1) * 512],
                                 start=(fi == 0), stop=(fi == 15),
                                 waits=(mw if fi == 0 else ()), inc=(T if fi == 15 else None))
                        t_mm = T.v
                        if wname == "q":
                            p.emit("vector",
                                   lambda e, k=k, fo=fo, tc=tc:
                                   e.tensor_copy(qt[:, fo, tc * 512:(tc + 1) * 512], PS[k % 2][:, :]),
                                   waits=[(T, t_mm)], inc=V)
                        else:
                            ej = kev % 2
                            p.emit("vector",
                                   lambda e, k=k, ej=ej: e.tensor_copy(evac[:, ej, :], PS[k % 2][:, :]),
                                   waits=[(T, t_mm), (Skw[ej], Skw[ej].v), (Svv[ej], Svv[ej].v)], inc=V)
                            p.emit("gpsimd",
                                   lambda e, ej=ej, fo=fo, half=half, tc=tc:
                                   e.dma_start(out=KSCRq[fo * 128:(fo + 1) * 128, tc, half, :],
                                               in_=evac[:, ej, :]),
                                   waits=[(V, V.v)], inc=Skw[ej], incn=16)
                            kev += 1
                        v_pj[k] = V.v
                    t_pj[gidx] = T.v
                    gidx += 1
            v_guard2 = V.v
            vseg = 0
            for fc in range(4):
                p.dma("gpsimd", Svw, vw[:, :, :],
                      WV[:, fc * 512:(fc + 1) * 512].rearrange("(c p) f -> p c f", p=128),
                      waits=[(T, t_pj[gidx - 1])] if (half, fc) != (0, 0) else [])
                g_vw = Svw.v
                for tb in range(16):
                    k = gidx * 4 + tb
                    mw = [(Svw, g_vw), (Sx2, g_x2)]
                    if vseg < 2:
                        mw.append((V, v_guard2))
                    elif vev >= 2:
                        mw.append((V, v_vev[vev - 2]))
                    vseg += 1
                    for fi in range(16):
                        p.mm(PS[(vev) % 2][:, :], x2bf[:, fi, tb * 128:(tb + 1) * 128],
                             vw[:, fi, :],
                             start=(fi == 0), stop=(fi == 15),
                             waits=(mw if fi == 0 else ()), inc=(T if fi == 15 else None))
                    t_mm = T.v
                    ej = vev % 2
                    evw = [(T, t_mm), (Svv[vev % 2], Svv[vev % 2].v), (Skw[ej], Skw[ej].v)]
                    p.emit("vector",
                           lambda e, ej=ej, vev=vev: e.tensor_copy(evac[:, ej, :], PS[vev % 2][:, :]),
                           waits=evw, inc=V)
                    v_vev[vev] = V.v
                    p.emit("gpsimd",
                           lambda e, ej=ej, fc=fc, half=half, tb=tb:
                           e.dma_start(out=VSCRq[tb // 4, half, tb % 4, :, fc * 512:(fc + 1) * 512],
                                       in_=evac[:, ej, :]),
                           waits=[(V, V.v)], inc=Svv[vev % 2], incn=16)
                    vev += 1
                t_pj[gidx] = T.v
                gidx += 1
            t_half_end = T.v
        kv_done = [(Skw[0], Skw[0].v), (Skw[1], Skw[1].v), (Svv[0], Svv[0].v), (Svv[1], Svv[1].v)]
        t_p1_end = T.v

        si = 0
        c_exp = {}
        v_mask = {}
        t_pv = {}
        v_comb = {}
        v_stage0 = {}
        v_part1 = {}
        c_rstd_d = {}
        t_ssum_d = {}
        t_h_end = {}

        def _c_part1(bb):
            pw = [(V, v_stage0[(bb, 1)])]
            if bb - 1 in v_comb:
                pw.append((V, v_comb[bb - 1]))
            p.mm(CB0[:, :], ones1[0:1, :], crec[0:1, bb % 2, 0, :], start=True, stop=True,
                 waits=pw, inc=T)
            p.mm(CB1[:, :], ones1[0:1, :], crec[0:1, bb % 2, 1, :], start=True, stop=True, inc=T)
            t_bc = T.v
            for j in range(4):
                p.emit("vector",
                       lambda e, bb=bb, j=j: e.tensor_tensor(cf32[:, bb % 2, j, :],
                                                             cf32[:, bb % 2, j, :],
                                                             (CB0 if j < 2 else CB1)[:, :], ALU.mult),
                       waits=[(T, t_bc), (V, V.v)], inc=V)
            for j in range(2):
                p.emit("vector",
                       lambda e, bb=bb, j=j:
                       e.scalar_tensor_tensor(cf32[:, bb % 2, j, :], cf32[:, bb % 2, 2 + j, :],
                                              lam[:, 2:3], cf32[:, bb % 2, j, :],
                                              ALU.mult, ALU.add),
                       waits=[(V, V.v)], inc=V)
            v_part1[bb] = V.v
            sqw = [(V, v_part1[bb])]
            if bb - 1 in t_ssum_d:
                sqw.append((T, t_ssum_d[bb - 1]))
            p.emit("scalar", lambda e, bb=bb: e.activation(out=sqb2[:, 0, :],
                                                           in_=cf32[:, bb % 2, 0, :],
                                                           func=AF.Square), waits=sqw, inc=C)
            p.emit("scalar", lambda e, bb=bb: e.activation(out=sqb2[:, 1, :],
                                                           in_=cf32[:, bb % 2, 1, :],
                                                           func=AF.Square), inc=C)
            c_rstd_d[bb] = C.v

        def _c_part2(bb):
            ssw = [(C, c_rstd_d[bb])]
            if bb - 1 in c_rstd_d:
                pass
            p.mm(CB2[0:1, :], ones128[:, 0:1], sqb2[:, 0, :], start=True, stop=False,
                 waits=ssw, inc=None)
            p.mm(CB2[0:1, :], ones128[:, 0:1], sqb2[:, 1, :], start=False, stop=True, inc=T)
            t_ssum_d[bb] = T.v
            p.emit("scalar", lambda e: e.activation(out=srow[0:1, 0, :], in_=CB2[0:1, :],
                                                    func=AF.Ln, bias=EPS, scale=f32(1.0 / 256.0)),
                   waits=[(T, t_ssum_d[bb]), (C, C.v)], inc=C)
            p.emit("scalar", lambda e: e.activation(out=srow[0:1, 0, :], in_=srow[0:1, 0, :],
                                                    func=AF.Exp, scale=f32(-0.5)),
                   waits=[(C, C.v)], inc=C)
            c_r = C.v
            p.mm(CB0[:, :], ones1[0:1, :], srow[0:1, 0, :], start=True, stop=True,
                 waits=[(C, c_r), (V, v_part1[bb])], inc=T)
            t_bcr = T.v
            for j in range(2):
                p.emit("vector",
                       lambda e, bb=bb, j=j: e.tensor_tensor(cf32[:, bb % 2, j, :],
                                                             cf32[:, bb % 2, j, :],
                                                             CB0[:, :], ALU.mult),
                       waits=[(T, t_bcr), (V, V.v)], inc=V)
                p.emit("vector",
                       lambda e, bb=bb, j=j:
                       e.tensor_scalar(aot[:, 2 * (bb // 4) + j, (bb % 4) * 512:(bb % 4) * 512 + 512],
                                       cf32[:, bb % 2, j, :], subcol[:, j:j + 1], None,
                                       op0=ALU.mult),
                       waits=[(V, V.v)], inc=V)
            v_comb[bb] = V.v
        for h in range(8):
            vsw = list(kv_done)
            if h >= 1:
                vsw.append((T, t_h_end[h - 1]))
            p.dma("sync", Svs, vs[:, :, :],
                  VSCR[:, h * 256:(h + 1) * 256].rearrange("(c p) f -> p c f", p=128),
                  waits=vsw)
            s_vs = Svs.v

            for qc in range(4):
                nkc = 8 * qc + 8
                aa = h * 4 + qc
                for br in range(2):
                    ktw = [(T, t_pv[si - 1])] if si >= 1 else []
                    p.dma("sync", Skt, kt[:, 0, :],
                          KSCR[(2 * h + br) * 128:(2 * h + br + 1) * 128, :], waits=ktw)
                    s_kt = Skt.v
                    for kc2 in range(0, nkc, 2):
                        t_sts = {}
                        for o in (0, 1):
                            kc = kc2 + o
                            s2 = si + o
                            stw = [(C, c_exp[s2 - 2])] if s2 - 2 in c_exp else []
                            if kc == 0:
                                stw += [(Skt, s_kt), (Svs, s_vs), (SpreG, spreg), (Spre, spre)]
                                prevk = (aa, br - 1) if br == 1 else (aa - 1, 1)
                                if prevk in v_stage0:
                                    stw.append((V, v_stage0[prevk]))
                            p.mm(PS[s2 % 2][:, :], kt[:, 0, kc * 128:(kc + 1) * 128],
                                 qt[:, 2 * h + br, qc * 512:(qc + 1) * 512],
                                 start=True, stop=True, waits=stw, inc=T)
                            t_sts[o] = T.v
                        vm = {}
                        for o in (0, 1):
                            kc = kc2 + o
                            s2 = si + o
                            expw = [(T, t_sts[o])]
                            if s2 - 2 in t_pv:
                                expw.append((T, t_pv[s2 - 2]))
                            p.emit("scalar",
                                   lambda e, s2=s2: e.activation(out=pt[:, s2 % 2, :],
                                                                 in_=PS[s2 % 2][:, :],
                                                                 func=AF.Exp, scale=f32(SCALE)),
                                   waits=expw, inc=C)
                            c_exp[s2] = C.v
                            if kc >= 8 * qc:
                                j = kc - 8 * qc
                                if j < 4:
                                    p.emit("vector",
                                           lambda e, s2=s2, j=j:
                                           e.tensor_tensor(pt[:, s2 % 2, :], pt[:, s2 % 2, :],
                                                           masks[:, j, :], ALU.mult),
                                           waits=[(C, c_exp[s2]), (V, V.v)], inc=V)
                                else:
                                    p.emit("vector",
                                           lambda e, s2=s2, j=j:
                                           e.tensor_scalar_mul(pt[:, s2 % 2, :], pt[:, s2 % 2, :],
                                                               mcol[:, j:j + 1]),
                                           waits=[(C, c_exp[s2]), (V, V.v)], inc=V)
                                vm[o] = V.v
                        for o in (0, 1):
                            kc = kc2 + o
                            s2 = si + o
                            pvw0 = [(V, vm[o])] if o in vm else [(C, c_exp[s2])]
                            p.mm(OAp[:, :], vs[:, kc, 0:128], pt[:, s2 % 2, :],
                                 start=(kc == 0), stop=(kc == nkc - 1), waits=pvw0, inc=None)
                            p.mm(OBp[:, :], vs[:, kc, 128:256], pt[:, s2 % 2, :],
                                 start=(kc == 0), stop=(kc == nkc - 1), inc=None)
                            p.mm(ODp[0:1, :], onesbf[:, 0:1], pt[:, s2 % 2, :],
                                 start=(kc == 0), stop=(kc == nkc - 1), inc=T)
                            t_pv[s2] = T.v
                        si += 2
                    p.emit("vector",
                           lambda e, aa=aa, br=br: e.tensor_copy(cf32[:, aa % 2, 2 * br, :], OAp[:, :]),
                           waits=[(T, t_pv[si - 1])], inc=V)
                    p.emit("vector",
                           lambda e, aa=aa, br=br: e.tensor_copy(cf32[:, aa % 2, 2 * br + 1, :], OBp[:, :]),
                           inc=V)
                    p.emit("vector",
                           lambda e, aa=aa, br=br: e.reciprocal(crec[0:1, aa % 2, br, :], ODp[0:1, :]),
                           inc=V)
                    v_stage0[(aa, br)] = V.v
                    if br == 0 and aa >= 1:
                        _c_part1(aa - 1)
                if aa >= 1:
                    _c_part2(aa - 1)
            t_h_end[h] = T.v
        _c_part1(31)
        _c_part2(31)
        v_aot = V.v

        t_wo_end = {}
        t_stat = {}
        v_y = {}
        t_tr = {}
        v_lnread = 0
        widx = 0
        for tc in range(4):
            for fo in range(16):
                wwait = [(T, t_p1_end)] if widx < 2 else [(T, t_wo_end[widx - 2])]
                p.dma("gpsimd", Sw[widx % 2], wstr[:, widx % 2, :, :],
                      WOL[:, fo * 128:(fo + 1) * 128].rearrange("(c p) f -> p c f", p=128),
                      waits=wwait)
                g_wo = Sw[widx % 2].v
                xw = [(V, v_y[widx - 2])] if widx >= 2 else []
                p.dma("sync", Sxr[widx % 2], xres[:, widx % 2, :], X2Tv[fo * 128:(fo + 1) * 128, tc, :],
                      waits=xw)
                sy_x = Sxr[widx % 2].v
                mw = [(Sw[widx % 2], g_wo), (V, v_aot)]
                if widx >= 2:
                    mw.append((V, v_y[widx - 2]))
                for fi in range(16):
                    p.mm(PS[widx % 2][:, :], wstr[:, widx % 2, fi, :],
                         aot[:, fi, tc * 512:(tc + 1) * 512],
                         start=(fi == 0), stop=(fi == 15),
                         waits=(mw if fi == 0 else ()), inc=(T if fi == 15 else None))
                t_wo_end[widx] = T.v
                p.emit("vector",
                       lambda e, widx=widx, fo=fo:
                       e.scalar_tensor_tensor(ybuf[:, fo, :], PS[widx % 2][:, :], 1.0,
                                              xres[:, widx % 2, :], ALU.mult, ALU.add),
                       waits=[(T, t_wo_end[widx]), (Sxr[widx % 2], sy_x)], inc=V)
                v_y[widx] = V.v
                sqw = [(V, v_y[widx])]
                if widx >= 1:
                    sqw.append((T, t_stat[widx - 1]))
                p.emit("scalar",
                       lambda e, widx=widx, fo=fo: e.activation(out=ysq[:, 0, :],
                                                                in_=ybuf[:, fo, :],
                                                                func=AF.Square),
                       waits=sqw, inc=C)
                c_sq2 = C.v
                statw = [(V, v_y[widx])]
                if fo == 0:
                    statw.append((V, v_lnread))
                p.mm(OAp[0:1, :], ones128[:, 0:1], ybuf[:, fo, :], start=(fo == 0), stop=(fo == 15),
                     waits=statw, inc=None)
                p.mm(OBp[0:1, :], ones128[:, 0:1], ysq[:, 0, :], start=(fo == 0), stop=(fo == 15),
                     waits=([(C, c_sq2)] + ([(V, v_lnread)] if fo == 0 else [])), inc=T)
                t_stat[widx] = T.v
                widx += 1
            ln_rows(p, C, V, T, srow, var_i=2, rstd_i=3, m_i=0, ms_i=1, mr_i=2,
                    statY=OAp[0:1, :], statY2=OBp[0:1, :], ntok_inv=1.0 / 2048.0,
                    waits_stats=[(T, T.v)])
            v_lnread = V.v
            v_rows = V.v
            p.mm(OD[0][:, :], ones1[0:1, :], srow[0:1, 3, :], start=True, stop=True,
                 waits=[(V, v_rows)], inc=T)
            p.mm(OD[1][:, :], ones1[0:1, :], srow[0:1, 2, :], start=True, stop=True, inc=T)
            t_bc2 = T.v
            p.emit("vector", lambda e: e.tensor_copy(bcast[:, 0, 0:512], OD[0][:, :]),
                   waits=[(T, t_bc2)], inc=V)
            p.emit("vector", lambda e: e.tensor_copy(bcast[:, 1, 0:512], OD[1][:, :]), inc=V)
            v_bc = V.v
            sot_prev = Sot.v
            for fo in range(16):
                oidx = tc * 16 + fo
                ow = [(V, v_bc), (So[oidx % 2], So[oidx % 2].v)]
                if oidx - 2 in t_tr:
                    ow.append((T, t_tr[oidx - 2]))
                p.emit("vector",
                       lambda e, fo=fo:
                       e.scalar_tensor_tensor(x3c[:, fo % 2, :], ybuf[:, fo, :], 1.0,
                                              bcast[:, 0, 0:512], ALU.mult, ALU.mult),
                       waits=ow, inc=V)
                p.emit("vector",
                       lambda e, fo=fo: e.tensor_tensor(x3c[:, fo % 2, :], x3c[:, fo % 2, :],
                                                        bcast[:, 1, 0:512], ALU.subtract),
                       waits=[(V, V.v)], inc=V)
                v_x3 = V.v
                p.dma("sync", So[oidx % 2], X3Tv[fo * 128:(fo + 1) * 128, tc, :], x3c[:, fo % 2, :],
                      waits=[(V, v_x3)])
                for tb in range(4):
                    trw = [(V, v_x3), (SpreG, spreg)]
                    if (oidx, tb) != (0, 0):
                        trw.append((V, V.v))
                    p.emit("tensor",
                           lambda e, fo=fo, tb=tb:
                           e.transpose(CB0[:, 0:128].bitcast(F32) if tb % 2 == 0 else CB1[:, 0:128].bitcast(F32),
                                       x3c[:, fo % 2, tb * 128:(tb + 1) * 128], ident_s[:]),
                           waits=trw, inc=T)
                    p.emit("vector",
                           lambda e, fo=fo, tb=tb:
                           e.tensor_copy(x3ts[:, tb, fo * 128:(fo + 1) * 128],
                                         (CB0[:, 0:128] if tb % 2 == 0 else CB1[:, 0:128])),
                           waits=[(T, T.v), (Sot, sot_prev)], inc=V)
                t_tr[oidx] = T.v
            v_fin = V.v
            for tb in range(4):
                p.dma("sync", Sot, X3TOK[tc * 512 + tb * 128: tc * 512 + (tb + 1) * 128, :],
                      x3ts[:, tb, :], waits=[(V, v_fin)])
    return p




def emit_l1_moe(p, nc, X3T, X3TOK, RWT, RB, EWT, EB, OUT, ASCR):
    p.begin_part()
    BIG = 60000.0
    with ExitStack() as ctx:
        sb = lambda *a: ctx.enter_context(nc.sbuf_tensor(*a, allow_name_mangling=True))
        psum = lambda *a: ctx.enter_context(nc.psum_tensor(*a, allow_name_mangling=True))
        sem = lambda n: p.S(n)
        arena = sb("arena", [128, 32768], BF16)
        x3bf = arena.ap().rearrange("p (c f) -> p c f", c=16)
        arena2 = sb("arena2", [128, 16384], BF16)
        lg = arena2.ap().bitcast(F32).rearrange("p (c f) -> p c f", c=16)
        acc = lg
        rw = sb("rw", [128, 2, 16, 512], BF16)
        ew = rw
        farena = sb("farena", [128, 28672], BF16)
        far = farena.ap()
        rbb = far[:, 0:16384].rearrange("p (c f) -> p c f", c=32)
        t2 = far[:, 16384:26624].bitcast(F32).rearrange("p (c f) -> p c f", c=5)
        cbuf = sb("cbuf", [128, 16, 8], F32)
        chalf = sb("chalf", [128, 2, 8], F32)
        rbrow = sb("rbrow", [1, 2, 512], F32)
        ebrow = sb("ebrow", [1, 2, 512], F32)
        ones1 = sb("ones1", [1, 128], F32)
        yb = far[:, 0:8192].bitcast(F32).rearrange("p (c f) -> p c f", c=2)
        xtc = far[:, 8192:16384].bitcast(F32).rearrange("p (c f) -> p c f", c=2)
        outc = far[:, 16384:24576].bitcast(F32).rearrange("p (c f) -> p c f", c=2)
        sqf = far[:, 24576:28672].bitcast(F32)
        cols = sb("cols", [128, 8], F32)
        LG = [psum("LG0", [128, 512], F32), psum("LG1", [128, 512], F32),
              psum("LG2", [128, 512], F32), psum("LG3", [128, 512], F32)]
        RBP = psum("RBP", [128, 512], F32)
        T = sem("sT"); C = sem("sC"); V = sem("sV")
        Sx3 = sem("dX3"); Srw = [sem("dRw0"), sem("dRw1")]
        Srb = [sem("dRb0"), sem("dRb1")]
        Seb = [sem("dEb0"), sem("dEb1")]
        Sac = [sem("dAc0"), sem("dAc1")]
        Say = sem("dAy"); Syb = [sem("dYb0"), sem("dYb1")]
        Sxt = [sem("dXt0"), sem("dXt1")]
        So = [sem("dO0"), sem("dO1")]

        p.emit("vector", lambda e: e.memset(ones1[:], 1.0), inc=V)
        v_ones = V.v
        for fi in range(16):
            p.dma("gpsimd", Sx3, x3bf[:, fi, :], X3T[fi * 128:(fi + 1) * 128, :])
        g_x3 = Sx3.v
        t_rbb = {}
        v_rbb = {}
        for rc in range(32):
            p.dma("sync", Srb[rc % 2], rbrow[:, rc % 2, :], RB[:, rc * 512:(rc + 1) * 512],
                  waits=([(T, t_rbb[rc - 2])] if rc >= 2 else []))
            s_rb = Srb[rc % 2].v
            p.mm(RBP[:, :], ones1[0:1, :], rbrow[0:1, rc % 2, :], start=True, stop=True,
                 waits=[(Srb[rc % 2], s_rb), (V, v_ones)] + ([(V, v_rbb[rc - 1])] if rc >= 1 else []),
                 inc=T)
            t_rbb[rc] = T.v
            p.emit("vector", lambda e, rc=rc: e.tensor_copy(rbb[:, rc, :], RBP[:, :]),
                   waits=[(T, t_rbb[rc])], inc=V)
            v_rbb[rc] = V.v
        v_rbb_all = V.v

        pend_top2 = []

        def _top2_body(lgv, tcb, g4, v_half, VE):
            m1 = t2[:, 0, 0:512]
            m2 = t2[:, 1, 0:512]
            tmp = t2[:, 2, 0:512]
            w1 = t2[:, 3, 0:512]
            dd = t2[:, 4, 0:512]
            VE(lambda e: e.tensor_tensor(m1, lgv[:, :, 0], lgv[:, :, 1], ALU.max),
               w=[(V, v_half), (V, V.v)])
            for e8 in range(2, 8):
                VE(lambda e, e8=e8: e.tensor_tensor(m1, m1, lgv[:, :, e8], ALU.max))
            for e8 in range(8):
                VE(lambda e, e8=e8: e.tensor_tensor(tmp, lgv[:, :, e8], m1, ALU.is_equal))
                VE(lambda e, e8=e8: e.scalar_tensor_tensor(tmp, tmp, -BIG, lgv[:, :, e8],
                                                           ALU.mult, ALU.add))
                if e8 == 0:
                    VE(lambda e: e.tensor_copy(m2, tmp))
                else:
                    VE(lambda e: e.tensor_tensor(m2, m2, tmp, ALU.max))
            VE(lambda e: e.tensor_tensor(dd, m1, m2, ALU.subtract))
            p.emit("scalar", lambda e: e.activation(out=w1, in_=dd, func=AF.Sigmoid),
                   waits=[(V, V.v)], inc=C)
            c_w1 = C.v
            for e8 in range(8):
                VE(lambda e, e8=e8: e.scalar_tensor_tensor(
                    tmp, lgv[:, :, e8], 1.0, m2, ALU.mult, ALU.is_equal,
                    accum_out=chalf[:, 0, e8:e8 + 1]), w=[(V, V.v), (C, c_w1)])
                VE(lambda e, e8=e8: e.tensor_tensor(dd, lgv[:, :, e8], m1, ALU.is_equal))
                VE(lambda e: e.tensor_tensor(dd, dd, tmp, ALU.subtract))
                VE(lambda e, e8=e8: e.scalar_tensor_tensor(
                    tmp, dd, 1.0, w1, ALU.mult, ALU.mult,
                    accum_out=chalf[:, 1, e8:e8 + 1]))
                VE(lambda e, e8=e8: e.tensor_tensor(
                    chalf[:, 0, e8:e8 + 1], chalf[:, 0, e8:e8 + 1],
                    chalf[:, 1, e8:e8 + 1], ALU.add))
                if g4 == 0:
                    VE(lambda e, e8=e8, tcb=tcb: e.tensor_copy(
                        cbuf[:, tcb, e8:e8 + 1], chalf[:, 0, e8:e8 + 1]))
                else:
                    VE(lambda e, e8=e8, tcb=tcb: e.tensor_tensor(
                        cbuf[:, tcb, e8:e8 + 1], cbuf[:, tcb, e8:e8 + 1],
                        chalf[:, 0, e8:e8 + 1], ALU.add))

        t_lg = {}
        v_lg = {}
        v_top2 = {}
        pend_top2 = []
        ridx = 0
        for tcb in range(16):
            for g4 in range(2):
                pass
            for g4 in range(4):
                for rcl8 in range(8):
                    rc = g4 * 8 + rcl8
                    rcl = (g4 % 2) * 8 + rcl8
                    k = ridx
                    wwait = [(T, t_lg[k - 2])] if k >= 2 else []
                    p.dma("gpsimd", Srw[k % 2], rw[:, k % 2, :, :],
                          RWT[:, rc * 512:(rc + 1) * 512].rearrange("(c p) f -> p c f", p=128),
                          waits=wwait)
                    g_rw = Srw[k % 2].v
                    mw = [(Srw[k % 2], g_rw), (Sx3, g_x3)]
                    if k - 4 in v_lg:
                        mw.append((V, v_lg[k - 4]))
                    if rcl8 == 0:
                        gk = tcb * 4 + g4
                        if gk - 2 >= 0 and gk - 2 in v_top2:
                            mw.append((V, v_top2[gk - 2]))
                    for fi in range(16):
                        p.mm(LG[k % 4][:, :], x3bf[:, fi, tcb * 128:(tcb + 1) * 128],
                             rw[:, k % 2, fi, :],
                             start=(fi == 0), stop=(fi == 15),
                             waits=(mw if fi == 0 else ()), inc=(T if fi == 15 else None))
                    t_lg[k] = T.v
                    p.emit("vector",
                           lambda e, k=k, rcl=rcl, rc=rc:
                           e.scalar_tensor_tensor(lg[:, rcl, :], LG[k % 4][:, :], 1.0,
                                                  rbb[:, rc, :], ALU.mult, ALU.add),
                           waits=[(T, t_lg[k]), (V, v_rbb_all)], inc=V)
                    v_lg[k] = V.v
                    ridx += 1
                gk = tcb * 4 + g4
                lgv = lg.rearrange("p c f -> p (c f)")[:, (g4 % 2) * 4096:(g4 % 2) * 4096 + 4096]                         .rearrange("p (o e) -> p o e", e=8)
                v_grp = V.v

                def VE(fn, w=None):
                    p.emit("vector", fn, waits=([(V, V.v)] if w is None else w), inc=V)

                def _t2(lgv=lgv, tcb=tcb, g4=g4, v_grp=v_grp, gk=gk):
                    _top2_body(lgv, tcb, g4, v_grp, VE)
                    v_top2[gk] = V.v
                pend_top2.append(_t2)
                if len(pend_top2) > 1:
                    pend_top2.pop(0)()
        while pend_top2:
            pend_top2.pop(0)()
        v_c_all = V.v
        t_r_end = T.v

        t_eo = {}
        v_ac = {}
        eidx = 0
        for fc in range(4):
            for e8 in range(8):
                wwait = [(T, t_eo[eidx - 2])] if eidx >= 2 else [(V, v_c_all), (T, t_r_end), (Srw[0], Srw[0].v), (Srw[1], Srw[1].v)]
                p.dma("gpsimd", Srw[eidx % 2], rw[:, eidx % 2, :, :],
                      EWT[e8].rearrange("(c p) f -> p c f", p=128)[:, :, fc * 512:(fc + 1) * 512],
                      waits=wwait)
                g_ew = Srw[eidx % 2].v
                p.dma("sync", Seb[eidx % 2], ebrow[:, eidx % 2, :],
                      EB[e8:e8 + 1, fc * 512:(fc + 1) * 512],
                      waits=([(T, t_eo[eidx - 2])] if eidx >= 2 else []))
                s_eb = Seb[eidx % 2].v
                for tcb in range(16):
                    kk = eidx * 16 + tcb
                    mw = [(Srw[eidx % 2], g_ew), (Seb[eidx % 2], s_eb), (Sx3, g_x3)]
                    if kk - 2 in v_ac:
                        mw.append((V, v_ac[kk - 2]))
                    for fi in range(16):
                        p.mm(LG[kk % 2][:, :], x3bf[:, fi, tcb * 128:(tcb + 1) * 128],
                             rw[:, eidx % 2, fi, :],
                             start=(fi == 0), stop=False,
                             waits=(mw if fi == 0 else ()), inc=None)
                    p.mm(LG[kk % 2][:, :], ones1[0:1, :], ebrow[0:1, eidx % 2, :],
                         start=False, stop=True, inc=T)
                    t_mm = T.v
                    if e8 == 0:
                        p.emit("vector",
                               lambda e, kk=kk, tcb=tcb, e8=e8:
                               e.tensor_scalar(acc[:, tcb, :], LG[kk % 2][:, :],
                                               cbuf[:, tcb, e8:e8 + 1], None, op0=ALU.mult),
                               waits=[(T, t_mm), (V, v_c_all)] +
                                     ([(Say, Say.v)] if fc > 0 else []), inc=V)
                    else:
                        p.emit("vector",
                               lambda e, kk=kk, tcb=tcb, e8=e8:
                               e.scalar_tensor_tensor(acc[:, tcb, :], LG[kk % 2][:, :],
                                                      cbuf[:, tcb, e8:e8 + 1], acc[:, tcb, :],
                                                      ALU.mult, ALU.add),
                               waits=[(T, t_mm), (V, V.v)], inc=V)
                    v_ac[kk] = V.v
                t_eo[eidx] = T.v
                eidx += 1
            for tcb in range(16):
                p.dma("sync", Say, ASCR[tcb * 128:(tcb + 1) * 128, fc * 512:(fc + 1) * 512],
                      acc[:, tcb, :], waits=[(V, v_ac[(eidx - 1) * 16 + tcb])])
        a_done = Say.v

        v_fin = {}
        for tcb in range(16):
            j = tcb % 2
            p.dma("sync", Syb[j], yb[:, j, :], ASCR[tcb * 128:(tcb + 1) * 128, :],
                  waits=([(V, v_fin[tcb - 2])] if tcb >= 2 else [(Say, a_done)]))
            s_y = Syb[j].v
            p.dma("sync", Sxt[j], xtc[:, j, :], X3TOK[tcb * 128:(tcb + 1) * 128, :],
                  waits=([(V, v_fin[tcb - 2])] if tcb >= 2 else []))
            s_x = Sxt[j].v
            p.emit("vector",
                   lambda e, j=j: e.tensor_tensor(yb[:, j, :], yb[:, j, :], xtc[:, j, :], ALU.add),
                   waits=[(Syb[j], s_y), (Sxt[j], s_x), (V, V.v)], inc=V)
            v_y = V.v
            p.emit("vector", lambda e, j=j: e.reduce_sum(cols[:, 0:1], yb[:, j, :],
                                                         axis=mybir.AxisListType.X),
                   waits=[(V, v_y)], inc=V)
            p.emit("scalar", lambda e, j=j: e.activation(out=sqf[:, :], in_=yb[:, j, :],
                                                         func=AF.Square),
                   waits=[(V, v_y), (V, V.v)], inc=C)
            c_sq = C.v
            p.emit("vector", lambda e: e.reduce_sum(cols[:, 1:2], sqf[:, :],
                                                    axis=mybir.AxisListType.X),
                   waits=[(C, c_sq), (V, V.v)], inc=V)
            p.emit("vector", lambda e: e.tensor_scalar_mul(cols[:, 2:3], cols[:, 0:1], 1.0 / 2048.0),
                   waits=[(V, V.v)], inc=V)
            p.emit("vector", lambda e: e.tensor_scalar_mul(cols[:, 3:4], cols[:, 1:2], 1.0 / 2048.0),
                   waits=[(V, V.v)], inc=V)
            p.emit("vector", lambda e: e.tensor_tensor(cols[:, 4:5], cols[:, 2:3], cols[:, 2:3],
                                                       ALU.mult), waits=[(V, V.v)], inc=V)
            p.emit("vector", lambda e: e.tensor_tensor(cols[:, 4:5], cols[:, 3:4], cols[:, 4:5],
                                                       ALU.subtract), waits=[(V, V.v)], inc=V)
            v_var = V.v
            p.emit("scalar", lambda e: e.activation(out=cols[:, 5:6], in_=cols[:, 4:5],
                                                    func=AF.Ln, bias=EPS),
                   waits=[(V, v_var)], inc=C)
            p.emit("scalar", lambda e: e.activation(out=cols[:, 5:6], in_=cols[:, 5:6],
                                                    func=AF.Exp, scale=f32(-0.5)),
                   waits=[(C, C.v)], inc=C)
            c_rstd = C.v
            ow = [(C, c_rstd), (V, V.v), (So[j], So[j].v)]
            p.emit("vector",
                   lambda e, j=j: e.tensor_scalar(outc[:, j, :], yb[:, j, :], cols[:, 2:3],
                                                  None, op0=ALU.subtract),
                   waits=ow, inc=V)
            p.emit("vector",
                   lambda e, j=j: e.tensor_scalar(outc[:, j, :], outc[:, j, :], cols[:, 5:6],
                                                  None, op0=ALU.mult),
                   waits=[(V, V.v)], inc=V)
            v_fin[tcb] = V.v
            p.dma("sync", So[j], OUT[tcb * 128:(tcb + 1) * 128, :], outc[:, j, :],
                  waits=[(V, v_fin[tcb])])
    return p




def _declare_all(nc):
    d = {}
    t = lambda n, shp, dt, k="ExternalInput": nc.dram_tensor(n, shp, dt, kind=k).ap()
    d["XT"] = t("xt", [D, S], F32)
    d["WQKV"] = t("wqkv", [D, 3072], F32)
    d["BQKV"] = t("bqkv", [128, 24], F32)
    d["WO"] = t("wo", [D, D], F32)
    d["BO"] = t("bo", [128, 16], F32)
    d["IDENT"] = t("ident", [128, 128], F32)
    d["GIDXT"] = t("gidxt", [D, 5504], I32)
    d["UIDXT"] = t("uidxt", [D, 5504], I32)
    d["DIDXT"] = t("didxt", [5504, D], I32)
    d["LUT"] = t("lut", [1, 16], F32)
    d["WQ"] = t("wq", [D, D], F32)
    d["WK"] = t("wk", [D, D], F32)
    d["WV"] = t("wv", [D, D], F32)
    d["WOL"] = t("wol", [D, D], F32)
    d["LQ1"] = t("lq1", [128, 1], F32)
    d["LK1"] = t("lk1", [128, 1], F32)
    d["LQ2"] = t("lq2", [128, 1], F32)
    d["LK2"] = t("lk2", [128, 1], F32)
    d["SUBW"] = t("subw", [128, 2], F32)
    d["MASKS"] = t("maskst", [128, 8 * 512], F32).rearrange("p (j q) -> p j q", j=8)
    d["MCOL"] = t("mcol", [128, 8], F32)
    d["META"] = t("meta", [1, 64], U32)
    d["RWT"] = t("rwt", [D, 16384], F32)
    d["RB"] = t("rb", [1, 16384], F32)
    d["EWT"] = t("ewt", [8, D, D], F32)
    d["EB"] = t("eb", [8, D], F32)
    d["OUT"] = t("outk", [TOWN, D], F32, "ExternalOutput")
    s = lambda n, shp, dt: nc.dram_tensor(n, shp, dt).ap()
    d["KVS"] = s("kvs", [1024, S], BF16)
    d["X1T"] = s("x1ts", [D, TOWN], F32)
    d["GSCR"] = s("gscr", [D, 5504], BF16)
    d["USCR"] = s("uscr", [D, 5504], BF16)
    d["DSCR"] = s("dscr", [5504, D], BF16)
    d["HSCR"] = s("hscr", [5504, TOWN], BF16)
    d["X2T"] = s("x2ts", [D, TOWN], F32)
    d["KSCR"] = s("kscr", [D, S], BF16)
    d["VSCR"] = s("vscr", [S, D], BF16)
    d["X3T"] = s("x3ts_d", [D, TOWN], F32)
    d["X3TOK"] = s("x3tok", [TOWN, D], F32)
    d["ASCR"] = s("ascr", [TOWN, D], F32)
    d["SHX2"] = nc.dram_tensor("shx2", [2, D, TOWN], BF16, addr_space="Shared").ap()
    d["SHFLAG"] = nc.dram_tensor("shflag", [2, 16], U32, addr_space="Shared").ap()
    return d


def build_fused(stub=False):
    nc = bass.Bass("TRN2", target_bir_lowering=False)
    d = _declare_all(nc)
    with ExitStack() as stk:
        p = P(nc)
        p._stack = stk
        emit_l0_attn(p, nc, d["XT"], d["WQKV"], d["BQKV"], d["WO"], d["BO"], d["IDENT"],
                     d["X1T"], d["KVS"],
                     MAT=(d["GIDXT"], d["UIDXT"], d["DIDXT"], d["LUT"],
                          d["GSCR"], d["USCR"], d["DSCR"]))
        emit_l0_ffn(p, nc, d["X1T"], d["GIDXT"], d["UIDXT"], d["DIDXT"], d["LUT"],
                    d["X2T"], d["GSCR"], d["USCR"], d["DSCR"], d["HSCR"])
        emit_l1_attn(p, nc, d["X2T"], None, d["WQ"], d["WK"], d["WV"], d["WOL"],
                     d["LQ1"], d["LK1"], d["LQ2"], d["LK2"], d["SUBW"],
                     d["MASKS"], d["MCOL"], d["META"], d["IDENT"], d["X3T"], d["X3TOK"],
                     d["KSCR"], d["VSCR"],
                     EXCH=dict(shx2=d["SHX2"], shflag=d["SHFLAG"], stub=stub))
        emit_l1_moe(p, nc, d["X3T"], d["X3TOK"], d["RWT"], d["RB"], d["EWT"], d["EB"],
                    d["OUT"], d["ASCR"])
        block = stk.enter_context(nc.Block())
        p.emit("sync", lambda e: None,
               waits=[(c, c.v) for nm, c in p.pool.items() if nm != "dSp"])
        p.run(block)
    return nc


from concourse.bass_utils import run_bass_kernel_spmd


def _perm_for_core(c):
    pp = c & 1
    own = [2 * i + pp for i in range(4)]
    other = [2 * i + (1 - pp) for i in range(4)]
    return np.concatenate([np.arange(g * 512, (g + 1) * 512) for g in own + other])


_NC = {}


def kernel(L0_attn_x, L0_attn_Wqkv, L0_attn_bqkv, L0_attn_Wo, L0_attn_bo,
           L0_ffn_lut, L0_ffn_gate_idx, L0_ffn_up_idx, L0_ffn_down_idx,
           L1_attn_Wq, L1_attn_Wk, L1_attn_Wv, L1_attn_Wo,
           L1_attn_lambda_q1, L1_attn_lambda_k1, L1_attn_lambda_q2, L1_attn_lambda_k2,
           L1_attn_subln_weight, L1_ffn_router_w, L1_ffn_router_b,
           L1_ffn_expert_w, L1_ffn_expert_b):
    x = np.ascontiguousarray(np.asarray(L0_attn_x, dtype=np.float32))
    if "F" not in _NC:
        _NC["F"] = build_fused(stub=False)
    nc = _NC["F"]
    ident = np.eye(128, dtype=np.float32)
    bq = np.ascontiguousarray(np.asarray(L0_attn_bqkv, np.float32).reshape(24, 128).T)
    bo = np.ascontiguousarray(np.asarray(L0_attn_bo, np.float32).reshape(16, 128).T)
    gidxt = np.ascontiguousarray(np.asarray(L0_ffn_gate_idx).T)
    uidxt = np.ascontiguousarray(np.asarray(L0_ffn_up_idx).T)
    didxt = np.ascontiguousarray(np.asarray(L0_ffn_down_idx).T)
    lut = np.asarray(L0_ffn_lut, np.float32).reshape(1, 16)
    subw = np.ascontiguousarray(np.asarray(L1_attn_subln_weight, np.float32).reshape(2, 128).T)
    rwt = np.ascontiguousarray(np.asarray(L1_ffn_router_w, np.float32).T)
    rb = np.asarray(L1_ffn_router_b, np.float32).reshape(1, 16384)
    ewt = np.ascontiguousarray(np.asarray(L1_ffn_expert_w, np.float32).transpose(0, 2, 1))
    eb = np.asarray(L1_ffn_expert_b, np.float32)
    nonce = np.uint32(np.random.randint(1, 2 ** 31 - 1))
    kk = np.arange(128)[:, None]
    qq = np.arange(512)[None, :]
    perms = [_perm_for_core(c) for c in range(8)]
    in_maps = []
    for c in range(8):
        b, pp = c // 2, c & 1
        m = np.zeros((8, 128, 512), np.float32)
        for q2 in range(2):
            for bb in range(4):
                m[q2 * 4 + bb] = ((bb * 128 + kk) <= qq).astype(np.float32) if q2 == 0 \
                    else (1.0 if pp == 1 else 0.0)
        mc = np.ones((128, 8), np.float32)
        mc[:, 4:8] = 1.0 if pp == 1 else 0.0
        meta = np.zeros((1, 64), np.uint32)
        meta[0, 40] = pp
        meta[0, 41] = 1 - pp
        meta[0, 42] = nonce
        in_maps.append(dict(
            xt=np.ascontiguousarray(x[b].T[:, perms[c]]),
            wqkv=np.asarray(L0_attn_Wqkv, np.float32), bqkv=bq,
            wo=np.asarray(L0_attn_Wo, np.float32), bo=bo, ident=ident,
            gidxt=gidxt, uidxt=uidxt, didxt=didxt, lut=lut,
            wq=np.asarray(L1_attn_Wq, np.float32), wk=np.asarray(L1_attn_Wk, np.float32),
            wv=np.asarray(L1_attn_Wv, np.float32), wol=np.asarray(L1_attn_Wo, np.float32),
            lq1=np.asarray(L1_attn_lambda_q1, np.float32).reshape(128, 1),
            lk1=np.asarray(L1_attn_lambda_k1, np.float32).reshape(128, 1),
            lq2=np.asarray(L1_attn_lambda_q2, np.float32).reshape(128, 1),
            lk2=np.asarray(L1_attn_lambda_k2, np.float32).reshape(128, 1),
            subw=subw,
            maskst=np.ascontiguousarray(m.transpose(1, 0, 2).reshape(128, 8 * 512)),
            mcol=mc,
            meta=meta, rwt=rwt, rb=rb, ewt=ewt, eb=eb,
        ))
    res = run_bass_kernel_spmd(nc, in_maps, core_ids=list(range(8)))
    out = np.empty((4, 4096, 2048), np.float32)
    for c in range(8):
        out[c // 2][perms[c][:2048]] = res.results[c]["outk"]
    return out


# version 07
# speedup vs baseline: 1.1606x; 1.0685x over previous
import numpy as np
import concourse.bass as bass
import concourse.mybir as mybir

F32 = mybir.dt.float32
BF16 = mybir.dt.bfloat16
I32 = mybir.dt.int32
U32 = mybir.dt.uint32
AF = mybir.ActivationFunctionType
ALU = mybir.AluOpType
f32 = np.float32

D = 2048
S = 4096
TOWN = 2048
NF = 16
EPS = f32(1e-5)

ENGS = ("tensor", "vector", "scalar", "sync", "gpsimd")


class Ctr:
    def __init__(self, sem, name=""):
        self.sem = sem
        self.v = 0
        self.name = name


class P:
    def __init__(self, nc):
        self.nc = nc
        self.seq = {k: [] for k in ENGS}
        self.pool = {}
        self._stack = None
        self.pending = {}

    def S(self, name):
        if name not in self.pool:
            self.pool[name] = Ctr(self._stack.enter_context(self.nc.semaphore(name)))
        return self.pool[name]

    def begin_part(self):
        snap = [(c, c.v) for c in self.pool.values()]
        self.pending = {ek: list(snap) for ek in ENGS}

    def emit(self, ek, fn, waits=(), inc=None, incn=1):
        waits = list(waits)
        if self.pending.get(ek):
            waits = self.pending.pop(ek) + waits
        wl = tuple((c.sem, int(v)) for (c, v) in waits)
        if inc is not None:
            inc.v += incn
            sem, n = inc.sem, incn
        else:
            sem = None
            n = 0

        def run(eng, fn=fn, wl=wl, sem=sem, n=n):
            for s, v in wl:
                eng.wait_ge(s, v)
            inst = fn(eng)
            if sem is not None:
                assert inst is not None
                inst.then_inc(sem, n)
        self.seq[ek].append(run)

    def dma(self, ek, ctr, out, in_, waits=()):
        self.emit(ek, lambda e, o=out, i=in_: e.dma_start(out=o, in_=i),
                  waits=waits, inc=ctr, incn=16)
        return ctr.v

    def mm(self, out, lhsT, rhs, start, stop, waits=(), inc=None, **kw):
        self.emit("tensor",
                  lambda e, o=out, l=lhsT, r=rhs, st=start, sp=stop, kw=kw:
                  e.matmul(o, l, r, start=st, stop=sp, **kw),
                  waits=waits, inc=inc)
        return None if inc is None else inc.v

    def run(self, block):
        for ek in ENGS:
            fns = self.seq[ek]
            getattr(block, ek)(lambda eng, fns=fns: [fn(eng) for fn in fns])


def ln_rows(p, C, V, T, srow, var_i, rstd_i, m_i, ms_i, mr_i, statY, statY2, ntok_inv,
            waits_stats):
    p.emit("vector", lambda e: e.tensor_scalar_mul(srow[0:1, m_i, :], statY, float(ntok_inv)),
           waits=waits_stats, inc=V)
    p.emit("vector", lambda e: e.tensor_scalar_mul(srow[0:1, ms_i, :], statY2, float(ntok_inv)),
           inc=V)
    p.emit("vector", lambda e: e.tensor_tensor(srow[0:1, var_i, :], srow[0:1, m_i, :],
                                               srow[0:1, m_i, :], ALU.mult), waits=[(V, V.v)], inc=V)
    p.emit("vector", lambda e: e.tensor_tensor(srow[0:1, var_i, :], srow[0:1, ms_i, :],
                                               srow[0:1, var_i, :], ALU.subtract), waits=[(V, V.v)], inc=V)
    p.emit("scalar", lambda e: e.activation(out=srow[0:1, rstd_i, :], in_=srow[0:1, var_i, :],
                                            func=AF.Ln, bias=EPS),
           waits=[(V, V.v)], inc=C)
    p.emit("scalar", lambda e: e.activation(out=srow[0:1, rstd_i, :], in_=srow[0:1, rstd_i, :],
                                            func=AF.Exp, scale=f32(-0.5)), waits=[(C, C.v)], inc=C)
    p.emit("vector", lambda e: e.tensor_tensor(srow[0:1, mr_i, :], srow[0:1, m_i, :],
                                               srow[0:1, rstd_i, :], ALU.mult),
           waits=[(C, C.v), (V, V.v)], inc=V)




from contextlib import ExitStack


def emit_l0_attn(p, nc, XT, WQKV, BQKV, WO, BO, IDENT, X1T, KVS, MAT=None):
    p.begin_part()
    with ExitStack() as ctx:
        sb = lambda *a: ctx.enter_context(nc.sbuf_tensor(*a, allow_name_mangling=True))
        psum = lambda *a: ctx.enter_context(nc.psum_tensor(*a, allow_name_mangling=True))
        sem = lambda n: p.S(n)
        xbf = sb("xbf", [128, 16, 2048], BF16)
        qt = sb("qt", [128, 16, 2048], BF16)
        wstr = sb("wstr", [128, 2, 16, 128], BF16)
        bcol24 = sb("bcol24", [128, 24], F32)
        bocol = sb("bocol", [128, 16], F32)
        evac = sb("evac", [128, 2, 512], BF16)
        kg2 = sb("kg2", [128, 1, S], BF16)
        vg = sb("vg", [64, 1, S], BF16)
        vtok = sb("vtok", [128, 32, 65], BF16)
        pt = sb("pt", [128, 3, 512], BF16)
        rrowb = sb("rrowb", [1, 2, 512], F32)
        otmp = sb("otmp", [64, 512], F32)
        ones128 = sb("ones128", [128, 1], F32)
        ones1 = sb("ones1", [1, 128], F32)
        ident_s = sb("ident_s", [128, 128], BF16)
        xres = sb("xres", [128, 2, 512], F32)
        ysq = sb("ysq", [128, 2, 512], F32)
        srow = sb("srow", [1, 5, 512], F32)
        bcast = sb("bcast", [128, 2, 512], F32)
        x1c = sb("x1c", [128, 2, 512], F32)
        aot = xbf
        ybuf = qt.ap().rearrange("p c f -> p (c f)").bitcast(F32).rearrange("p (c f) -> p c f", c=16)
        PP = [psum("PP0", [128, 512], F32), psum("PP1", [128, 512], F32)]
        STp = [psum("ST0", [128, 512], F32), psum("ST1", [128, 512], F32)]
        STp3 = [STp[0], STp[1], PP[1]]
        OPp = [psum("OP0", [65, 512], F32), psum("OP1", [65, 512], F32)]
        TRB = [psum("TRB0", [128, 128], BF16), psum("TRB1", [128, 128], BF16)]
        ST0, ST1 = STp[0], STp[1]
        T = sem("sT"); C = sem("sC"); V = sem("sV")
        Spre = sem("dPre"); SpreG = sem("dPreG"); Sxbf = sem("dXbf")
        Sw = [sem("dW0"), sem("dW1")]
        Skvw = [sem("dKvw0"), sem("dKvw1")]
        Skg = sem("dKg")
        Sxr = [sem("dXr0"), sem("dXr1")]
        Sx1 = [sem("dX10"), sem("dX11")]

        XTv = XT.rearrange("d (b c) -> d b c", c=512)
        X1Tv = X1T.rearrange("d (b c) -> d b c", c=512)

        p.emit("vector", lambda e: e.memset(ones128[:], 1.0), inc=V)
        p.emit("vector", lambda e: e.memset(ones1[:], 1.0), inc=V)
        p.emit("vector", lambda e: e.memset(vtok[:, :, 64:65], 1.0), inc=V)
        v_pre = V.v
        bg = []
        if MAT is not None:
            GIDXT, UIDXT, DIDXT, LUT, GSCR, USCR, DSCR = MAT
            idxq = sb("idxq", [128, 2, 1376], BF16)
            accq = sb("accq", [128, 2, 1376], BF16)
            maskq = sb("maskq", [128, 1376], BF16)
            lutrow = sb("lutrow", [1, 16], F32)
            lutcol = sb("lutcol", [128, 16], F32)
            Sidx = [p.S("dIdx0"), p.S("dIdx1")]
            Sacc = [p.S("dAcc0"), p.S("dAcc1")]
            Slut = p.S("dLut")
            p.dma("sync", Slut, lutrow[:], LUT)
            p.mm(PP[0][:, 0:16], ones1[0:1, :], lutrow[0:1, :], start=True, stop=True,
                 waits=[(Slut, Slut.v), (V, v_pre)], inc=T)
            p.emit("vector", lambda e: e.tensor_copy(lutcol[:, :], PP[0][:, 0:16]),
                   waits=[(T, T.v)], inc=V)
            v_lut = V.v
            mstate = {"mi": 0, "vmat": {}}

            def _mk_thunks(IDXT, SCR, nfi, F):
                for fi in range(nfi):
                    for q4 in range(4):
                        qf = F // 4
                        mi = mstate["mi"]

                        def t_dma(mi=mi, fi=fi, q4=q4, qf=qf, IDXT=IDXT):
                            iw = [(V, mstate["vmat"].get(mi - 2, v_lut))]
                            p.dma("gpsimd", Sidx[mi % 2], idxq[:, mi % 2, 0:qf],
                                  IDXT[fi * 128:(fi + 1) * 128, q4 * qf:(q4 + 1) * qf],
                                  waits=iw)
                        bg.append(t_dma)
                        for vg in range(4):
                            def t_ops(mi=mi, vg=vg, qf=qf):
                                g_i = Sidx[mi % 2].v
                                for vv in range(vg * 4, vg * 4 + 4):
                                    w0 = [(Sidx[mi % 2], g_i), (V, V.v)]
                                    if vv == 0:
                                        w0.append((Sacc[mi % 2], Sacc[mi % 2].v))
                                        p.emit("vector",
                                               lambda e, vv=vv, qf=qf, mi=mi:
                                               e.tensor_scalar(accq[:, mi % 2, 0:qf],
                                                               idxq[:, mi % 2, 0:qf],
                                                               float(vv), lutcol[:, vv:vv + 1],
                                                               op0=ALU.is_equal, op1=ALU.mult),
                                               waits=w0, inc=V)
                                    else:
                                        p.emit("vector",
                                               lambda e, vv=vv, qf=qf, mi=mi:
                                               e.tensor_scalar(maskq[:, 0:qf],
                                                               idxq[:, mi % 2, 0:qf],
                                                               float(vv), lutcol[:, vv:vv + 1],
                                                               op0=ALU.is_equal, op1=ALU.mult),
                                               waits=w0, inc=V)
                                        p.emit("vector",
                                               lambda e, qf=qf, mi=mi:
                                               e.tensor_tensor(accq[:, mi % 2, 0:qf],
                                                               accq[:, mi % 2, 0:qf],
                                                               maskq[:, 0:qf], ALU.add),
                                               waits=[(V, V.v)], inc=V)
                            bg.append(t_ops)

                        def t_store(mi=mi, fi=fi, q4=q4, qf=qf, SCR=SCR):
                            mstate["vmat"][mi] = V.v
                            p.dma("sync", Sacc[mi % 2],
                                  SCR[fi * 128:(fi + 1) * 128, q4 * qf:(q4 + 1) * qf],
                                  accq[:, mi % 2, 0:qf], waits=[(V, V.v)])
                        bg.append(t_store)
                        mstate["mi"] = mi + 1

            _mk_thunks(GIDXT, GSCR, 16, 5504)
            _mk_thunks(UIDXT, USCR, 16, 5504)
            _mk_thunks(DIDXT, DSCR, 43, 2048)

        def drain(n):
            for _ in range(n):
                if bg:
                    bg.pop(0)()
        p.dma("sync", Spre, bcol24[:, :], BQKV)
        p.dma("sync", Spre, bocol[:, :], BO, waits=[(Spre, Spre.v)])
        p.dma("gpsimd", SpreG, ident_s[:], IDENT)
        spre = Spre.v
        spreg = SpreG.v

        t_group_end = {}
        v_evac = {}
        gidx = 0
        ev_i = 0
        t_half_end = 0
        for half in range(2):
            for fi in range(16):
                p.dma("gpsimd", Sxbf, xbf[:, fi, :],
                      XT[fi * 128:(fi + 1) * 128, half * 2048:(half + 1) * 2048],
                      waits=([(T, t_half_end)] if (half == 1 and fi == 0) else ()))
            g_x = Sxbf.v
            nfo = 24 if half == 0 else 8
            for fo in range(nfo):
                col = (16 + fo) if fo < 8 else (fo - 8)
                wwait = [(T, t_group_end[gidx - 2])] if gidx >= 2 else []
                p.dma("gpsimd", Sw[gidx % 2], wstr[:, gidx % 2, :, :],
                      WQKV[:, col * 128:(col + 1) * 128].rearrange("(c p) f -> p c f", p=128),
                      waits=wwait)
                g_w = Sw[gidx % 2].v
                for tc in range(4):
                    pj = (gidx * 4 + tc) % 2
                    key = gidx * 4 + tc
                    mwaits = [(Sw[gidx % 2], g_w), (Sxbf, g_x), (Spre, spre)]
                    if MAT is not None and gidx == 0 and tc == 0:
                        mwaits.append((V, v_lut))
                    if key - 2 in v_evac:
                        mwaits.append((V, v_evac[key - 2]))
                    for fi in range(16):
                        p.mm(PP[pj][:, :], wstr[:, gidx % 2, fi, :],
                             xbf[:, fi, tc * 512:(tc + 1) * 512],
                             start=(fi == 0), stop=(fi == 15),
                             waits=(mwaits if fi == 0 else ()), inc=(T if fi == 15 else None))
                    t_mm = T.v
                    if fo < 8:
                        ej = ev_i % 2
                        ewaits = [(T, t_mm), (Spre, spre), (Skvw[ej], Skvw[ej].v)]
                        p.emit("vector",
                               lambda e, ej=ej, pj=pj, col=col:
                               e.tensor_scalar_add(evac[:, ej, :], PP[pj][:, :], bcol24[:, col:col + 1]),
                               waits=ewaits, inc=V)
                        v_evac[key] = V.v
                        p.dma("sync", Skvw[ej],
                              KVS[(col - 16) * 128:(col - 15) * 128,
                                  half * 2048 + tc * 512: half * 2048 + (tc + 1) * 512],
                              evac[:, ej, :], waits=[(V, V.v)])
                        ev_i += 1
                    else:
                        p.emit("vector",
                               lambda e, pj=pj, col=col, tc=tc:
                               e.tensor_scalar_add(qt[:, col, tc * 512:(tc + 1) * 512],
                                                   PP[pj][:, :], bcol24[:, col:col + 1]),
                               waits=[(T, t_mm), (Spre, spre)], inc=V)
                        v_evac[key] = V.v
                    t_group_end[gidx] = t_mm
                    drain(2)
                gidx += 1
            t_half_end = T.v
        kv_done = [(Skvw[0], Skvw[0].v), (Skvw[1], Skvw[1].v)]
        t_p1_end = T.v

        t_g_end = {}
        si = 0
        c_exp = {}
        t_pv = {}
        v_norm = {}
        v_recd = {}
        prev_tr = [0, 0]

        def _a_norm_tail(b):
            blo = ((b // 4) % 2) * 64
            bqtile = (b // 4) // 2
            bqc = b % 4
            p.mm(PP[0][0:64, :], ones1[0:1, 0:64], rrowb[0:1, b % 2, :], start=True, stop=True,
                 waits=[(V, v_recd[b])] + ([(V, v_norm[b - 1])] if b - 1 in v_norm else []),
                 inc=T)
            t_bc = T.v
            p.emit("vector", lambda e, b=b: e.tensor_copy(otmp[:], OPp[b % 2][0:64, :]),
                   waits=[(T, t_bc)], inc=V)
            p.emit("vector",
                   lambda e, blo=blo, bqtile=bqtile, bqc=bqc:
                   e.tensor_tensor(aot[blo:blo + 64, bqtile, bqc * 512:(bqc + 1) * 512],
                                   otmp[:], PP[0][0:64, :], ALU.mult),
                   waits=[(V, V.v)], inc=V)
            v_norm[b] = V.v
        for g8 in range(8):
            kwait = list(kv_done)
            if g8 >= 1:
                kwait.append((T, t_g_end[g8 - 1]))
            p.dma("sync", Skg, kg2[0:64, 0, :], KVS[g8 * 64:(g8 + 1) * 64, :], waits=kwait)
            p.dma("sync", Skg, kg2[64:128, 0, :], KVS[g8 * 64:(g8 + 1) * 64, :])
            p.dma("sync", Skg, vg[:, 0, :], KVS[512 + g8 * 64:512 + (g8 + 1) * 64, :])
            sy_kgv = Skg.v
            v_tr = {}
            for kc in range(32):
                trw = [(Skg, sy_kgv), (V, v_pre), (SpreG, spreg)]
                trw.append((V, prev_tr[kc % 2]) if kc < 2 else (V, v_tr[kc - 2]))
                p.emit("tensor",
                       lambda e, kc=kc: e.transpose(TRB[kc % 2][:, 0:64],
                                                    vg[:, 0, kc * 128:(kc + 1) * 128],
                                                    ident_s[0:64, 0:64]),
                       waits=trw, inc=T)
                p.emit("vector",
                       lambda e, kc=kc: e.tensor_copy(vtok[:, kc, 0:64], TRB[kc % 2][:, 0:64]),
                       waits=[(T, T.v)], inc=V)
                v_tr[kc] = V.v
            prev_tr = [v_tr[30], v_tr[31]]
            v_vtok = V.v
            for h in range(4):
                hh = g8 * 4 + h
                lo = (hh % 2) * 64
                qtile = hh // 2
                for qc in range(4):
                    a = hh * 4 + qc

                    def _st(kc, si0):
                        stw = []
                        if kc <= 1:
                            stw.append((V, v_vtok))
                        if si0 - 3 in c_exp:
                            stw.append((C, c_exp[si0 - 3]))
                        p.mm(STp3[si0 % 3][:, :],
                             kg2[lo:lo + 64, 0, kc * 128:(kc + 1) * 128],
                             qt[lo:lo + 64, qtile, qc * 512:(qc + 1) * 512],
                             start=True, stop=True, waits=stw, inc=T)
                        expw = [(T, T.v)]
                        if si0 - 3 in t_pv:
                            expw.append((T, t_pv[si0 - 3]))
                        p.emit("scalar",
                               lambda e, si0=si0: e.activation(out=pt[:, si0 % 3, :],
                                                               in_=STp3[si0 % 3][:, :],
                                                               func=AF.Exp, scale=f32(0.125)),
                               waits=expw, inc=C)
                        c_exp[si0] = C.v

                    _st(0, si)
                    _st(1, si + 1)
                    for kc in range(32):
                        if kc + 2 < 32:
                            _st(kc + 2, si + kc + 2)
                        s2 = si + kc
                        p.mm(OPp[a % 2][:, :], vtok[:, kc, :], pt[:, s2 % 3, :],
                             start=(kc == 0), stop=(kc == 31),
                             waits=[(C, c_exp[s2])] + ([(V, v_norm[a - 2])] if (kc == 0 and a - 2 in v_norm) else []),
                             inc=T)
                        t_pv[s2] = T.v
                    si += 32
                    drain(10)
                    p.emit("vector", lambda e, a=a: e.reciprocal(rrowb[:, a % 2, :],
                                                                 OPp[a % 2][64:65, :]),
                           waits=[(T, t_pv[si - 1])], inc=V)
                    v_recd[a] = V.v
                    if a >= 1:
                        _a_norm_tail(a - 1)
            t_g_end[g8] = T.v
        _a_norm_tail(127)
        v_aot = V.v

        t_wo_end = {}
        t_stat = {}
        v_y = {}
        v_lnread = 0
        widx = 0
        for tc in range(4):
            for fo in range(16):
                wwait = [(T, t_p1_end)] if widx < 2 else [(T, t_wo_end[widx - 2])]
                p.dma("gpsimd", Sw[widx % 2], wstr[:, widx % 2, :, :],
                      WO[:, fo * 128:(fo + 1) * 128].rearrange("(c p) f -> p c f", p=128),
                      waits=wwait)
                g_wo = Sw[widx % 2].v
                xw = [(V, v_y[widx - 2])] if widx >= 2 else []
                p.dma("sync", Sxr[widx % 2], xres[:, widx % 2, :], XTv[fo * 128:(fo + 1) * 128, tc, :],
                      waits=xw)
                sy_x = Sxr[widx % 2].v
                mwaits = [(Sw[widx % 2], g_wo), (V, v_aot)]
                if widx >= 2:
                    mwaits.append((V, v_y[widx - 2]))
                for fi in range(16):
                    p.mm(PP[widx % 2][:, :], wstr[:, widx % 2, fi, :],
                         aot[:, fi, tc * 512:(tc + 1) * 512],
                         start=(fi == 0), stop=(fi == 15),
                         waits=(mwaits if fi == 0 else ()), inc=(T if fi == 15 else None))
                t_wo_end[widx] = T.v
                p.emit("vector",
                       lambda e, widx=widx, fo=fo:
                       e.scalar_tensor_tensor(ybuf[:, fo, 0:512], PP[widx % 2][:, :], bocol[:, fo:fo + 1],
                                              xres[:, widx % 2, :], ALU.add, ALU.add),
                       waits=[(T, t_wo_end[widx]), (Sxr[widx % 2], sy_x), (Spre, spre)], inc=V)
                v_y[widx] = V.v
                sqw = [(V, v_y[widx])]
                if widx >= 2:
                    sqw.append((T, t_stat[widx - 2]))
                p.emit("scalar",
                       lambda e, widx=widx, fo=fo: e.activation(out=ysq[:, widx % 2, :],
                                                                in_=ybuf[:, fo, 0:512],
                                                                func=AF.Square),
                       waits=sqw, inc=C)
                c_sq = C.v
                statw = [(V, v_y[widx])]
                if fo == 0:
                    statw.append((V, v_lnread))
                p.mm(OPp[0][0:1, :], ones128[:, :], ybuf[:, fo, 0:512], start=(fo == 0), stop=(fo == 15),
                     waits=statw, inc=None)
                p.mm(OPp[1][0:1, :], ones128[:, :], ysq[:, widx % 2, :], start=(fo == 0), stop=(fo == 15),
                     waits=([(C, c_sq)] + ([(V, v_lnread)] if fo == 0 else [])), inc=T)
                t_stat[widx] = T.v
                drain(2)
                widx += 1
            ln_rows(p, C, V, T, srow, var_i=2, rstd_i=3, m_i=0, ms_i=1, mr_i=4,
                    statY=OPp[0][0:1, :], statY2=OPp[1][0:1, :], ntok_inv=1.0 / 2048.0,
                    waits_stats=[(T, T.v)])
            v_lnread = V.v
            v_rows = V.v
            p.mm(ST0[:, :], ones1[0:1, :], srow[0:1, 3, :], start=True, stop=True,
                 waits=[(V, v_rows)], inc=T)
            p.mm(ST1[:, :], ones1[0:1, :], srow[0:1, 4, :], start=True, stop=True, inc=T)
            t_bc2 = T.v
            p.emit("vector", lambda e: e.tensor_copy(bcast[:, 0, :], ST0[:, :]),
                   waits=[(T, t_bc2)], inc=V)
            p.emit("vector", lambda e: e.tensor_copy(bcast[:, 1, :], ST1[:, :]), inc=V)
            v_bc = V.v
            for fo in range(16):
                oidx = tc * 16 + fo
                ow = [(V, v_bc), (Sx1[oidx % 2], Sx1[oidx % 2].v)]
                p.emit("vector",
                       lambda e, fo=fo:
                       e.scalar_tensor_tensor(x1c[:, fo % 2, :], ybuf[:, fo, 0:512], 1.0,
                                              bcast[:, 0, :], ALU.mult, ALU.mult),
                       waits=ow, inc=V)
                p.emit("vector",
                       lambda e, fo=fo: e.tensor_tensor(x1c[:, fo % 2, :], x1c[:, fo % 2, :],
                                                        bcast[:, 1, :], ALU.subtract),
                       waits=[(V, V.v)], inc=V)
                p.dma("sync", Sx1[oidx % 2], X1Tv[fo * 128:(fo + 1) * 128, tc, :], x1c[:, fo % 2, :],
                      waits=[(V, V.v)])
        while bg:
            bg.pop(0)()
    return p




def emit_l0_ffn(p, nc, X1T, GIDXT, UIDXT, DIDXT, LUT, X2T, GSCR, USCR, DSCR, HSCR, MOVED_MAT=True):
    p.begin_part()
    NI = 5504
    NIC = 43
    with ExitStack() as ctx:
        sb = lambda *a: ctx.enter_context(nc.sbuf_tensor(*a, allow_name_mangling=True))
        psum = lambda *a: ctx.enter_context(nc.psum_tensor(*a, allow_name_mangling=True))
        sem = lambda n: p.S(n)
        arena = sb("arena", [128, 61440], BF16)
        ar = arena.ap()
        x1bf = ar[:, 0:32768].rearrange("p (c f) -> p c f", c=16)
        idxb = ar[:, 32768:43776].rearrange("p (c f) -> p c f", c=2)
        accb = ar[:, 43776:54784].rearrange("p (c f) -> p c f", c=2)
        lutrow = sb("lutrow", [1, 16], F32)
        lutcol = sb("lutcol", [128, 16], F32)
        ones1 = sb("ones1", [1, 128], F32)
        ones128 = sb("ones128", [128, 1], F32)
        gw = sb("gw", [128, 2, 16, 128], BF16)
        uw = sb("uw", [128, 2, 16, 128], BF16)
        sg = sb("sg", [128, 512], BF16)
        hc = sb("hc", [128, 2, 512], BF16)
        ht = ar[:, 0:22016].rearrange("p (c f) -> p c f", c=NIC)
        dw = ar[:, 22016:33024].rearrange("p (b c f) -> p b c f", b=2, c=NIC)
        xres = sb("xres", [128, 2, 512], F32)
        ybuf = ar[:, 33024:49408].bitcast(F32).rearrange("p (c f) -> p c f", c=16)
        ysq = sb("ysq", [128, 2, 512], F32)
        srow = sb("srow", [1, 5, 512], F32)
        bcast = sb("bcast", [128, 2, 512], F32)
        x2c = sb("x2c", [128, 2, 512], F32)
        PP = [psum("PP0", [128, 512], F32), psum("PP1", [128, 512], F32)]
        GP = [psum("GP0", [128, 512], F32), psum("GP1", [128, 512], F32)]
        UP = [psum("UP0", [128, 512], F32), psum("UP1", [128, 512], F32)]
        SB0 = psum("SB0", [128, 512], F32)
        SB1 = psum("SB1", [128, 512], F32)
        T = sem("sT"); C = sem("sC"); V = sem("sV")
        Sx1 = sem("dX1"); Sidx = [sem("dIdx0"), sem("dIdx1")]; Sacc = [sem("dAcc0"), sem("dAcc1")]
        Sgw = [sem("dGw0"), sem("dGw1")]; Suw = [sem("dUw0"), sem("dUw1")]
        Shw = [sem("dHw0"), sem("dHw1")]
        Sht = sem("dHt"); Sdw = [sem("dDw0"), sem("dDw1")]
        Sxr = [sem("dXr0"), sem("dXr1")]; So = [sem("dO0"), sem("dO1")]
        Spre = sem("dPre")

        X1Tv = X1T.rearrange("d (b c) -> d b c", c=512)
        X2Tv = X2T.rearrange("d (b c) -> d b c", c=512)

        p.emit("vector", lambda e: e.memset(ones1[:], 1.0), inc=V)
        p.emit("vector", lambda e: e.memset(ones128[:], 1.0), inc=V)
        v_ones = V.v
        p.dma("sync", Spre, lutrow[:], LUT)
        spre = Spre.v
        p.mm(SB1[:, 0:16], ones1[0:1, :], lutrow[0:1, :], start=True, stop=True,
             waits=[(Spre, spre), (V, v_ones)], inc=T)
        p.emit("vector", lambda e: e.tensor_copy(lutcol[:, :], SB1[:, 0:16]),
               waits=[(T, T.v)], inc=V)
        v_lut = V.v
        for fi in range(16):
            p.dma("gpsimd", Sx1, x1bf[:, fi, :], X1T[fi * 128:(fi + 1) * 128, :])
        g_x1 = Sx1.v

        sgm = ar[:, 54784:60288]
        def materialize(IDXT, SCR, nfi, F):
            nonlocal mat_i
            for fi in range(nfi):
                mi = mat_i
                iw = [(V, v_mat.get(mi - 2, v_lut))]
                p.dma("gpsimd", Sidx[mi % 2], idxb[:, mi % 2, 0:F],
                      IDXT[fi * 128:(fi + 1) * 128, :], waits=iw)
                g_i = Sidx[mi % 2].v
                aw = [(Sacc[mi % 2], Sacc[mi % 2].v), (Sidx[mi % 2], g_i), (V, v_lut)]
                p.emit("vector", lambda e, mi=mi, F=F: e.memset(accb[:, mi % 2, 0:F], 0.0),
                       waits=aw, inc=V)
                for vv in range(16):
                    p.emit("vector",
                           lambda e, mi=mi, vv=vv, F=F:
                           e.tensor_scalar(idxb[:, (mi % 2), 0:F] if False else sgm[:, 0:F],
                                           idxb[:, mi % 2, 0:F], float(vv), lutcol[:, vv:vv + 1],
                                           op0=ALU.is_equal, op1=ALU.mult),
                           waits=[(V, V.v)], inc=V)
                    p.emit("vector",
                           lambda e, mi=mi, F=F:
                           e.tensor_tensor(accb[:, mi % 2, 0:F], accb[:, mi % 2, 0:F],
                                           sgm[:, 0:F], ALU.add),
                           waits=[(V, V.v)], inc=V)
                v_mat[mi] = V.v
                p.dma("sync", Sacc[mi % 2], SCR[fi * 128:(fi + 1) * 128, 0:F],
                      accb[:, mi % 2, 0:F], waits=[(V, V.v)])
                mat_i += 1
        mat_i = 0
        v_mat = {}
        if MOVED_MAT is None:
            materialize(GIDXT, GSCR, 16, NI)
            materialize(UIDXT, USCR, 16, NI)
            materialize(DIDXT, DSCR, NIC, D)
        mat_done = [(Sacc[0], Sacc[0].v), (Sacc[1], Sacc[1].v)]

        t_gu_end = {}
        v_h = {}
        c_sg = {}
        for fo in range(NIC):
            wwait = list(mat_done) if fo < 2 else [(T, t_gu_end[fo - 2])]
            p.dma("gpsimd", Sgw[fo % 2], gw[:, fo % 2, :, :],
                  GSCR[:, fo * 128:(fo + 1) * 128].rearrange("(c p) f -> p c f", p=128),
                  waits=wwait)
            g_gw = Sgw[fo % 2].v
            p.dma("gpsimd", Suw[fo % 2], uw[:, fo % 2, :, :],
                  USCR[:, fo * 128:(fo + 1) * 128].rearrange("(c p) f -> p c f", p=128))
            g_uw = Suw[fo % 2].v
            for tc in range(4):
                k = fo * 4 + tc
                gw0 = [(Sgw[fo % 2], g_gw), (Sx1, g_x1)]
                if k - 2 in c_sg:
                    gw0.append((C, c_sg[k - 2]))
                for fi in range(16):
                    p.mm(GP[k % 2][:, :], gw[:, fo % 2, fi, :], x1bf[:, fi, tc * 512:(tc + 1) * 512],
                         start=(fi == 0), stop=(fi == 15),
                         waits=(gw0 if fi == 0 else ()), inc=(T if fi == 15 else None))
                t_g = T.v
                uw0 = [(Suw[fo % 2], g_uw)]
                if k - 2 in v_h:
                    uw0.append((V, v_h[k - 2]))
                for fi in range(16):
                    p.mm(UP[k % 2][:, :], uw[:, fo % 2, fi, :], x1bf[:, fi, tc * 512:(tc + 1) * 512],
                         start=(fi == 0), stop=(fi == 15),
                         waits=(uw0 if fi == 0 else ()), inc=(T if fi == 15 else None))
                t_u = T.v
                p.emit("scalar", lambda e, k=k: e.activation(out=sg[:, :], in_=GP[k % 2][:, :],
                                                             func=AF.Silu),
                       waits=[(T, t_g), (V, v_h.get(k - 1, 0))], inc=C)
                c_sg[k] = C.v
                hw = [(C, c_sg[k]), (T, t_u), (Shw[k % 2], Shw[k % 2].v)]
                p.emit("vector",
                       lambda e, k=k: e.tensor_tensor(hc[:, k % 2, :], sg[:, :], UP[k % 2][:, :],
                                                      ALU.mult),
                       waits=hw, inc=V)
                v_h[k] = V.v
                p.dma("sync", Shw[k % 2],
                      HSCR[fo * 128:(fo + 1) * 128, tc * 512:(tc + 1) * 512],
                      hc[:, k % 2, :], waits=[(V, V.v)])
            t_gu_end[fo] = T.v
        h_done = [(Shw[0], Shw[0].v), (Shw[1], Shw[1].v)]
        t_gu_all = T.v

        t_dn_end = {}
        t_stat = {}
        v_y = {}
        v_lnread = 0
        widx = 0
        for tc in range(4):
            hw2 = list(h_done) + ([(T, t_gu_all)] if tc == 0 else [(T, t_dn_end[(tc - 1) * 16 + 15])])
            p.dma("sync", Sht, ht[:, :, :],
                  HSCR[:, tc * 512:(tc + 1) * 512].rearrange("(c p) f -> p c f", p=128),
                  waits=hw2)
            g_ht = Sht.v
            for fo in range(16):
                wwait = [(T, t_gu_all)] if widx < 2 else [(T, t_dn_end[widx - 2])]
                p.dma("gpsimd", Sdw[widx % 2], dw[:, widx % 2, :, :],
                      DSCR[:, fo * 128:(fo + 1) * 128].rearrange("(c p) f -> p c f", p=128),
                      waits=wwait)
                g_dw = Sdw[widx % 2].v
                xw = [(V, v_y[widx - 2])] if widx >= 2 else []
                p.dma("sync", Sxr[widx % 2], xres[:, widx % 2, :], X1Tv[fo * 128:(fo + 1) * 128, tc, :],
                      waits=xw)
                sy_x = Sxr[widx % 2].v
                mw = [(Sdw[widx % 2], g_dw), (Sht, g_ht)]
                if widx >= 2:
                    mw.append((V, v_y[widx - 2]))
                for fi in range(NIC):
                    p.mm(PP[widx % 2][:, :], dw[:, widx % 2, fi, :], ht[:, fi, :],
                         start=(fi == 0), stop=(fi == NIC - 1),
                         waits=(mw if fi == 0 else ()), inc=(T if fi == NIC - 1 else None))
                t_dn_end[widx] = T.v
                p.emit("vector",
                       lambda e, widx=widx, fo=fo:
                       e.scalar_tensor_tensor(ybuf[:, fo, :], PP[widx % 2][:, :], 1.0,
                                              xres[:, widx % 2, :], ALU.mult, ALU.add),
                       waits=[(T, t_dn_end[widx]), (Sxr[widx % 2], sy_x)], inc=V)
                v_y[widx] = V.v
                sqw = [(V, v_y[widx])]
                if widx >= 2:
                    sqw.append((T, t_stat[widx - 2]))
                p.emit("scalar",
                       lambda e, widx=widx, fo=fo: e.activation(out=ysq[:, widx % 2, :],
                                                                in_=ybuf[:, fo, :],
                                                                func=AF.Square),
                       waits=sqw, inc=C)
                c_sq = C.v
                statw = [(V, v_y[widx])]
                if fo == 0:
                    statw.append((V, v_lnread))
                p.mm(SB0[0:1, :], ones128[:, :], ybuf[:, fo, :], start=(fo == 0), stop=(fo == 15),
                     waits=statw, inc=None)
                p.mm(SB1[0:1, :], ones128[:, :], ysq[:, widx % 2, :], start=(fo == 0), stop=(fo == 15),
                     waits=([(C, c_sq)] + ([(V, v_lnread)] if fo == 0 else [])), inc=T)
                t_stat[widx] = T.v
                widx += 1
            ln_rows(p, C, V, T, srow, var_i=2, rstd_i=3, m_i=0, ms_i=1, mr_i=4,
                    statY=SB0[0:1, :], statY2=SB1[0:1, :], ntok_inv=1.0 / 2048.0,
                    waits_stats=[(T, T.v)])
            v_lnread = V.v
            v_rows = V.v
            p.mm(GP[0][:, :], ones1[0:1, :], srow[0:1, 3, :], start=True, stop=True,
                 waits=[(V, v_rows)], inc=T)
            p.mm(GP[1][:, :], ones1[0:1, :], srow[0:1, 4, :], start=True, stop=True, inc=T)
            t_bc2 = T.v
            p.emit("vector", lambda e: e.tensor_copy(bcast[:, 0, :], GP[0][:, :]),
                   waits=[(T, t_bc2)], inc=V)
            p.emit("vector", lambda e: e.tensor_copy(bcast[:, 1, :], GP[1][:, :]), inc=V)
            v_bc = V.v
            for fo in range(16):
                oidx = tc * 16 + fo
                ow = [(V, v_bc), (So[oidx % 2], So[oidx % 2].v)]
                p.emit("vector",
                       lambda e, fo=fo:
                       e.scalar_tensor_tensor(x2c[:, fo % 2, :], ybuf[:, fo, :], 1.0,
                                              bcast[:, 0, :], ALU.mult, ALU.mult),
                       waits=ow, inc=V)
                p.emit("vector",
                       lambda e, fo=fo: e.tensor_tensor(x2c[:, fo % 2, :], x2c[:, fo % 2, :],
                                                        bcast[:, 1, :], ALU.subtract),
                       waits=[(V, V.v)], inc=V)
                p.dma("sync", So[oidx % 2], X2Tv[fo * 128:(fo + 1) * 128, tc, :], x2c[:, fo % 2, :],
                      waits=[(V, V.v)])
    return p




def emit_l1_attn(p, nc, X2T, X2FULLT, WQ, WK, WV, WOL, LQ1, LK1, LQ2, LK2, SUBW,
                 MASKS, MCOL, META, IDENT, X3T, X3TOK, KSCR, VSCR, EXCH=None):
    p.begin_part()
    LAMBDA_INIT = 0.8 - 0.6 * float(np.exp(-0.3 * 12))
    SCALE = float(128 ** -0.5)
    with ExitStack() as ctx:
        sb = lambda *a: ctx.enter_context(nc.sbuf_tensor(*a, allow_name_mangling=True))
        psum = lambda *a: ctx.enter_context(nc.psum_tensor(*a, allow_name_mangling=True))
        sem = lambda n: p.S(n)
        arena = sb("arena", [128, 32768], BF16)
        ar = arena.ap()
        x2bf = ar.rearrange("p (c f) -> p c f", c=16)
        aot = ar.rearrange("p (c f) -> p c f", c=16)
        arena2 = sb("arena2", [128, 32768], BF16)
        ar2 = arena2.ap()
        qt = ar2.rearrange("p (c f) -> p c f", c=16)
        ybuf = ar2[:, 0:16384].bitcast(F32).rearrange("p (c f) -> p c f", c=16)
        x3ts = ar2[:, 16384:32768].bitcast(F32).rearrange("p (b f) -> p b f", b=4)
        wstr = sb("wstr", [128, 2, 16, 128], BF16)
        kvarena = sb("kvarena", [128, 12288], BF16)
        kva = kvarena.ap()
        vw = kva[:, 0:8192].rearrange("p (c f) -> p c f", c=16)
        kt = kva[:, 0:4096].rearrange("p (c f) -> p c f", c=1)
        vs = kva[:, 4096:12288].rearrange("p (c f) -> p c f", c=32)
        evac = sb("evac", [128, 2, 512], BF16)
        masks = sb("masks", [128, 4, 512], BF16)
        mcol = sb("mcol", [128, 8], F32)
        pt = sb("pt", [128, 2, 512], BF16)
        cf32 = sb("cf32", [128, 2, 4, 512], F32)
        sqb2 = sb("sqb2", [128, 2, 512], F32)
        lam = sb("lam", [128, 4], F32)
        lrows = sb("lrows", [1, 8], F32)
        subcol = sb("subcol", [128, 2], F32)
        ones1 = sb("ones1", [1, 128], F32)
        ones128 = sb("ones128", [128, 1], F32)
        onesbf = sb("onesbf", [128, 1], BF16)
        ident_s = sb("ident_s", [128, 128], F32)
        xres = sb("xres", [128, 2, 512], F32)
        ysq = sqb2.ap()
        srow = sb("srow", [1, 5, 512], F32)
        crec = srow.ap()[:, 1:5, :].rearrange("a (p b) q -> a p b q", p=2)
        bcast = masks.ap().rearrange("p j q -> p (j q)").bitcast(F32).rearrange("p (c f) -> p c f", c=2)
        x3c = sb("x3c", [128, 2, 512], F32)
        meta_s = sb("meta_s", [1, 64], U32)
        PS = [psum("PS0", [128, 512], F32), psum("PS1", [128, 512], F32)]
        OAp = psum("OAp", [128, 512], F32)
        OBp = psum("OBp", [128, 512], F32)
        ODp = psum("ODp", [128, 512], F32)
        CB0 = psum("CB0", [128, 512], F32)
        CB1 = psum("CB1", [128, 512], F32)
        CB2 = psum("CB2", [128, 512], F32)
        OD = [ODp, CB2]
        T = sem("sT"); C = sem("sC"); V = sem("sV")
        Spre = sem("dPre"); SpreG = sem("dPreG"); Sx2 = sem("dX2")
        Sw = [sem("dW0"), sem("dW1")]
        Svw = sem("dVw")
        Skw = [sem("dKw0"), sem("dKw1")]
        Svv = [sem("dVv0"), sem("dVv1")]
        Skt = sem("dKt")
        Svs = sem("dVs")
        Sxr = [sem("dXr0"), sem("dXr1")]
        So = [sem("dO0"), sem("dO1")]
        Sot = sem("dOt")

        X2Tv = X2T.rearrange("d (b c) -> d b c", c=512)
        X3Tv = X3T.rearrange("d (b c) -> d b c", c=512)
        KSCRq = KSCR.rearrange("d (i q c) -> d i q c", q=2, c=512)
        VSCRq = VSCR.rearrange("(i q b c) f -> i q b c f", q=2, b=4, c=128)


        p.emit("vector", lambda e: e.memset(ones1[:], 1.0), inc=V)
        p.emit("vector", lambda e: e.memset(ones128[:], 1.0), inc=V)
        p.emit("vector", lambda e: e.memset(onesbf[:], 1.0), inc=V)
        v_ones = V.v
        p.dma("sync", Spre, meta_s[:], META)
        p.dma("sync", Spre, lam[:, 0:1], LQ1, waits=[(Spre, Spre.v)])
        p.dma("sync", Spre, lam[:, 1:2], LK1, waits=[(Spre, Spre.v)])
        p.dma("sync", Spre, lam[:, 2:3], LQ2, waits=[(Spre, Spre.v)])
        p.dma("sync", Spre, lam[:, 3:4], LK2, waits=[(Spre, Spre.v)])
        p.dma("sync", Spre, subcol[:, :], SUBW, waits=[(Spre, Spre.v)])
        spre = Spre.v
        p.dma("gpsimd", SpreG, ident_s[:], IDENT)
        p.dma("gpsimd", SpreG, masks[:, :, :], MASKS[:, 0:4, :])
        p.dma("sync", Spre, mcol[:, :], MCOL, waits=[(Spre, Spre.v)])
        spre = Spre.v
        spreg = SpreG.v
        p.emit("vector", lambda e: e.tensor_tensor(lam[:, 0:1], lam[:, 0:1], lam[:, 1:2], ALU.mult),
               waits=[(Spre, spre)], inc=V)
        p.emit("vector", lambda e: e.tensor_tensor(lam[:, 1:2], lam[:, 2:3], lam[:, 3:4], ALU.mult),
               waits=[(V, V.v)], inc=V)
        v_lamprod = V.v
        p.mm(OD[0][0:1, 0:2], ones128[:, 0:1], lam[:, 0:2], start=True, stop=True,
             waits=[(V, v_lamprod)], inc=T)
        p.emit("scalar", lambda e: e.activation(out=lrows[0:1, 0:2], in_=OD[0][0:1, 0:2],
                                                func=AF.Exp), waits=[(T, T.v)], inc=C)
        p.emit("vector", lambda e: e.tensor_tensor(lrows[0:1, 2:3], lrows[0:1, 0:1],
                                                   lrows[0:1, 1:2], ALU.subtract),
               waits=[(C, C.v)], inc=V)
        p.emit("vector", lambda e: e.tensor_scalar(lrows[0:1, 3:4], lrows[0:1, 2:3],
                                                   float(LAMBDA_INIT), -1.0,
                                                   op0=ALU.add, op1=ALU.mult),
               waits=[(V, V.v)], inc=V)
        v_nlam = V.v
        p.mm(OD[1][:, 0:1], ones1[0:1, :], lrows[0:1, 3:4], start=True, stop=True,
             waits=[(V, v_nlam)], inc=T)
        p.emit("vector", lambda e: e.tensor_copy(lam[:, 2:3], OD[1][:, 0:1]),
               waits=[(T, T.v)], inc=V)
        p.emit("vector", lambda e: e.tensor_scalar_mul(subcol[:, :], subcol[:, :],
                                                       float(1.0 - LAMBDA_INIT)),
               waits=[(Spre, spre), (V, V.v)], inc=V)
        v_const = V.v

        HOLD = {}
        if EXCH is not None:
            Sex = sem("dEx")
            Sfl = sem("dFl")
            Ssp = sem("dSp")
            flag_in = sb("flag_in", [1, 16], U32)
            shflagv = EXCH["shflag"].rearrange("(a one) f -> a one f", one=1)

            def _push(e):
                HOLD["po"] = e.value_load(meta_s[0:1, 40:41])
                HOLD["pa"] = e.value_load(meta_s[0:1, 41:42])
                HOLD["nonce"] = e.value_load(meta_s[0:1, 42:43])
                return e.dma_start(out=EXCH["shx2"][HOLD["po"]], in_=X2T)
            p.emit("gpsimd", _push, waits=[(Spre, spre)], inc=Sex, incn=16)
            p.emit("gpsimd",
                   lambda e: e.dma_start(out=shflagv[HOLD["po"]][:, 0:1],
                                         in_=meta_s[0:1, 42:43]),
                   waits=[(Sex, Sex.v)], inc=Sfl, incn=16)

            def _spin(e):
                if EXCH.get("stub"):
                    return None
                eng = [e.engine]
                loop_id = nc.next_id()
                ln_, dn_ = f"spin_{loop_id}", f"spin_{loop_id}_done"
                cnt = nc.alloc_registers(f"cnt_{loop_id}", engines=eng)
                nc.regs_mov(cnt, Ssp.v)
                nc.br(ln_, engines=eng)
                with nc.body(ln_, valid_engines=eng):
                    nc.regs_alu(cnt, cnt, 16, op=ALU.add)
                    e.dma_start(out=flag_in[:, 0:1], in_=shflagv[HOLD["pa"]][:, 0:1]).then_inc(Ssp.sem, 16)
                    e.wait_ge(Ssp.sem, nc.snap(cnt, engines=eng))
                    v = e.value_load(flag_in[0:1, 0:1])
                    nc.br_cmp(v, HOLD["nonce"], on_true=dn_, on_false=ln_,
                              comp_op="IS_EQ", engines=eng)
                nc.switch_bb(dn_)
                return None

        t_pj = {}
        v_pj = {}
        v_vev = {}
        gidx = 0
        kev = 0
        vev = 0
        t_half_end = 0
        for half in range(2):
            if EXCH is not None and half == 1:
                p.emit("gpsimd", _spin, waits=[(Sfl, Sfl.v)])
            for fi in range(16):
                hw0 = [(T, t_half_end)] if (half == 1 and fi == 0) else []
                if EXCH is None:
                    p.dma("gpsimd", Sx2, x2bf[:, fi, :],
                          X2FULLT[fi * 128:(fi + 1) * 128, half * 2048:(half + 1) * 2048],
                          waits=hw0)
                elif half == 0:
                    p.dma("gpsimd", Sx2, x2bf[:, fi, :],
                          X2T[fi * 128:(fi + 1) * 128, :], waits=hw0)
                else:
                    key = "po" if EXCH.get("stub") else "pa"
                    p.emit("gpsimd",
                           lambda e, fi=fi, key=key:
                           e.dma_start(out=x2bf[:, fi, :],
                                       in_=EXCH["shx2"][HOLD[key]][fi * 128:(fi + 1) * 128, :]),
                           waits=hw0, inc=Sx2, incn=16)
            g_x2 = Sx2.v
            wlist = [("k", WK)] + ([("q", WQ)] if half == 0 else [])
            v_guard = V.v
            kseg = 0
            for wname, W in wlist:
                for fo in range(16):
                    wwait = [(T, t_pj[gidx - 2])] if gidx >= 2 else []
                    p.dma("gpsimd", Sw[gidx % 2], wstr[:, gidx % 2, :, :],
                          W[:, fo * 128:(fo + 1) * 128].rearrange("(c p) f -> p c f", p=128),
                          waits=wwait)
                    g_w = Sw[gidx % 2].v
                    for tc in range(4):
                        k = gidx * 4 + tc
                        mw = [(Sw[gidx % 2], g_w), (Sx2, g_x2)]
                        if kseg < 2:
                            mw.append((V, v_guard))
                        elif k - 2 in v_pj:
                            mw.append((V, v_pj[k - 2]))
                        kseg += 1
                        for fi in range(16):
                            p.mm(PS[k % 2][:, :], wstr[:, gidx % 2, fi, :],
                                 x2bf[:, fi, tc * 512:(tc + 1) * 512],
                                 start=(fi == 0), stop=(fi == 15),
                                 waits=(mw if fi == 0 else ()), inc=(T if fi == 15 else None))
                        t_mm = T.v
                        if wname == "q":
                            p.emit("vector",
                                   lambda e, k=k, fo=fo, tc=tc:
                                   e.tensor_copy(qt[:, fo, tc * 512:(tc + 1) * 512], PS[k % 2][:, :]),
                                   waits=[(T, t_mm)], inc=V)
                        else:
                            ej = kev % 2
                            p.emit("vector",
                                   lambda e, k=k, ej=ej: e.tensor_copy(evac[:, ej, :], PS[k % 2][:, :]),
                                   waits=[(T, t_mm), (Skw[ej], Skw[ej].v), (Svv[ej], Svv[ej].v)], inc=V)
                            p.emit("gpsimd",
                                   lambda e, ej=ej, fo=fo, half=half, tc=tc:
                                   e.dma_start(out=KSCRq[fo * 128:(fo + 1) * 128, tc, half, :],
                                               in_=evac[:, ej, :]),
                                   waits=[(V, V.v)], inc=Skw[ej], incn=16)
                            kev += 1
                        v_pj[k] = V.v
                    t_pj[gidx] = T.v
                    gidx += 1
            v_guard2 = V.v
            vseg = 0
            for fc in range(4):
                p.dma("gpsimd", Svw, vw[:, :, :],
                      WV[:, fc * 512:(fc + 1) * 512].rearrange("(c p) f -> p c f", p=128),
                      waits=[(T, t_pj[gidx - 1])] if (half, fc) != (0, 0) else [])
                g_vw = Svw.v
                for tb in range(16):
                    k = gidx * 4 + tb
                    mw = [(Svw, g_vw), (Sx2, g_x2)]
                    if vseg < 2:
                        mw.append((V, v_guard2))
                    elif vev >= 2:
                        mw.append((V, v_vev[vev - 2]))
                    vseg += 1
                    for fi in range(16):
                        p.mm(PS[(vev) % 2][:, :], x2bf[:, fi, tb * 128:(tb + 1) * 128],
                             vw[:, fi, :],
                             start=(fi == 0), stop=(fi == 15),
                             waits=(mw if fi == 0 else ()), inc=(T if fi == 15 else None))
                    t_mm = T.v
                    ej = vev % 2
                    evw = [(T, t_mm), (Svv[vev % 2], Svv[vev % 2].v), (Skw[ej], Skw[ej].v)]
                    p.emit("vector",
                           lambda e, ej=ej, vev=vev: e.tensor_copy(evac[:, ej, :], PS[vev % 2][:, :]),
                           waits=evw, inc=V)
                    v_vev[vev] = V.v
                    p.emit("gpsimd",
                           lambda e, ej=ej, fc=fc, half=half, tb=tb:
                           e.dma_start(out=VSCRq[tb // 4, half, tb % 4, :, fc * 512:(fc + 1) * 512],
                                       in_=evac[:, ej, :]),
                           waits=[(V, V.v)], inc=Svv[vev % 2], incn=16)
                    vev += 1
                t_pj[gidx] = T.v
                gidx += 1
            t_half_end = T.v
        kv_done = [(Skw[0], Skw[0].v), (Skw[1], Skw[1].v), (Svv[0], Svv[0].v), (Svv[1], Svv[1].v)]
        t_p1_end = T.v

        si = 0
        c_exp = {}
        v_mask = {}
        t_pv = {}
        v_comb = {}
        v_stage0 = {}
        v_part1 = {}
        c_rstd_d = {}
        t_ssum_d = {}
        t_h_end = {}

        def _c_part1(bb):
            pw = [(V, v_stage0[(bb, 1)])]
            if bb - 1 in v_comb:
                pw.append((V, v_comb[bb - 1]))
            p.mm(CB0[:, :], ones1[0:1, :], crec[0:1, bb % 2, 0, :], start=True, stop=True,
                 waits=pw, inc=T)
            p.mm(CB1[:, :], ones1[0:1, :], crec[0:1, bb % 2, 1, :], start=True, stop=True, inc=T)
            t_bc = T.v
            for j in range(4):
                p.emit("vector",
                       lambda e, bb=bb, j=j: e.tensor_tensor(cf32[:, bb % 2, j, :],
                                                             cf32[:, bb % 2, j, :],
                                                             (CB0 if j < 2 else CB1)[:, :], ALU.mult),
                       waits=[(T, t_bc), (V, V.v)], inc=V)
            for j in range(2):
                p.emit("vector",
                       lambda e, bb=bb, j=j:
                       e.scalar_tensor_tensor(cf32[:, bb % 2, j, :], cf32[:, bb % 2, 2 + j, :],
                                              lam[:, 2:3], cf32[:, bb % 2, j, :],
                                              ALU.mult, ALU.add),
                       waits=[(V, V.v)], inc=V)
            v_part1[bb] = V.v
            sqw = [(V, v_part1[bb])]
            if bb - 1 in t_ssum_d:
                sqw.append((T, t_ssum_d[bb - 1]))
            p.emit("scalar", lambda e, bb=bb: e.activation(out=sqb2[:, 0, :],
                                                           in_=cf32[:, bb % 2, 0, :],
                                                           func=AF.Square), waits=sqw, inc=C)
            p.emit("scalar", lambda e, bb=bb: e.activation(out=sqb2[:, 1, :],
                                                           in_=cf32[:, bb % 2, 1, :],
                                                           func=AF.Square), inc=C)
            c_rstd_d[bb] = C.v

        def _c_part2(bb):
            ssw = [(C, c_rstd_d[bb])]
            if bb - 1 in c_rstd_d:
                pass
            p.mm(CB2[0:1, :], ones128[:, 0:1], sqb2[:, 0, :], start=True, stop=False,
                 waits=ssw, inc=None)
            p.mm(CB2[0:1, :], ones128[:, 0:1], sqb2[:, 1, :], start=False, stop=True, inc=T)
            t_ssum_d[bb] = T.v
            p.emit("scalar", lambda e: e.activation(out=srow[0:1, 0, :], in_=CB2[0:1, :],
                                                    func=AF.Ln, bias=EPS, scale=f32(1.0 / 256.0)),
                   waits=[(T, t_ssum_d[bb]), (C, C.v)], inc=C)
            p.emit("scalar", lambda e: e.activation(out=srow[0:1, 0, :], in_=srow[0:1, 0, :],
                                                    func=AF.Exp, scale=f32(-0.5)),
                   waits=[(C, C.v)], inc=C)
            c_r = C.v
            p.mm(CB0[:, :], ones1[0:1, :], srow[0:1, 0, :], start=True, stop=True,
                 waits=[(C, c_r), (V, v_part1[bb])], inc=T)
            t_bcr = T.v
            for j in range(2):
                p.emit("vector",
                       lambda e, bb=bb, j=j: e.tensor_tensor(cf32[:, bb % 2, j, :],
                                                             cf32[:, bb % 2, j, :],
                                                             CB0[:, :], ALU.mult),
                       waits=[(T, t_bcr), (V, V.v)], inc=V)
                p.emit("vector",
                       lambda e, bb=bb, j=j:
                       e.tensor_scalar(aot[:, 2 * (bb // 4) + j, (bb % 4) * 512:(bb % 4) * 512 + 512],
                                       cf32[:, bb % 2, j, :], subcol[:, j:j + 1], None,
                                       op0=ALU.mult),
                       waits=[(V, V.v)], inc=V)
            v_comb[bb] = V.v
        for h in range(8):
            vsw = list(kv_done)
            if h >= 1:
                vsw.append((T, t_h_end[h - 1]))
            p.dma("sync", Svs, vs[:, :, :],
                  VSCR[:, h * 256:(h + 1) * 256].rearrange("(c p) f -> p c f", p=128),
                  waits=vsw)
            s_vs = Svs.v

            for qc in range(4):
                nkc = 8 * qc + 8
                aa = h * 4 + qc
                for br in range(2):
                    ktw = [(T, t_pv[si - 1])] if si >= 1 else []
                    p.dma("sync", Skt, kt[:, 0, :],
                          KSCR[(2 * h + br) * 128:(2 * h + br + 1) * 128, :], waits=ktw)
                    s_kt = Skt.v
                    for kc2 in range(0, nkc, 2):
                        t_sts = {}
                        for o in (0, 1):
                            kc = kc2 + o
                            s2 = si + o
                            stw = [(C, c_exp[s2 - 2])] if s2 - 2 in c_exp else []
                            if kc == 0:
                                stw += [(Skt, s_kt), (Svs, s_vs), (SpreG, spreg), (Spre, spre)]
                                prevk = (aa, br - 1) if br == 1 else (aa - 1, 1)
                                if prevk in v_stage0:
                                    stw.append((V, v_stage0[prevk]))
                            p.mm(PS[s2 % 2][:, :], kt[:, 0, kc * 128:(kc + 1) * 128],
                                 qt[:, 2 * h + br, qc * 512:(qc + 1) * 512],
                                 start=True, stop=True, waits=stw, inc=T)
                            t_sts[o] = T.v
                        vm = {}
                        for o in (0, 1):
                            kc = kc2 + o
                            s2 = si + o
                            expw = [(T, t_sts[o])]
                            if s2 - 2 in t_pv:
                                expw.append((T, t_pv[s2 - 2]))
                            p.emit("scalar",
                                   lambda e, s2=s2: e.activation(out=pt[:, s2 % 2, :],
                                                                 in_=PS[s2 % 2][:, :],
                                                                 func=AF.Exp, scale=f32(SCALE)),
                                   waits=expw, inc=C)
                            c_exp[s2] = C.v
                            if kc >= 8 * qc:
                                j = kc - 8 * qc
                                if j < 4:
                                    p.emit("vector",
                                           lambda e, s2=s2, j=j:
                                           e.tensor_tensor(pt[:, s2 % 2, :], pt[:, s2 % 2, :],
                                                           masks[:, j, :], ALU.mult),
                                           waits=[(C, c_exp[s2]), (V, V.v)], inc=V)
                                else:
                                    p.emit("vector",
                                           lambda e, s2=s2, j=j:
                                           e.tensor_scalar_mul(pt[:, s2 % 2, :], pt[:, s2 % 2, :],
                                                               mcol[:, j:j + 1]),
                                           waits=[(C, c_exp[s2]), (V, V.v)], inc=V)
                                vm[o] = V.v
                        for o in (0, 1):
                            kc = kc2 + o
                            s2 = si + o
                            pvw0 = [(V, vm[o])] if o in vm else [(C, c_exp[s2])]
                            p.mm(OAp[:, :], vs[:, kc, 0:128], pt[:, s2 % 2, :],
                                 start=(kc == 0), stop=(kc == nkc - 1), waits=pvw0, inc=None)
                            p.mm(OBp[:, :], vs[:, kc, 128:256], pt[:, s2 % 2, :],
                                 start=(kc == 0), stop=(kc == nkc - 1), inc=None)
                            p.mm(ODp[0:1, :], onesbf[:, 0:1], pt[:, s2 % 2, :],
                                 start=(kc == 0), stop=(kc == nkc - 1), inc=T)
                            t_pv[s2] = T.v
                        si += 2
                    p.emit("vector",
                           lambda e, aa=aa, br=br: e.tensor_copy(cf32[:, aa % 2, 2 * br, :], OAp[:, :]),
                           waits=[(T, t_pv[si - 1])], inc=V)
                    p.emit("vector",
                           lambda e, aa=aa, br=br: e.tensor_copy(cf32[:, aa % 2, 2 * br + 1, :], OBp[:, :]),
                           inc=V)
                    p.emit("vector",
                           lambda e, aa=aa, br=br: e.reciprocal(crec[0:1, aa % 2, br, :], ODp[0:1, :]),
                           inc=V)
                    v_stage0[(aa, br)] = V.v
                    if br == 0 and aa >= 1:
                        _c_part1(aa - 1)
                if aa >= 1:
                    _c_part2(aa - 1)
            t_h_end[h] = T.v
        _c_part1(31)
        _c_part2(31)
        v_aot = V.v

        t_wo_end = {}
        t_stat = {}
        v_y = {}
        t_tr = {}
        v_lnread = 0
        widx = 0
        for tc in range(4):
            for fo in range(16):
                wwait = [(T, t_p1_end)] if widx < 2 else [(T, t_wo_end[widx - 2])]
                p.dma("gpsimd", Sw[widx % 2], wstr[:, widx % 2, :, :],
                      WOL[:, fo * 128:(fo + 1) * 128].rearrange("(c p) f -> p c f", p=128),
                      waits=wwait)
                g_wo = Sw[widx % 2].v
                xw = [(V, v_y[widx - 2])] if widx >= 2 else []
                p.dma("sync", Sxr[widx % 2], xres[:, widx % 2, :], X2Tv[fo * 128:(fo + 1) * 128, tc, :],
                      waits=xw)
                sy_x = Sxr[widx % 2].v
                mw = [(Sw[widx % 2], g_wo), (V, v_aot)]
                if widx >= 2:
                    mw.append((V, v_y[widx - 2]))
                for fi in range(16):
                    p.mm(PS[widx % 2][:, :], wstr[:, widx % 2, fi, :],
                         aot[:, fi, tc * 512:(tc + 1) * 512],
                         start=(fi == 0), stop=(fi == 15),
                         waits=(mw if fi == 0 else ()), inc=(T if fi == 15 else None))
                t_wo_end[widx] = T.v
                p.emit("vector",
                       lambda e, widx=widx, fo=fo:
                       e.scalar_tensor_tensor(ybuf[:, fo, :], PS[widx % 2][:, :], 1.0,
                                              xres[:, widx % 2, :], ALU.mult, ALU.add),
                       waits=[(T, t_wo_end[widx]), (Sxr[widx % 2], sy_x)], inc=V)
                v_y[widx] = V.v
                sqw = [(V, v_y[widx])]
                if widx >= 1:
                    sqw.append((T, t_stat[widx - 1]))
                p.emit("scalar",
                       lambda e, widx=widx, fo=fo: e.activation(out=ysq[:, 0, :],
                                                                in_=ybuf[:, fo, :],
                                                                func=AF.Square),
                       waits=sqw, inc=C)
                c_sq2 = C.v
                statw = [(V, v_y[widx])]
                if fo == 0:
                    statw.append((V, v_lnread))
                p.mm(OAp[0:1, :], ones128[:, 0:1], ybuf[:, fo, :], start=(fo == 0), stop=(fo == 15),
                     waits=statw, inc=None)
                p.mm(OBp[0:1, :], ones128[:, 0:1], ysq[:, 0, :], start=(fo == 0), stop=(fo == 15),
                     waits=([(C, c_sq2)] + ([(V, v_lnread)] if fo == 0 else [])), inc=T)
                t_stat[widx] = T.v
                widx += 1
            ln_rows(p, C, V, T, srow, var_i=2, rstd_i=3, m_i=0, ms_i=1, mr_i=2,
                    statY=OAp[0:1, :], statY2=OBp[0:1, :], ntok_inv=1.0 / 2048.0,
                    waits_stats=[(T, T.v)])
            v_lnread = V.v
            v_rows = V.v
            p.mm(OD[0][:, :], ones1[0:1, :], srow[0:1, 3, :], start=True, stop=True,
                 waits=[(V, v_rows)], inc=T)
            p.mm(OD[1][:, :], ones1[0:1, :], srow[0:1, 2, :], start=True, stop=True, inc=T)
            t_bc2 = T.v
            p.emit("vector", lambda e: e.tensor_copy(bcast[:, 0, 0:512], OD[0][:, :]),
                   waits=[(T, t_bc2)], inc=V)
            p.emit("vector", lambda e: e.tensor_copy(bcast[:, 1, 0:512], OD[1][:, :]), inc=V)
            v_bc = V.v
            sot_prev = Sot.v
            for fo in range(16):
                oidx = tc * 16 + fo
                ow = [(V, v_bc), (So[oidx % 2], So[oidx % 2].v)]
                if oidx - 2 in t_tr:
                    ow.append((T, t_tr[oidx - 2]))
                p.emit("vector",
                       lambda e, fo=fo:
                       e.scalar_tensor_tensor(x3c[:, fo % 2, :], ybuf[:, fo, :], 1.0,
                                              bcast[:, 0, 0:512], ALU.mult, ALU.mult),
                       waits=ow, inc=V)
                p.emit("vector",
                       lambda e, fo=fo: e.tensor_tensor(x3c[:, fo % 2, :], x3c[:, fo % 2, :],
                                                        bcast[:, 1, 0:512], ALU.subtract),
                       waits=[(V, V.v)], inc=V)
                v_x3 = V.v
                p.dma("sync", So[oidx % 2], X3Tv[fo * 128:(fo + 1) * 128, tc, :], x3c[:, fo % 2, :],
                      waits=[(V, v_x3)])
                for tb in range(4):
                    trw = [(V, v_x3), (SpreG, spreg)]
                    if (oidx, tb) != (0, 0):
                        trw.append((V, V.v))
                    p.emit("tensor",
                           lambda e, fo=fo, tb=tb:
                           e.transpose(CB0[:, 0:128].bitcast(F32) if tb % 2 == 0 else CB1[:, 0:128].bitcast(F32),
                                       x3c[:, fo % 2, tb * 128:(tb + 1) * 128], ident_s[:]),
                           waits=trw, inc=T)
                    p.emit("vector",
                           lambda e, fo=fo, tb=tb:
                           e.tensor_copy(x3ts[:, tb, fo * 128:(fo + 1) * 128],
                                         (CB0[:, 0:128] if tb % 2 == 0 else CB1[:, 0:128])),
                           waits=[(T, T.v), (Sot, sot_prev)], inc=V)
                t_tr[oidx] = T.v
            v_fin = V.v
            for tb in range(4):
                p.dma("sync", Sot, X3TOK[tc * 512 + tb * 128: tc * 512 + (tb + 1) * 128, :],
                      x3ts[:, tb, :], waits=[(V, v_fin)])
    return p




def emit_l1_moe(p, nc, X3T, X3TOK, RWT, RB, EWT, EB, OUT, ASCR):
    p.begin_part()
    BIG = 60000.0
    with ExitStack() as ctx:
        sb = lambda *a: ctx.enter_context(nc.sbuf_tensor(*a, allow_name_mangling=True))
        psum = lambda *a: ctx.enter_context(nc.psum_tensor(*a, allow_name_mangling=True))
        sem = lambda n: p.S(n)
        arena = sb("arena", [128, 32768], BF16)
        x3bf = arena.ap().rearrange("p (c f) -> p c f", c=16)
        arena2 = sb("arena2", [128, 16384], BF16)
        lg = arena2.ap().bitcast(F32).rearrange("p (c f) -> p c f", c=16)
        acc = lg
        rw = sb("rw", [128, 2, 16, 512], BF16)
        ew = rw
        farena = sb("farena", [128, 28672], BF16)
        far = farena.ap()
        rbb = far[:, 0:16384].rearrange("p (c f) -> p c f", c=32)
        t2 = far[:, 16384:26624].bitcast(F32).rearrange("p (c f) -> p c f", c=5)
        cbuf = sb("cbuf", [128, 16, 8], F32)
        chalf = sb("chalf", [128, 2, 8], F32)
        rbrow = sb("rbrow", [1, 2, 512], F32)
        ebrow = sb("ebrow", [1, 2, 512], F32)
        ones1 = sb("ones1", [1, 128], F32)
        yb = far[:, 0:8192].bitcast(F32).rearrange("p (c f) -> p c f", c=2)
        xtc = far[:, 8192:16384].bitcast(F32).rearrange("p (c f) -> p c f", c=2)
        outc = far[:, 16384:24576].bitcast(F32).rearrange("p (c f) -> p c f", c=2)
        sqf = far[:, 24576:28672].bitcast(F32)
        cols = sb("cols", [128, 8], F32)
        LG = [psum("LG0", [128, 512], F32), psum("LG1", [128, 512], F32),
              psum("LG2", [128, 512], F32), psum("LG3", [128, 512], F32)]
        RBP = psum("RBP", [128, 512], F32)
        T = sem("sT"); C = sem("sC"); V = sem("sV")
        Sx3 = sem("dX3"); Srw = [sem("dRw0"), sem("dRw1")]
        Srb = [sem("dRb0"), sem("dRb1")]
        Seb = [sem("dEb0"), sem("dEb1")]
        Sac = [sem("dAc0"), sem("dAc1")]
        Say = sem("dAy"); Syb = [sem("dYb0"), sem("dYb1")]
        Sxt = [sem("dXt0"), sem("dXt1")]
        So = [sem("dO0"), sem("dO1")]

        p.emit("vector", lambda e: e.memset(ones1[:], 1.0), inc=V)
        v_ones = V.v
        for fi in range(16):
            p.dma("gpsimd", Sx3, x3bf[:, fi, :], X3T[fi * 128:(fi + 1) * 128, :])
        g_x3 = Sx3.v
        t_rbb = {}
        v_rbb = {}
        for rc in range(32):
            p.dma("sync", Srb[rc % 2], rbrow[:, rc % 2, :], RB[:, rc * 512:(rc + 1) * 512],
                  waits=([(T, t_rbb[rc - 2])] if rc >= 2 else []))
            s_rb = Srb[rc % 2].v
            p.mm(RBP[:, :], ones1[0:1, :], rbrow[0:1, rc % 2, :], start=True, stop=True,
                 waits=[(Srb[rc % 2], s_rb), (V, v_ones)] + ([(V, v_rbb[rc - 1])] if rc >= 1 else []),
                 inc=T)
            t_rbb[rc] = T.v
            p.emit("vector", lambda e, rc=rc: e.tensor_copy(rbb[:, rc, :], RBP[:, :]),
                   waits=[(T, t_rbb[rc])], inc=V)
            v_rbb[rc] = V.v
        v_rbb_all = V.v

        pend_top2 = []

        def _top2_body(lgv, tcb, g4, v_half, VE):
            m1 = t2[:, 0, 0:512]
            m2 = t2[:, 1, 0:512]
            tmp = t2[:, 2, 0:512]
            w1 = t2[:, 3, 0:512]
            dd = t2[:, 4, 0:512]
            VE(lambda e: e.tensor_tensor(m1, lgv[:, :, 0], lgv[:, :, 1], ALU.max),
               w=[(V, v_half), (V, V.v)])
            for e8 in range(2, 8):
                VE(lambda e, e8=e8: e.tensor_tensor(m1, m1, lgv[:, :, e8], ALU.max))
            for e8 in range(8):
                VE(lambda e, e8=e8: e.tensor_tensor(tmp, lgv[:, :, e8], m1, ALU.is_equal))
                VE(lambda e, e8=e8: e.scalar_tensor_tensor(tmp, tmp, -BIG, lgv[:, :, e8],
                                                           ALU.mult, ALU.add))
                if e8 == 0:
                    VE(lambda e: e.tensor_copy(m2, tmp))
                else:
                    VE(lambda e: e.tensor_tensor(m2, m2, tmp, ALU.max))
            VE(lambda e: e.tensor_tensor(dd, m1, m2, ALU.subtract))
            p.emit("scalar", lambda e: e.activation(out=w1, in_=dd, func=AF.Sigmoid),
                   waits=[(V, V.v)], inc=C)
            c_w1 = C.v
            for e8 in range(8):
                VE(lambda e, e8=e8: e.scalar_tensor_tensor(
                    tmp, lgv[:, :, e8], 1.0, m2, ALU.mult, ALU.is_equal,
                    accum_out=chalf[:, 0, e8:e8 + 1]), w=[(V, V.v), (C, c_w1)])
                VE(lambda e, e8=e8: e.tensor_tensor(dd, lgv[:, :, e8], m1, ALU.is_equal))
                VE(lambda e: e.tensor_tensor(dd, dd, tmp, ALU.subtract))
                VE(lambda e, e8=e8: e.scalar_tensor_tensor(
                    tmp, dd, 1.0, w1, ALU.mult, ALU.mult,
                    accum_out=chalf[:, 1, e8:e8 + 1]))
                VE(lambda e, e8=e8: e.tensor_tensor(
                    chalf[:, 0, e8:e8 + 1], chalf[:, 0, e8:e8 + 1],
                    chalf[:, 1, e8:e8 + 1], ALU.add))
                if g4 == 0:
                    VE(lambda e, e8=e8, tcb=tcb: e.tensor_copy(
                        cbuf[:, tcb, e8:e8 + 1], chalf[:, 0, e8:e8 + 1]))
                else:
                    VE(lambda e, e8=e8, tcb=tcb: e.tensor_tensor(
                        cbuf[:, tcb, e8:e8 + 1], cbuf[:, tcb, e8:e8 + 1],
                        chalf[:, 0, e8:e8 + 1], ALU.add))

        t_lg = {}
        v_lg = {}
        v_top2 = {}
        pend_top2 = []
        ridx = 0
        for tcb in range(16):
            for g4 in range(2):
                pass
            for g4 in range(4):
                for rcl8 in range(8):
                    rc = g4 * 8 + rcl8
                    rcl = (g4 % 2) * 8 + rcl8
                    k = ridx
                    wwait = [(T, t_lg[k - 2])] if k >= 2 else []
                    p.dma("gpsimd", Srw[k % 2], rw[:, k % 2, :, :],
                          RWT[:, rc * 512:(rc + 1) * 512].rearrange("(c p) f -> p c f", p=128),
                          waits=wwait)
                    g_rw = Srw[k % 2].v
                    mw = [(Srw[k % 2], g_rw), (Sx3, g_x3)]
                    if k - 4 in v_lg:
                        mw.append((V, v_lg[k - 4]))
                    if rcl8 == 0:
                        gk = tcb * 4 + g4
                        if gk - 2 >= 0 and gk - 2 in v_top2:
                            mw.append((V, v_top2[gk - 2]))
                    for fi in range(16):
                        p.mm(LG[k % 4][:, :], x3bf[:, fi, tcb * 128:(tcb + 1) * 128],
                             rw[:, k % 2, fi, :],
                             start=(fi == 0), stop=(fi == 15),
                             waits=(mw if fi == 0 else ()), inc=(T if fi == 15 else None))
                    t_lg[k] = T.v
                    p.emit("vector",
                           lambda e, k=k, rcl=rcl, rc=rc:
                           e.scalar_tensor_tensor(lg[:, rcl, :], LG[k % 4][:, :], 1.0,
                                                  rbb[:, rc, :], ALU.mult, ALU.add),
                           waits=[(T, t_lg[k]), (V, v_rbb_all)], inc=V)
                    v_lg[k] = V.v
                    ridx += 1
                gk = tcb * 4 + g4
                lgv = lg.rearrange("p c f -> p (c f)")[:, (g4 % 2) * 4096:(g4 % 2) * 4096 + 4096]                         .rearrange("p (o e) -> p o e", e=8)
                v_grp = V.v

                def VE(fn, w=None):
                    p.emit("vector", fn, waits=([(V, V.v)] if w is None else w), inc=V)

                def _t2(lgv=lgv, tcb=tcb, g4=g4, v_grp=v_grp, gk=gk):
                    _top2_body(lgv, tcb, g4, v_grp, VE)
                    v_top2[gk] = V.v
                pend_top2.append(_t2)
                if len(pend_top2) > 1:
                    pend_top2.pop(0)()
        while pend_top2:
            pend_top2.pop(0)()
        v_c_all = V.v
        t_r_end = T.v

        t_eo = {}
        v_ac = {}
        eidx = 0
        for fc in range(4):
            for e8 in range(8):
                wwait = [(T, t_eo[eidx - 2])] if eidx >= 2 else [(V, v_c_all), (T, t_r_end), (Srw[0], Srw[0].v), (Srw[1], Srw[1].v)]
                p.dma("gpsimd", Srw[eidx % 2], rw[:, eidx % 2, :, :],
                      EWT[e8].rearrange("(c p) f -> p c f", p=128)[:, :, fc * 512:(fc + 1) * 512],
                      waits=wwait)
                g_ew = Srw[eidx % 2].v
                p.dma("sync", Seb[eidx % 2], ebrow[:, eidx % 2, :],
                      EB[e8:e8 + 1, fc * 512:(fc + 1) * 512],
                      waits=([(T, t_eo[eidx - 2])] if eidx >= 2 else []))
                s_eb = Seb[eidx % 2].v
                for tcb in range(16):
                    kk = eidx * 16 + tcb
                    mw = [(Srw[eidx % 2], g_ew), (Seb[eidx % 2], s_eb), (Sx3, g_x3)]
                    if kk - 4 in v_ac:
                        mw.append((V, v_ac[kk - 4]))
                    for fi in range(16):
                        p.mm(LG[kk % 4][:, :], x3bf[:, fi, tcb * 128:(tcb + 1) * 128],
                             rw[:, eidx % 2, fi, :],
                             start=(fi == 0), stop=False,
                             waits=(mw if fi == 0 else ()), inc=None)
                    p.mm(LG[kk % 4][:, :], ones1[0:1, :], ebrow[0:1, eidx % 2, :],
                         start=False, stop=True, inc=T)
                    t_mm = T.v
                    if e8 == 0:
                        p.emit("vector",
                               lambda e, kk=kk, tcb=tcb, e8=e8:
                               e.tensor_scalar(acc[:, tcb, :], LG[kk % 4][:, :],
                                               cbuf[:, tcb, e8:e8 + 1], None, op0=ALU.mult),
                               waits=[(T, t_mm), (V, v_c_all)] +
                                     ([(Say, Say.v)] if fc > 0 else []), inc=V)
                    else:
                        p.emit("vector",
                               lambda e, kk=kk, tcb=tcb, e8=e8:
                               e.scalar_tensor_tensor(acc[:, tcb, :], LG[kk % 4][:, :],
                                                      cbuf[:, tcb, e8:e8 + 1], acc[:, tcb, :],
                                                      ALU.mult, ALU.add),
                               waits=[(T, t_mm), (V, V.v)], inc=V)
                    v_ac[kk] = V.v
                t_eo[eidx] = T.v
                eidx += 1
            for tcb in range(16):
                p.dma("sync", Say, ASCR[tcb * 128:(tcb + 1) * 128, fc * 512:(fc + 1) * 512],
                      acc[:, tcb, :], waits=[(V, v_ac[(eidx - 1) * 16 + tcb])])
        a_done = Say.v

        v_fin = {}
        for tcb in range(16):
            j = tcb % 2
            p.dma("sync", Syb[j], yb[:, j, :], ASCR[tcb * 128:(tcb + 1) * 128, :],
                  waits=([(V, v_fin[tcb - 2])] if tcb >= 2 else [(Say, a_done)]))
            s_y = Syb[j].v
            p.dma("sync", Sxt[j], xtc[:, j, :], X3TOK[tcb * 128:(tcb + 1) * 128, :],
                  waits=([(V, v_fin[tcb - 2])] if tcb >= 2 else []))
            s_x = Sxt[j].v
            p.emit("vector",
                   lambda e, j=j: e.tensor_tensor(yb[:, j, :], yb[:, j, :], xtc[:, j, :], ALU.add),
                   waits=[(Syb[j], s_y), (Sxt[j], s_x), (V, V.v)], inc=V)
            v_y = V.v
            p.emit("vector", lambda e, j=j: e.reduce_sum(cols[:, 0:1], yb[:, j, :],
                                                         axis=mybir.AxisListType.X),
                   waits=[(V, v_y)], inc=V)
            p.emit("scalar", lambda e, j=j: e.activation(out=sqf[:, :], in_=yb[:, j, :],
                                                         func=AF.Square),
                   waits=[(V, v_y), (V, V.v)], inc=C)
            c_sq = C.v
            p.emit("vector", lambda e: e.reduce_sum(cols[:, 1:2], sqf[:, :],
                                                    axis=mybir.AxisListType.X),
                   waits=[(C, c_sq), (V, V.v)], inc=V)
            p.emit("vector", lambda e: e.tensor_scalar_mul(cols[:, 2:3], cols[:, 0:1], 1.0 / 2048.0),
                   waits=[(V, V.v)], inc=V)
            p.emit("vector", lambda e: e.tensor_scalar_mul(cols[:, 3:4], cols[:, 1:2], 1.0 / 2048.0),
                   waits=[(V, V.v)], inc=V)
            p.emit("vector", lambda e: e.tensor_tensor(cols[:, 4:5], cols[:, 2:3], cols[:, 2:3],
                                                       ALU.mult), waits=[(V, V.v)], inc=V)
            p.emit("vector", lambda e: e.tensor_tensor(cols[:, 4:5], cols[:, 3:4], cols[:, 4:5],
                                                       ALU.subtract), waits=[(V, V.v)], inc=V)
            v_var = V.v
            p.emit("scalar", lambda e: e.activation(out=cols[:, 5:6], in_=cols[:, 4:5],
                                                    func=AF.Ln, bias=EPS),
                   waits=[(V, v_var)], inc=C)
            p.emit("scalar", lambda e: e.activation(out=cols[:, 5:6], in_=cols[:, 5:6],
                                                    func=AF.Exp, scale=f32(-0.5)),
                   waits=[(C, C.v)], inc=C)
            c_rstd = C.v
            ow = [(C, c_rstd), (V, V.v), (So[j], So[j].v)]
            p.emit("vector",
                   lambda e, j=j: e.tensor_scalar(outc[:, j, :], yb[:, j, :], cols[:, 2:3],
                                                  None, op0=ALU.subtract),
                   waits=ow, inc=V)
            p.emit("vector",
                   lambda e, j=j: e.tensor_scalar(outc[:, j, :], outc[:, j, :], cols[:, 5:6],
                                                  None, op0=ALU.mult),
                   waits=[(V, V.v)], inc=V)
            v_fin[tcb] = V.v
            p.dma("sync", So[j], OUT[tcb * 128:(tcb + 1) * 128, :], outc[:, j, :],
                  waits=[(V, v_fin[tcb])])
    return p




def _declare_all(nc):
    d = {}
    t = lambda n, shp, dt, k="ExternalInput": nc.dram_tensor(n, shp, dt, kind=k).ap()
    d["XT"] = t("xt", [D, S], F32)
    d["WQKV"] = t("wqkv", [D, 3072], F32)
    d["BQKV"] = t("bqkv", [128, 24], F32)
    d["WO"] = t("wo", [D, D], F32)
    d["BO"] = t("bo", [128, 16], F32)
    d["IDENT"] = t("ident", [128, 128], F32)
    d["GIDXT"] = t("gidxt", [D, 5504], I32)
    d["UIDXT"] = t("uidxt", [D, 5504], I32)
    d["DIDXT"] = t("didxt", [5504, D], I32)
    d["LUT"] = t("lut", [1, 16], F32)
    d["WQ"] = t("wq", [D, D], F32)
    d["WK"] = t("wk", [D, D], F32)
    d["WV"] = t("wv", [D, D], F32)
    d["WOL"] = t("wol", [D, D], F32)
    d["LQ1"] = t("lq1", [128, 1], F32)
    d["LK1"] = t("lk1", [128, 1], F32)
    d["LQ2"] = t("lq2", [128, 1], F32)
    d["LK2"] = t("lk2", [128, 1], F32)
    d["SUBW"] = t("subw", [128, 2], F32)
    d["MASKS"] = t("maskst", [128, 8 * 512], F32).rearrange("p (j q) -> p j q", j=8)
    d["MCOL"] = t("mcol", [128, 8], F32)
    d["META"] = t("meta", [1, 64], U32)
    d["RWT"] = t("rwt", [D, 16384], F32)
    d["RB"] = t("rb", [1, 16384], F32)
    d["EWT"] = t("ewt", [8, D, D], F32)
    d["EB"] = t("eb", [8, D], F32)
    d["OUT"] = t("outk", [TOWN, D], F32, "ExternalOutput")
    s = lambda n, shp, dt: nc.dram_tensor(n, shp, dt).ap()
    d["KVS"] = s("kvs", [1024, S], BF16)
    d["X1T"] = s("x1ts", [D, TOWN], F32)
    d["GSCR"] = s("gscr", [D, 5504], BF16)
    d["USCR"] = s("uscr", [D, 5504], BF16)
    d["DSCR"] = s("dscr", [5504, D], BF16)
    d["HSCR"] = s("hscr", [5504, TOWN], BF16)
    d["X2T"] = s("x2ts", [D, TOWN], F32)
    d["KSCR"] = s("kscr", [D, S], BF16)
    d["VSCR"] = s("vscr", [S, D], BF16)
    d["X3T"] = s("x3ts_d", [D, TOWN], F32)
    d["X3TOK"] = s("x3tok", [TOWN, D], F32)
    d["ASCR"] = s("ascr", [TOWN, D], F32)
    d["SHX2"] = nc.dram_tensor("shx2", [2, D, TOWN], BF16, addr_space="Shared").ap()
    d["SHFLAG"] = nc.dram_tensor("shflag", [2, 16], U32, addr_space="Shared").ap()
    return d


def build_fused(stub=False):
    nc = bass.Bass("TRN2", target_bir_lowering=False)
    d = _declare_all(nc)
    with ExitStack() as stk:
        p = P(nc)
        p._stack = stk
        emit_l0_attn(p, nc, d["XT"], d["WQKV"], d["BQKV"], d["WO"], d["BO"], d["IDENT"],
                     d["X1T"], d["KVS"],
                     MAT=(d["GIDXT"], d["UIDXT"], d["DIDXT"], d["LUT"],
                          d["GSCR"], d["USCR"], d["DSCR"]))
        emit_l0_ffn(p, nc, d["X1T"], d["GIDXT"], d["UIDXT"], d["DIDXT"], d["LUT"],
                    d["X2T"], d["GSCR"], d["USCR"], d["DSCR"], d["HSCR"])
        emit_l1_attn(p, nc, d["X2T"], None, d["WQ"], d["WK"], d["WV"], d["WOL"],
                     d["LQ1"], d["LK1"], d["LQ2"], d["LK2"], d["SUBW"],
                     d["MASKS"], d["MCOL"], d["META"], d["IDENT"], d["X3T"], d["X3TOK"],
                     d["KSCR"], d["VSCR"],
                     EXCH=dict(shx2=d["SHX2"], shflag=d["SHFLAG"], stub=stub))
        emit_l1_moe(p, nc, d["X3T"], d["X3TOK"], d["RWT"], d["RB"], d["EWT"], d["EB"],
                    d["OUT"], d["ASCR"])
        block = stk.enter_context(nc.Block())
        p.emit("sync", lambda e: None,
               waits=[(c, c.v) for nm, c in p.pool.items() if nm != "dSp"])
        p.run(block)
    return nc


from concourse.bass_utils import run_bass_kernel_spmd


def _perm_for_core(c):
    pp = c & 1
    own = [2 * i + pp for i in range(4)]
    other = [2 * i + (1 - pp) for i in range(4)]
    return np.concatenate([np.arange(g * 512, (g + 1) * 512) for g in own + other])


_NC = {}


def kernel(L0_attn_x, L0_attn_Wqkv, L0_attn_bqkv, L0_attn_Wo, L0_attn_bo,
           L0_ffn_lut, L0_ffn_gate_idx, L0_ffn_up_idx, L0_ffn_down_idx,
           L1_attn_Wq, L1_attn_Wk, L1_attn_Wv, L1_attn_Wo,
           L1_attn_lambda_q1, L1_attn_lambda_k1, L1_attn_lambda_q2, L1_attn_lambda_k2,
           L1_attn_subln_weight, L1_ffn_router_w, L1_ffn_router_b,
           L1_ffn_expert_w, L1_ffn_expert_b):
    x = np.ascontiguousarray(np.asarray(L0_attn_x, dtype=np.float32))
    if "F" not in _NC:
        _NC["F"] = build_fused(stub=False)
    nc = _NC["F"]
    ident = np.eye(128, dtype=np.float32)
    bq = np.ascontiguousarray(np.asarray(L0_attn_bqkv, np.float32).reshape(24, 128).T)
    bo = np.ascontiguousarray(np.asarray(L0_attn_bo, np.float32).reshape(16, 128).T)
    gidxt = np.ascontiguousarray(np.asarray(L0_ffn_gate_idx).T)
    uidxt = np.ascontiguousarray(np.asarray(L0_ffn_up_idx).T)
    didxt = np.ascontiguousarray(np.asarray(L0_ffn_down_idx).T)
    lut = np.asarray(L0_ffn_lut, np.float32).reshape(1, 16)
    subw = np.ascontiguousarray(np.asarray(L1_attn_subln_weight, np.float32).reshape(2, 128).T)
    rwt = np.ascontiguousarray(np.asarray(L1_ffn_router_w, np.float32).T)
    rb = np.asarray(L1_ffn_router_b, np.float32).reshape(1, 16384)
    ewt = np.ascontiguousarray(np.asarray(L1_ffn_expert_w, np.float32).transpose(0, 2, 1))
    eb = np.asarray(L1_ffn_expert_b, np.float32)
    nonce = np.uint32(np.random.randint(1, 2 ** 31 - 1))
    kk = np.arange(128)[:, None]
    qq = np.arange(512)[None, :]
    perms = [_perm_for_core(c) for c in range(8)]
    in_maps = []
    for c in range(8):
        b, pp = c // 2, c & 1
        m = np.zeros((8, 128, 512), np.float32)
        for q2 in range(2):
            for bb in range(4):
                m[q2 * 4 + bb] = ((bb * 128 + kk) <= qq).astype(np.float32) if q2 == 0 \
                    else (1.0 if pp == 1 else 0.0)
        mc = np.ones((128, 8), np.float32)
        mc[:, 4:8] = 1.0 if pp == 1 else 0.0
        meta = np.zeros((1, 64), np.uint32)
        meta[0, 40] = pp
        meta[0, 41] = 1 - pp
        meta[0, 42] = nonce
        in_maps.append(dict(
            xt=np.ascontiguousarray(x[b].T[:, perms[c]]),
            wqkv=np.asarray(L0_attn_Wqkv, np.float32), bqkv=bq,
            wo=np.asarray(L0_attn_Wo, np.float32), bo=bo, ident=ident,
            gidxt=gidxt, uidxt=uidxt, didxt=didxt, lut=lut,
            wq=np.asarray(L1_attn_Wq, np.float32), wk=np.asarray(L1_attn_Wk, np.float32),
            wv=np.asarray(L1_attn_Wv, np.float32), wol=np.asarray(L1_attn_Wo, np.float32),
            lq1=np.asarray(L1_attn_lambda_q1, np.float32).reshape(128, 1),
            lk1=np.asarray(L1_attn_lambda_k1, np.float32).reshape(128, 1),
            lq2=np.asarray(L1_attn_lambda_q2, np.float32).reshape(128, 1),
            lk2=np.asarray(L1_attn_lambda_k2, np.float32).reshape(128, 1),
            subw=subw,
            maskst=np.ascontiguousarray(m.transpose(1, 0, 2).reshape(128, 8 * 512)),
            mcol=mc,
            meta=meta, rwt=rwt, rb=rb, ewt=ewt, eb=eb,
        ))
    res = run_bass_kernel_spmd(nc, in_maps, core_ids=list(range(8)))
    out = np.empty((4, 4096, 2048), np.float32)
    for c in range(8):
        out[c // 2][perms[c][:2048]] = res.results[c]["outk"]
    return out
